# version 10
# baseline (speedup 1.0000x reference)
import math
import numpy as np
import ml_dtypes
from contextlib import ExitStack

import concourse.bass as bass
import concourse.mybir as mybir
from concourse.bass_utils import run_bass_kernel_spmd

F32 = mybir.dt.float32
BF16 = mybir.dt.bfloat16
AF = mybir.ActivationFunctionType
ALU = mybir.AluOpType
AX = mybir.AxisListType
BF = ml_dtypes.bfloat16

N_CORE = 8
T = 4096
C = 1024
NT = T // 128
NTC = T // 512
NH = 16
HD = 64
RNE = 12582912.0

ENGINES = ("t", "v", "s", "g", "y")
DEBUG_LN_BARRIER = False


class Slot:
    __slots__ = ("sem", "cnt", "name")

    def __init__(self, sem, name):
        self.sem = sem
        self.cnt = 0
        self.name = name


class B:
    def __init__(self, nc):
        self.nc = nc
        self.st = ExitStack()
        self.sems = {}
        self.cnt = {e: 0 for e in ENGINES}
        self.hw = {e: {} for e in ENGINES}
        self.steps = {e: [] for e in ENGINES}
        self.nalloc = 0
        self.spool = []
        self.sall = []
        for e in ENGINES:
            self.sems[e] = self.st.enter_context(nc.semaphore("sem_" + e))

    def sb(self, name, shape, dt, stack=None):
        self.nalloc += 1
        return (stack or self.st).enter_context(
            self.nc.sbuf_tensor(f"{name}_{self.nalloc}", shape, dt))

    def ps(self, name, shape, dt=F32, stack=None):
        self.nalloc += 1
        return (stack or self.st).enter_context(
            self.nc.psum_tensor(f"{name}_{self.nalloc}", shape, dt))

    def ssem(self):
        if self.spool:
            return self.spool.pop()
        self.nalloc += 1
        s = Slot(self.st.enter_context(self.nc.semaphore(f"dsem_{self.nalloc}")),
                 f"dsem_{self.nalloc}")
        self.sall.append(s)
        return s

    def ssems(self, n):
        return [self.ssem() for _ in range(n)]

    def sfree(self, slots):
        for s in slots:
            self.spool.append(s)

    def wait(self, eng, on, val=None):
        if val is None:
            val = self.cnt[on]
        if val <= 0 or self.hw[eng].get(on, 0) >= val:
            return
        self.hw[eng][on] = val
        sem = self.sems[on]
        step = lambda E, sem=sem, val=val: E.wait_ge(sem, val)
        step.meta = ("wait", on, val)
        self.steps[eng].append(step)

    def wsem(self, eng, slot, val=None):
        if val is None:
            val = slot.cnt
        key = ("slot", id(slot))
        if val <= 0 or self.hw[eng].get(key, 0) >= val:
            return
        self.hw[eng][key] = val
        step = lambda E, sem=slot.sem, val=val: E.wait_ge(sem, val)
        step.meta = ("wait", key, val)
        self.steps[eng].append(step)

    def emit(self, eng, fn, dep=1):
        if eng in ("v", "s", "g"):
            tgt = self.cnt[eng] + 1 - dep
            if tgt > 0:
                self.wait(eng, eng, tgt)
        self.cnt[eng] += 1
        sem = self.sems[eng]

        def step(E, fn=fn, sem=sem):
            fn(E).then_inc(sem, 1)
        step.meta = ("inc", eng, 1)
        self.steps[eng].append(step)
        return self.cnt[eng]

    def dma(self, eng, slot, fn):
        slot.cnt += 16

        def step(E, fn=fn, sem=slot.sem):
            fn(E).then_inc(sem, 16)
        step.meta = ("inc", ("slot", id(slot)), 16)
        self.steps[eng].append(step)
        return slot.cnt

    def check_deadlock(self):
        pos = {e: 0 for e in ENGINES}
        sem = {}
        while True:
            progress = False
            for e in ENGINES:
                while pos[e] < len(self.steps[e]):
                    m = getattr(self.steps[e][pos[e]], "meta", None)
                    if m is None:
                        pos[e] += 1
                        progress = True
                        continue
                    kind, key, val = m
                    if kind == "inc":
                        sem[key] = sem.get(key, 0) + val
                        pos[e] += 1
                        progress = True
                    else:
                        if sem.get(key, 0) >= val:
                            pos[e] += 1
                            progress = True
                        else:
                            break
            if all(pos[e] >= len(self.steps[e]) for e in ENGINES):
                return
            if not progress:
                rep = []
                for e in ENGINES:
                    if pos[e] < len(self.steps[e]):
                        m = getattr(self.steps[e][pos[e]], "meta", None)
                        rep.append(f"{e}@{pos[e]}/{len(self.steps[e])} blocked on {m} have={sem.get(m[1], 0) if m else '?'}")
                raise RuntimeError("DEADLOCK:\n" + "\n".join(rep))

    def barrier(self, engines=None):
        engines = engines or ENGINES
        snap = dict(self.cnt)
        for e in engines:
            for o in engines:
                self.wait(e, o, snap[o])
            if engines is ENGINES:
                for s in self.sall:
                    self.wsem(e, s, s.cnt)

    def finalize(self):
        self.check_deadlock()
        for h in list(self.sems.values()) + [s.sem for s in self.sall]:
            self.nc.gpsimd.sem_clear(h)
        self.nc.all_engine_barrier()
        with self.nc.Block() as block:
            for name, deco in (("y", block.sync), ("t", block.tensor),
                               ("v", block.vector), ("s", block.scalar),
                               ("g", block.gpsimd)):
                steps = self.steps[name]

                def body(E, steps=steps):
                    for s_ in steps:
                        s_(E)
                deco(body)
        self.st.close()





def row_ap(t, off, w):
    return bass.AP(t, off, [[1, 1], [1, w]])

CONST_VALS = [1e-5, -16.0, 1e-6, 0.0, 1.0]
DEBUG_LN_NOSTATS = False


def emit_consts(b, io):
    cb = {}
    cb["identb"] = b.sb("c_identb", [128, 128], BF16)
    cb["ones_f"] = b.sb("c_ones_f", [128, 128], F32)
    cb["ones_b"] = b.sb("c_ones_b", [1, 512], BF16)
    cb["cpool"] = b.sb("c_cpool", [128, max(8, len(CONST_VALS))], F32)
    ss = b.ssem()
    b.dma("y", ss, lambda E, d=cb["identb"], s=io["identb"]: E.dma_start(d[:, :], s.ap()))
    b.emit("v", lambda E, d=cb["ones_f"]: E.memset(d[:, :], 1.0))
    b.emit("v", lambda E, d=cb["ones_b"]: E.memset(d[:, :], 1.0))
    for i, v in enumerate(CONST_VALS):
        b.emit("v", lambda E, d=cb["cpool"], i=i, v=v: E.memset(d[:, i:i + 1], float(v)))
    b.barrier()
    b.sfree([ss])
    return cb


def ccol(b, v):
    return b.cb["cpool"][:, CONST_VALS.index(v):CONST_VALS.index(v) + 1]



def emit_ln_t(b, src, xT, g_bc=None, b_bc=None, rms_scale_bc=None,
              norm_f32_dst=None, eps=1e-5, mirror_scr=None):
    st = ExitStack()
    NXF = 4
    xf = [b.sb(f"ln_xf{i}", [128, C], F32, st) for i in range(NXF)]
    xn = [b.sb(f"ln_xn{i}", [128, C], BF16, st) for i in range(3)]
    xg = [b.sb(f"ln_xg{i}", [128, C], F32, st) for i in range(2)] if norm_f32_dst is not None else None
    stats = [b.sb(f"ln_st{i}", [128, 8], F32, st) for i in range(3)]
    sq = [b.sb(f"ln_sq{i}", [128, C], F32, st) for i in range(2)]
    pt = [b.ps(f"ln_pt{i}", [128, 1024], BF16, st) for i in range(2)]
    sx = b.ssems(NXF)
    sg_ = b.ssems(2) if norm_f32_dst is not None else None
    smir = b.ssem() if mirror_scr is not None else None
    rel = {}

    def stats_stage(i):
        fx, sxx = xf[i % NXF], sx[i % NXF]
        sl = stats[i % 3]
        if ("xf", i % NXF) in rel:
            e, c = rel[("xf", i % NXF)]
            b.wait("y", e, c)
        b.dma("y", sxx, lambda E, d=fx, s=src, i=i: E.dma_start(
            d[:, :], s.ap()[i * 128:(i + 1) * 128, :]))
        b.wsem("v", sxx)
        if ("st", i % 3) in rel:
            e, c = rel[("st", i % 3)]
            b.wait("v", e, c)
            b.wait("s", e, c)
        if rms_scale_bc is None:
            b.emit("v", lambda E, d=sl, s_=fx: E.reduce_sum(d[:, 0:1], s_[:, :], AX.X))
        b.wsem("s", sxx)
        if ("sq", i % 2) in rel:
            e, c = rel[("sq", i % 2)]
            b.wait("s", e, c)
        b.emit("s", lambda E, d=sl, s_=fx, q=sq[i % 2]: E.activation(
            q[:, :], s_[:, :], AF.Square, accum_out=d[:, 1:2]), dep=2)
        rel[("sq", i % 2)] = ("s", b.cnt["s"])
        b.wait("v", "s", b.cnt["s"])
        if rms_scale_bc is None:
            b.emit("v", lambda E, d=sl: E.tensor_scalar(d[:, 2:3], d[:, 0:1], 1.0 / C, None, ALU.mult))
            b.emit("v", lambda E, d=sl: E.tensor_tensor(d[:, 3:4], d[:, 2:3], d[:, 2:3], ALU.mult))
            b.emit("v", lambda E, d=sl: E.tensor_scalar(d[:, 4:5], d[:, 1:2], 1.0 / C, d[:, 3:4], ALU.mult, ALU.subtract))
            b.wait("s", "v", b.cnt["v"])
            b.emit("s", lambda E, d=sl: E.activation(d[:, 5:6], d[:, 4:5], AF.Ln, bias=ccol(b, float(eps))))
            b.emit("s", lambda E, d=sl: E.activation(d[:, 6:7], d[:, 5:6], AF.Exp, scale=-0.5))
        else:
            b.emit("v", lambda E, d=sl: E.memset(d[:, 2:3], 0.0))
            b.wait("s", "v", b.cnt["v"])
            b.emit("s", lambda E, d=sl: E.activation(d[:, 5:6], d[:, 1:2], AF.Ln))
            b.emit("s", lambda E, d=sl: E.activation(d[:, 6:7], d[:, 5:6], AF.Exp, scale=-0.5))
        return b.cnt["s"]

    def norm_stage(i, sdone):
        fx = xf[i % NXF]
        sl = stats[i % 3]
        xo = xn[i % 3]
        b.wait("v", "s", sdone)
        if rms_scale_bc is not None:
            b.emit("v", lambda E, d=sl: E.tensor_scalar(d[:, 6:7], d[:, 6:7], math.sqrt(C) + 1e-6, None, ALU.mult))
        if ("xn", i % 3) in rel:
            e, c = rel[("xn", i % 3)]
            b.wait("v", e, c)
        b.emit("v", lambda E, d=xo, s_=fx, sl=sl: E.tensor_scalar(
            d[:, :], s_[:, :], sl[:, 2:3], sl[:, 6:7], ALU.subtract, ALU.mult))
        if rms_scale_bc is not None:
            b.emit("v", lambda E, d=xo, g=rms_scale_bc: E.tensor_tensor(d[:, :], d[:, :], g[:, :], ALU.mult))
        if g_bc is not None:
            b.emit("v", lambda E, d=xo, g=g_bc: E.tensor_tensor(d[:, :], d[:, :], g[:, :], ALU.mult))
        if b_bc is not None:
            b.emit("v", lambda E, d=xo, g=b_bc: E.tensor_tensor(d[:, :], d[:, :], g[:, :], ALU.add))
        vdone = b.cnt["v"]
        rel[("xf", i % NXF)] = ("v", vdone)
        rel[("st", i % 3)] = ("v", vdone)
        if norm_f32_dst is not None:
            fg, sgx = xg[i % 2], sg_[i % 2]
            b.wsem("v", sgx)
            b.emit("v", lambda E, d=fg, s_=xo: E.tensor_copy(d[:, :], s_[:, :]))
            b.wait("y", "v", b.cnt["v"])
            b.dma("y", sgx, lambda E, s_=fg, d=norm_f32_dst, i=i: E.dma_start(
                d.ap()[i * 128:(i + 1) * 128, :], s_[:, :]))
        po = pt[i % 2]
        b.wait("t", "v", vdone)
        if ("pt", i % 2) in rel:
            e, c = rel[("pt", i % 2)]
            b.wait("t", e, c)
        for cch in range(8):
            b.emit("t", lambda E, po=po, s_=xo, cch=cch, ib=b.cb["identb"]: E.transpose(
                po[:, cch * 128:(cch + 1) * 128], s_[:, cch * 128:(cch + 1) * 128], ib[:, :]))
        rel[("xn", i % 3)] = ("t", b.cnt["t"])
        return b.cnt["t"]

    def evac_stage(i, tdone):
        po = pt[i % 2]
        b.wait("s", "t", tdone)
        b.emit("s", lambda E, po=po, xT=xT, i=i: E.copy(
            xT.ap().rearrange("p (c n) -> p c n", c=8)[:, :, i * 128:(i + 1) * 128],
            po.ap().rearrange("p (c n) -> p c n", c=8)), dep=2)
        rel[("pt", i % 2)] = ("s", b.cnt["s"])
        if mirror_scr is not None:
            b.wait("y", "s", b.cnt["s"])
            b.dma("y", smir, lambda E, d=mirror_scr, s_=xT, i=i: E.dma_start(
                d.ap()[:, i * 128:(i + 1) * 128].rearrange("(a p) f -> p a f", p=128),
                s_.ap().rearrange("p (c n) -> p c n", c=8)[:, :, i * 128:(i + 1) * 128]))

    pend = []
    norms = []
    for i in range(NT):
        sdone = stats_stage(i)
        pend.append((i, sdone))
        if len(pend) > 1:
            j, sd = pend.pop(0)
            norms.append((j, norm_stage(j, sd)))
        if len(norms) > 1:
            j, td = norms.pop(0)
            evac_stage(j, td)
    for j, sd in pend:
        norms.append((j, norm_stage(j, sd)))
    for j, td in norms:
        evac_stage(j, td)
    b.barrier()
    b.sfree(sx + (sg_ or []) + ([smir] if smir else []))
    st.close()


def _w_cast_load(dst_sb, w_hbm, cols, rows=None, col0=0, row0=0):
    R = rows if rows is not None else w_hbm.shape[0]
    nchunk = R // 128

    def fn(E, d=dst_sb, s=w_hbm, cols=cols, nchunk=nchunk, col0=col0, row0=row0):
        src = s.ap()[row0:row0 + nchunk * 128, col0:col0 + cols].rearrange(
            "(a p) f -> p a f", p=128)
        dd = d.ap().rearrange("p (a f) -> p a f", a=nchunk)
        return E.dma_start(dd, src)
    return fn


def emit_proj_T(b, xT, w_hbm, col0, fout, dst_scr, act=AF.Copy,
                bias_col=None, evac_scale=1.0, ntok=T, act2=None):
    st = ExitStack()
    ntc = ntok // 512
    wl = [b.sb(f"pT_w{i}", [128, 8 * 128], BF16, st) for i in range(2)]
    stg = [b.sb(f"pT_s{i}", [128, ntok], BF16, st) for i in range(2)]
    tmp2 = b.sb("pT_t2", [128, 512], F32, st) if act2 is not None else None
    pp = [b.ps(f"pT_p{i}", [128, 512], F32, st) for i in range(2)]
    sw = b.ssems(2)
    so = b.ssems(2)
    rel = {}
    for ft in range(fout // 128):
        w, sww = wl[ft % 2], sw[ft % 2]
        sg, soo = stg[ft % 2], so[ft % 2]
        if ("w", ft % 2) in rel:
            e, c = rel[("w", ft % 2)]
            b.wait("g", e, c)
        b.dma("g", sww, _w_cast_load(w, w_hbm, 128, col0=col0 + ft * 128))
        b.wsem("s", soo)
        for tc in range(ntc):
            po = pp[tc % 2]
            b.wsem("t", sww)
            if ("pp", tc % 2) in rel:
                e, c = rel[("pp", tc % 2)]
                b.wait("t", e, c)
            for a in range(8):
                b.emit("t", lambda E, po=po, w=w, xT=xT, a=a, tc=tc, ntok=ntok: E.matmul(
                    po[:, :], w[:, a * 128:(a + 1) * 128],
                    xT[:, a * ntok + tc * 512: a * ntok + tc * 512 + 512],
                    start=(a == 0), stop=(a == 7)))
            b.wait("s", "t", b.cnt["t"])
            kw = {}
            if bias_col is not None:
                kw["bias"] = bias_col[:, ft:ft + 1]
            if act == AF.Copy and bias_col is not None:
                b.wait("v", "t", b.cnt["t"])
                b.wsem("v", soo)
                b.emit("v", lambda E, po=po, sg=sg, tc=tc, bc=bias_col, ft=ft: E.tensor_scalar(
                    sg[:, tc * 512:(tc + 1) * 512], po[:, :], bc[:, ft:ft + 1], None, ALU.add))
                rel[("pp", tc % 2)] = ("v", b.cnt["v"])
                continue
            if act2 is None:
                b.emit("s", lambda E, po=po, sg=sg, tc=tc, act=act, kw=kw, sc=evac_scale: E.activation(
                    sg[:, tc * 512:(tc + 1) * 512], po[:, :], act, scale=sc, **kw), dep=2)
            else:
                b.emit("s", lambda E, po=po, act=act, kw=kw, sc=evac_scale: E.activation(
                    tmp2[:, :], po[:, :], act, scale=sc, **kw))
                a2f, a2s = act2
                b.emit("s", lambda E, sg=sg, tc=tc, a2f=a2f, a2s=a2s: E.activation(
                    sg[:, tc * 512:(tc + 1) * 512], tmp2[:, :], a2f, scale=a2s))
            rel[("pp", tc % 2)] = ("s", b.cnt["s"])
        b.wait("y", "s", b.cnt["s"])
        b.wait("y", "v", b.cnt["v"])
        b.dma("y", soo, lambda E, sg=sg, d=dst_scr, ft=ft: E.dma_start(
            d.ap()[ft * 128:(ft + 1) * 128, :], sg[:, :]))
        rel[("w", ft % 2)] = ("t", b.cnt["t"])
    b.barrier()
    b.sfree(sw + so)
    st.close()


def emit_proj_N(b, xT, w_hbm, col0, fout, dst_scr, act=AF.Copy,
                bias_row=None, bias_off=0, evac_scale=1.0, ntok=T, act2=None,
                dst_col0=0):
    st = ExitStack()
    nt_ = ntok // 128
    wr = [b.sb(f"pN_w{i}", [128, 8 * 512], BF16, st) for i in range(2)]
    stg = [b.sb(f"pN_s{i}", [128, 512], BF16, st) for i in range(4)]
    tmp2 = b.sb("pN_t2", [128, 512], F32, st) if act2 is not None else None
    pp = [b.ps(f"pN_p{i}", [128, 512], F32, st) for i in range(2)]
    sw = b.ssems(2)
    so = b.ssems(4)
    br = None
    if bias_row is not None:
        br = b.sb("pN_br", [1, fout], BF16, st)
        sb_ = b.ssem()
        b.dma("g", sb_, lambda E, d=br, s=bias_row, o=bias_off, f=fout: E.dma_start(
            d[0:1, :], row_ap(s, o, f)))
        b.wsem("t", sb_)
        b.sfree([sb_])
    rel = {}
    for fc in range(fout // 512):
        w, sww = wr[fc % 2], sw[fc % 2]
        if ("w", fc % 2) in rel:
            e, c = rel[("w", fc % 2)]
            b.wait("g", e, c)
        b.dma("g", sww, _w_cast_load(w, w_hbm, 512, col0=col0 + fc * 512))
        for tt in range(nt_):
            po = pp[tt % 2]
            sg, soo = stg[tt % 4], so[tt % 4]
            b.wsem("t", sww)
            if ("pp", tt % 2) in rel:
                e, c = rel[("pp", tt % 2)]
                b.wait("t", e, c)
            for a in range(8):
                b.emit("t", lambda E, po=po, w=w, xT=xT, a=a, tt=tt, nt=ntok: E.matmul(
                    po[:, :], xT[:, a * nt + tt * 128: a * nt + tt * 128 + 128],
                    w[:, a * 512:(a + 1) * 512],
                    start=(a == 0), stop=(bias_row is None and a == 7)))
            if bias_row is not None:
                b.emit("t", lambda E, po=po, br=br, fc=fc, ob=b.cb["ones_b"]: E.matmul(
                    po[:, :], ob[0:1, 0:128], br[0:1, fc * 512:(fc + 1) * 512],
                    start=False, stop=True))
            b.wait("s", "t", b.cnt["t"])
            b.wsem("s", soo)
            if act2 is None:
                b.emit("s", lambda E, po=po, sg=sg, act=act, sc=evac_scale: E.activation(
                    sg[:, :], po[:, :], act, scale=sc), dep=2)
            else:
                b.emit("s", lambda E, po=po, act=act, sc=evac_scale: E.activation(
                    tmp2[:, :], po[:, :], act, scale=sc))
                a2f, a2s = act2
                b.emit("s", lambda E, sg=sg, a2f=a2f, a2s=a2s: E.activation(
                    sg[:, :], tmp2[:, :], a2f, scale=a2s))
            rel[("pp", tt % 2)] = ("s", b.cnt["s"])
            b.wait("y", "s", b.cnt["s"])
            b.dma("y", soo, lambda E, sg=sg, d=dst_scr, tt=tt, fc=fc, o=dst_col0: E.dma_start(
                d.ap()[tt * 128:(tt + 1) * 128, o + fc * 512:o + (fc + 1) * 512], sg[:, :]))
        rel[("w", fc % 2)] = ("t", b.cnt["t"])
    b.barrier()
    b.sfree(sw + so)
    st.close()


def emit_lin_attn_causal(b, qT_scr, kT_scr, kn_scr, vn_scr, masks_sb, attT_scr, eps=1e-5):
    st = ExitStack()
    qTp = b.sb("la_q", [128, T], BF16, st)
    kTp = b.sb("la_k", [128, T], BF16, st)
    knp = b.sb("la_kn", [128, NT * 128], BF16, st)
    vp = [b.sb(f"la_v{i}", [128, NT * 65], BF16, st) for i in range(2)]
    state_f = b.sb("la_stf", [128, 65], F32, st)
    state_b = b.sb("la_stb", [128, 65], BF16, st)
    on_sb = [b.sb(f"la_on{i}", [128, 512], F32, st) for i in range(2)]
    d_sb = b.sb("la_d", [128, 512], F32, st)
    obh = [b.sb(f"la_ob{i}", [64, T], BF16, st) for i in range(2)]
    msk = [b.sb(f"la_m{i}", [128, 512], BF16, st) for i in range(4)]
    PO = [[b.ps(f"la_po{i}{j}", [65, 512], F32, st) for j in range(2)] for i in range(2)]
    PS = [b.ps(f"la_ps{i}", [128, 512], F32, st) for i in range(2)]
    PK = b.ps("la_pk", [128, 65], F32, st)
    PB = b.ps("la_pb", [64, 512], F32, st)
    sl = {k: b.ssem() for k in ("q", "k", "kn", "v0", "v1", "ob0", "ob1")}
    rel = {}
    for p in range(NH // 2):
        if ("pair",) in rel:
            e, c = rel[("pair",)]
            b.wait("y", e, c)
        b.dma("y", sl["q"], lambda E, d=qTp, s=qT_scr, p=p: E.dma_start(
            d[:, :], s.ap()[p * 128:(p + 1) * 128, :]))
        b.dma("y", sl["k"], lambda E, d=kTp, s=kT_scr, p=p: E.dma_start(
            d[:, :], s.ap()[p * 128:(p + 1) * 128, :]))
        b.dma("y", sl["kn"], lambda E, d=knp, s=kn_scr, p=p: E.dma_start(
            d.ap().rearrange("p (a f) -> p a f", a=NT),
            s.ap()[:, p * 128:(p + 1) * 128].rearrange("(a p) f -> p a f", p=128)))
        for hh in range(2):
            vb = vp[hh]
            if ("pair",) in rel:
                e, c = rel[("pair",)]
                b.wait("v", e, c)
            b.emit("v", lambda E, d=vb: E.memset(
                d.ap().rearrange("p (a f) -> p a f", a=NT)[:, :, 64:65], 1.0))
            b.wait("y", "v", b.cnt["v"])
            b.dma("y", sl["v" + str(hh)], lambda E, d=vb, s=vn_scr, p=p, hh=hh: E.dma_start(
                d.ap().rearrange("p (a f) -> p a f", a=NT)[:, :, 0:64],
                s.ap()[:, p * 128 + hh * 64: p * 128 + hh * 64 + 64].rearrange(
                    "(a p) f -> p a f", p=128)))
        for e_ in ("t", "v", "s"):
            for k_ in ("q", "k", "kn", "v0", "v1"):
                b.wsem(e_, sl[k_])
        b.wait("v", "t", b.cnt["t"])
        b.emit("v", lambda E, d=state_f: E.memset(d[:, :], 0.0))
        b.emit("v", lambda E, d=state_b: E.memset(d[:, :], 0.0))
        epis = []
        for gi in range(NTC):
            g0 = gi * 512
            gp = gi % 2
            b.wait("t", "v", b.cnt["v"])
            for hh in range(2):
                hb = 64 * hh
                if ("pog", hh, gp) in rel:
                    e, c = rel[("pog", hh, gp)]
                    b.wait("t", e, c)
                b.emit("t", lambda E, hb=hb, hh=hh, gp=gp, g0=g0: E.matmul(
                    PO[hh][gp][:, :], state_b[hb:hb + 64, :], qTp[hb:hb + 64, g0:g0 + 512],
                    start=True, stop=False))
            if epis:
                epis.pop(0)()
            for j in range(4):
                kb = gi * 4 + j
                for hh in range(2):
                    hb = 64 * hh
                    pS = PS[hh]
                    mk = msk[hh * 2 + j % 2]
                    if ("ps", hh) in rel:
                        e, c = rel[("ps", hh)]
                        b.wait("t", e, c)
                    b.emit("t", lambda E, pS=pS, hb=hb, kb=kb, g0=g0: E.matmul(
                        pS[:, :], kTp[hb:hb + 64, kb * 128:kb * 128 + 128],
                        qTp[hb:hb + 64, g0:g0 + 512], start=True, stop=True))
                    b.wait("v", "t", b.cnt["t"])
                    if ("mk", hh, j % 2) in rel:
                        e, c = rel[("mk", hh, j % 2)]
                        b.wait("v", e, c)
                    b.emit("v", lambda E, pS=pS, mk=mk, ms=masks_sb, j=j: E.tensor_tensor(
                        mk[:, :], pS[:, :], ms[:, j * 512:(j + 1) * 512], ALU.mult), dep=2)
                    rel[("ps", hh)] = ("v", b.cnt["v"])
                    b.wait("t", "v", b.cnt["v"])
                    b.emit("t", lambda E, hh=hh, gp=gp, mk=mk, kb=kb, j=j: E.matmul(
                        PO[hh][gp][:, :], vp[hh][:, kb * 65:kb * 65 + 65], mk[:, :],
                        start=False, stop=(j == 3)))
                    rel[("mk", hh, j % 2)] = ("t", b.cnt["t"])
            b.wait("t", "v", b.cnt["v"])
            for hh in range(2):
                hb = 64 * hh
                for j in range(4):
                    kb = gi * 4 + j
                    b.emit("t", lambda E, hb=hb, hh=hh, kb=kb, j=j: E.matmul(
                        PK[hb:hb + 64, :], knp[:, kb * 128 + hb: kb * 128 + hb + 64],
                        vp[hh][:, kb * 65:kb * 65 + 65], start=(j == 0), stop=(j == 3)))
            pkdone = b.cnt["t"]
            b.wait("v", "t", pkdone)
            for hh in range(2):
                hb = 64 * hh
                b.emit("v", lambda E, hb=hb: E.tensor_tensor(
                    state_f[hb:hb + 64, :], state_f[hb:hb + 64, :], PK[hb:hb + 64, :], ALU.add))
                b.emit("v", lambda E, hb=hb: E.tensor_copy(
                    state_b[hb:hb + 64, :], state_f[hb:hb + 64, :]))
            svdone = b.cnt["t"]

            def mk_epi(gi=gi, gp=gp, g0=g0, svdone=svdone):
                def epi():
                    for hh in range(2):
                        hb2 = 64 * hh
                        b.wait("v", "t", svdone)
                        b.emit("v", lambda E, hh=hh, gp=gp, hb2=hb2, eps=eps: E.tensor_scalar(
                            d_sb[hb2:hb2 + 1, :], PO[hh][gp][64:65, :], float(eps), None, ALU.add))
                        b.emit("v", lambda E, hb2=hb2: E.reciprocal(
                            d_sb[hb2:hb2 + 1, :], d_sb[hb2:hb2 + 1, :]))
                        vdone = b.cnt["v"]
                        b.wait("t", "v", vdone)
                        if ("pb",) in rel:
                            e, c = rel[("pb",)]
                            b.wait("t", e, c)
                        b.emit("t", lambda E, hb2=hb2, of=b.cb["ones_f"]: E.matmul(
                            PB[:, :], of[hb2:hb2 + 1, 0:64], d_sb[hb2:hb2 + 1, :],
                            start=True, stop=True))
                        b.wait("s", "t", b.cnt["t"])
                        b.emit("s", lambda E, hh=hh, gp=gp: E.copy(
                            on_sb[hh][0:64, :], PO[hh][gp][0:64, :]))
                        b.wait("v", "s", b.cnt["s"])
                        b.wsem("v", sl["ob" + str(hh)])
                        b.emit("v", lambda E, hh=hh, g0=g0: E.tensor_tensor(
                            obh[hh][0:64, g0:g0 + 512], on_sb[hh][0:64, :], PB[0:64, :], ALU.mult))
                        rel[("pb",)] = ("v", b.cnt["v"])
                        rel[("pog", hh, gp)] = ("v", b.cnt["v"])
                return epi
            epis.append(mk_epi())
        for ep in epis:
            ep()
        for hh in range(2):
            b.wait("y", "v", b.cnt["v"])
            b.dma("y", sl["ob" + str(hh)], lambda E, d=attT_scr, s=obh[hh], p=p, hh=hh: E.dma_start(
                d.ap()[(p * 2 + hh) * 64:(p * 2 + hh) * 64 + 64, :], s[:, :]))
        rel[("pair",)] = ("t", b.cnt["t"])
    b.barrier()
    b.sfree(list(sl.values()))
    st.close()


def emit_out_proj(b, inT_scr, w_hbm, fout, res_src, dst, bias_row=None,
                  extra_res=None, gamma_bc=None, cin=C):
    st = ExitStack()
    nch = cin // 128
    w_sb = b.sb("op_w", [128, nch * fout], BF16, st)
    aT = [b.sb(f"op_a{i}", [128, nch * 512], BF16, st) for i in range(2)]
    res = [b.sb(f"op_r{i}", [128, fout], F32, st) for i in range(2)]
    ex = [b.sb(f"op_e{i}", [128, fout], F32, st) for i in range(2)] if extra_res is not None else None
    out = [b.sb(f"op_o{i}", [128, fout], F32, st) for i in range(2)]
    tmp = b.sb("op_t", [128, 512], F32, st)
    br = b.sb("op_br", [1, fout], BF16, st) if bias_row is not None else None
    pp = [b.ps(f"op_p{i}", [128, 512], F32, st) for i in range(2)]
    sw = b.ssem()
    sa = b.ssems(2)
    sr = b.ssems(2)
    se = b.ssems(2) if extra_res is not None else None
    sd = b.ssems(2)
    b.dma("g", sw, _w_cast_load(w_sb, w_hbm, fout))
    if bias_row is not None:
        b.dma("g", sw, lambda E, d=br, s=bias_row: E.dma_start(
            d[0:1, :], row_ap(s, 0, fout)))
    rel = {}
    for tc in range(NTC):
        a, saa = aT[tc % 2], sa[tc % 2]
        if ("a", tc % 2) in rel:
            e, c = rel[("a", tc % 2)]
            b.wait("y", e, c)
        b.dma("y", saa, lambda E, d=a, s=inT_scr, tc=tc, nch=nch: E.dma_start(
            d.ap().rearrange("p (a f) -> p a f", a=nch),
            s.ap()[:, tc * 512:(tc + 1) * 512].rearrange("(a p) f -> p a f", p=128)))
        for tl in range(4):
            tt = tc * 4 + tl
            rs, srr = res[tt % 2], sr[tt % 2]
            ot, sdd = out[tt % 2], sd[tt % 2]
            if ("rs", tt % 2) in rel:
                e, c = rel[("rs", tt % 2)]
                b.wait("y", e, c)
            b.dma("y", srr, lambda E, d=rs, s=res_src, tt=tt: E.dma_start(
                d[:, :], s.ap()[tt * 128:(tt + 1) * 128, :]))
            if extra_res is not None:
                b.dma("y", se[tt % 2], lambda E, d=ex[tt % 2], s=extra_res, tt=tt: E.dma_start(
                    d[:, :], s.ap()[tt * 128:(tt + 1) * 128, :]))
            for fc in range(fout // 512):
                po = pp[fc % 2]
                b.wsem("t", sw)
                b.wsem("t", saa)
                if ("pp", fc % 2) in rel:
                    e, c = rel[("pp", fc % 2)]
                    b.wait("t", e, c)
                for ac in range(nch):
                    b.emit("t", lambda E, po=po, a=a, ac=ac, tl=tl, fc=fc, fout=fout: E.matmul(
                        po[:, :], a[:, ac * 512 + tl * 128: ac * 512 + tl * 128 + 128],
                        w_sb[:, ac * fout + fc * 512: ac * fout + fc * 512 + 512],
                        start=(ac == 0), stop=(bias_row is None and ac == nch - 1)))
                if bias_row is not None:
                    b.emit("t", lambda E, po=po, br=br, fc=fc, ob=b.cb["ones_b"]: E.matmul(
                        po[:, :], ob[0:1, 0:128], br[0:1, fc * 512:(fc + 1) * 512],
                        start=False, stop=True))
                tdone = b.cnt["t"]
                b.wait("v", "t", tdone)
                b.wsem("v", srr)
                if extra_res is not None:
                    b.wsem("v", se[tt % 2])
                b.wsem("v", sdd)
                if gamma_bc is None:
                    b.emit("v", lambda E, ot=ot, rs=rs, po=po, fc=fc: E.tensor_tensor(
                        ot[:, fc * 512:(fc + 1) * 512], rs[:, fc * 512:(fc + 1) * 512],
                        po[:, :], ALU.add))
                else:
                    b.emit("v", lambda E, po=po, g=gamma_bc, fc=fc: E.tensor_tensor(
                        tmp[:, :], po[:, :], g[:, fc * 512:(fc + 1) * 512], ALU.mult))
                    b.emit("v", lambda E, ot=ot, rs=rs, fc=fc: E.tensor_tensor(
                        ot[:, fc * 512:(fc + 1) * 512], rs[:, fc * 512:(fc + 1) * 512],
                        tmp[:, :], ALU.add))
                if extra_res is not None:
                    b.emit("v", lambda E, ot=ot, s_=ex[tt % 2], fc=fc: E.tensor_tensor(
                        ot[:, fc * 512:(fc + 1) * 512], ot[:, fc * 512:(fc + 1) * 512],
                        s_[:, fc * 512:(fc + 1) * 512], ALU.add))
                rel[("pp", fc % 2)] = ("v", b.cnt["v"])
            rel[("rs", tt % 2)] = ("v", b.cnt["v"])
            b.wait("y", "v", b.cnt["v"])
            b.dma("y", sdd, lambda E, s_=ot, d=dst, tt=tt: E.dma_start(
                d.ap()[tt * 128:(tt + 1) * 128, :], s_[:, :]))
        rel[("a", tc % 2)] = ("t", b.cnt["t"])
    b.barrier()
    b.sfree([sw] + sa + sr + (se or []) + sd)
    st.close()




def emit_bcast_row(b, vec_hbm, width, dst=None, off=0, scale=None):
    out = dst if dst is not None else b.sb("bc_out", [128, width], F32)
    st = ExitStack()
    row = b.sb("bc_row", [1, width], F32, st)
    pb = b.ps("bc_p", [128, 512], F32, st)
    ss = b.ssem()
    b.dma("y", ss, lambda E, d=row, s=vec_hbm, off=off, w=width: E.dma_start(
        d[0:1, :], row_ap(s, off, w)))
    b.wsem("t", ss)
    for fc in range((width + 511) // 512):
        w = min(512, width - fc * 512)
        b.wait("t", "s", b.cnt["s"])
        b.emit("t", lambda E, pb=pb, row=row, fc=fc, w=w, of=b.cb["ones_f"]: E.matmul(
            pb[:, 0:w], of[0:1, :], row[0:1, fc * 512:fc * 512 + w],
            start=True, stop=True))
        b.wait("s", "t", b.cnt["t"])
        if scale is None:
            b.emit("s", lambda E, pb=pb, out=out, fc=fc, w=w: E.copy(
                out[:, fc * 512:fc * 512 + w], pb[:, 0:w]))
        else:
            b.emit("s", lambda E, pb=pb, out=out, fc=fc, w=w, sc=scale: E.mul(
                out[:, fc * 512:fc * 512 + w], pb[:, 0:w], sc))
    b.barrier()
    b.sfree([ss])
    st.close()
    return out


def emit_bcast_scalar(b, sc_hbm, nparts=128):
    out = b.sb("bs_o", [128, 1], F32)
    st = ExitStack()
    cell = b.sb("bs_c", [1, 1], F32, st)
    pb = b.ps("bs_p", [128, 1], F32, st)
    ss = b.ssem()
    b.dma("y", ss, lambda E, d=cell, s=sc_hbm: E.dma_start(d[0:1, 0:1], s.ap()))
    b.wsem("t", ss)
    b.emit("t", lambda E, of=b.cb["ones_f"]: E.matmul(
        pb[:, :], of[0:1, :], cell[0:1, 0:1], start=True, stop=True))
    b.wait("s", "t", b.cnt["t"])
    b.emit("s", lambda E: E.copy(out[:, :], pb[:, :]))
    b.barrier()
    b.sfree([ss])
    st.close()
    return out


def emit_weff(b, w_hbm, rows, cols, dst_scr, barrier=True):
    st = ExitStack()
    wt = [b.sb(f"we_w{i}", [128, cols], F32, st) for i in range(2)]
    tm = b.sb("we_t", [128, cols], F32, st)
    stg = [b.sb(f"we_s{i}", [128, cols], BF16, st) for i in range(2)]
    sc = b.sb("we_sc", [128, 4], F32, st)
    sl_ = b.ssems(2)
    so = b.ssems(2)
    rel = {}
    for i in range(rows // 128):
        w, sw = wt[i % 2], sl_[i % 2]
        sg, soo = stg[i % 2], so[i % 2]
        if ("w", i % 2) in rel:
            e, c = rel[("w", i % 2)]
            b.wait("y", e, c)
        b.dma("y", sw, lambda E, d=w, s=w_hbm, i=i: E.dma_start(
            d[:, :], s.ap()[i * 128:(i + 1) * 128, :]))
        b.wsem("v", sw)
        b.emit("v", lambda E, w=w: E.reduce_sum(sc[:, 0:1], w[:, :], AX.X,
                                                apply_absolute_value=True))
        b.emit("v", lambda E, cols=cols: E.tensor_scalar(
            sc[:, 1:2], sc[:, 0:1], 1.0 / cols, 1e-5, ALU.mult, ALU.max))
        b.emit("v", lambda E: E.reciprocal(sc[:, 2:3], sc[:, 1:2]))
        b.emit("v", lambda E, w=w: E.tensor_scalar(
            tm[:, :], w[:, :], sc[:, 2:3], None, ALU.mult))
        b.emit("v", lambda E: E.tensor_scalar(
            tm[:, :], tm[:, :], RNE, -RNE, ALU.add, ALU.add))
        b.emit("v", lambda E: E.tensor_scalar(
            tm[:, :], tm[:, :], -1.0, 1.0, ALU.max, ALU.min))
        b.wsem("v", soo)
        b.emit("v", lambda E, sg=sg: E.tensor_scalar(
            sg[:, :], tm[:, :], sc[:, 1:2], None, ALU.mult))
        rel[("w", i % 2)] = ("v", b.cnt["v"])
        b.wait("y", "v", b.cnt["v"])
        b.dma("y", soo, lambda E, sg=sg, d=dst_scr, i=i: E.dma_start(
            d.ap()[i * 128:(i + 1) * 128, :], sg[:, :]))
    b.barrier()
    b.sfree(sl_ + so)
    st.close()
    return (list(so), b.cnt["v"], None)


def _ld_natural(w_hbm, cols, col0=0):
    def fn(b, dst_sb, slot):
        b.dma("g", slot, _w_cast_load(dst_sb, w_hbm, cols, col0=col0))
    return fn


def _ld_transposed(w_scr, rows, nchunk):
    def fn(b, dst_sb, slot):
        for a in range(nchunk):
            b.dma("y", slot, lambda E, d=dst_sb, s=w_scr, a=a, rows=rows: E.dma_start_transpose(
                d.ap().rearrange("p (a f) -> p a f", a=nchunk)[:, a, :],
                s.ap()[:, a * 128:(a + 1) * 128]))
    return fn


def emit_ffn(b, xT_scr, w1_ld, w2_ld, dst, res_src, act1, hwidth=4096,
             h_bias_col=None, out_bias_row=None, out_bias_off=0,
             extra_res=None, lowrank=None, gamma_bc=None, io=None):
    st = ExitStack()
    nhc = hwidth // 128
    w1 = b.sb("ff_w1", [128, 8 * hwidth], BF16, st)
    w2 = b.sb("ff_w2", [128, nhc * 1024], BF16, st)
    hT = b.sb("ff_h", [128, nhc * 512], BF16, st)
    xTb = [b.sb(f"ff_x{i}", [128, 8 * 512], BF16, st) for i in range(2)]
    rtmp = b.sb("ff_rt", [128, 512], F32, st)
    res = [b.sb(f"ff_r{i}", [128, C], F32, st) for i in range(2)]
    ex = [b.sb(f"ff_e{i}", [128, C], F32, st) for i in range(2)] if extra_res is not None else None
    outb = [b.sb(f"ff_o{i}", [128, C], F32, st) for i in range(2)]
    br = b.sb("ff_br", [1, C], BF16, st) if out_bias_row is not None else None
    ph = [b.ps(f"ff_ph{i}", [128, 512], F32, st) for i in range(2)]
    pout = [b.ps(f"ff_po{i}", [128, 512], F32, st) for i in range(2)]
    sw1, sw2 = b.ssem(), b.ssem()
    sx = b.ssems(2)
    sr = b.ssems(2)
    se = b.ssems(2) if extra_res is not None else None
    sd = b.ssems(2)
    w1_ld(b, w1, sw1)
    w2_ld(b, w2, sw2)
    if out_bias_row is not None:
        b.dma("g", sw2, lambda E, d=br, s=out_bias_row, o=out_bias_off: E.dma_start(
            d[0:1, :], row_ap(s, o, C)))
    lr = None
    if lowrank is not None:
        lr = {}
        lr["vT"] = b.sb("ff_vT", [128, 8 * 16], BF16, st)
        lr["fbT"] = b.sb("ff_fb", [16, 16], BF16, st)
        lr["uT"] = b.sb("ff_uT", [16, hwidth], BF16, st)
        lr["t1"] = b.sb("ff_t1", [16, 512], BF16, st)
        lr["t2"] = b.sb("ff_t2", [16, 512], BF16, st)
        lr["plr"] = b.ps("ff_plr", [16, 512], F32, st)
        b.dma("g", sw1, _w_cast_load(lr["vT"], lowrank["vT"], 16))
        b.dma("g", sw1, lambda E, d=lr["fbT"], s=lowrank["fbT"]: E.dma_start(
            d[:, :], s.ap()))
        b.dma("g", sw1, lambda E, d=lr["uT"], s=lowrank["uT"]: E.dma_start(
            d[:, :], s.ap()))
        gate_col = lowrank["gate_col"]
    b.barrier()
    rel = {}
    for tb in range(NTC):
        xb, sxx = xTb[tb % 2], sx[tb % 2]
        if ("x", tb % 2) in rel:
            e, c = rel[("x", tb % 2)]
            b.wait("y", e, c)
        b.dma("y", sxx, lambda E, d=xb, s=xT_scr, tb=tb: E.dma_start(
            d.ap().rearrange("p (a f) -> p a f", a=8),
            s.ap()[:, tb * 512:(tb + 1) * 512].rearrange("(a p) f -> p a f", p=128)))
        b.wsem("t", sxx)
        if lowrank is not None:
            plr = lr["plr"]
            b.wait("t", "v", b.cnt["v"])
            b.wait("t", "s", b.cnt["s"])
            for a in range(8):
                b.emit("t", lambda E, plr=plr, xb=xb, a=a: E.matmul(
                    plr[:, :], lr["vT"][:, a * 16:(a + 1) * 16], xb[:, a * 512:(a + 1) * 512],
                    start=(a == 0), stop=(a == 7)))
            b.wait("s", "t", b.cnt["t"])
            b.emit("s", lambda E: E.copy(lr["t1"][:, :], plr[:, :]))
            b.wait("t", "s", b.cnt["s"])
            b.emit("t", lambda E: E.matmul(plr[:, :], lr["fbT"][:, :], lr["t1"][:, :],
                                           start=True, stop=True))
            b.wait("v", "t", b.cnt["t"])
            b.emit("v", lambda E, gc=gate_col: E.tensor_scalar(
                lr["t2"][:, :], plr[:, :], gc[0:16, 0:1], None, ALU.mult))
            b.wait("t", "v", b.cnt["v"])
        for ft in range(nhc):
            po = ph[ft % 2]
            if ("ph", ft % 2) in rel:
                e, c = rel[("ph", ft % 2)]
                b.wait("t", e, c)
            nlast = 8 if lowrank is None else 9
            for a in range(8):
                b.emit("t", lambda E, po=po, xb=xb, a=a, ft=ft, hw=hwidth: E.matmul(
                    po[:, :], w1[:, a * hw + ft * 128: a * hw + ft * 128 + 128],
                    xb[:, a * 512:(a + 1) * 512],
                    start=(a == 0), stop=(a == nlast - 1)))
            if lowrank is not None:
                b.emit("t", lambda E, po=po, ft=ft: E.matmul(
                    po[:, :], lr["uT"][:, ft * 128:ft * 128 + 128], lr["t2"][:, :],
                    start=False, stop=True))
            if act1 == "relu2":
                b.wait("s", "t", b.cnt["t"])
                b.wait("s", "v", b.cnt["v"])
                b.emit("s", lambda E, po=po: E.activation(rtmp[:, :], po[:, :], AF.Relu))
                b.wait("v", "s", b.cnt["s"])
                b.emit("v", lambda E, ft=ft: E.tensor_tensor(
                    hT[:, ft * 512:(ft + 1) * 512], rtmp[:, :], rtmp[:, :], ALU.mult))
                rel[("ph", ft % 2)] = ("v", b.cnt["v"])
                rel[("hT",)] = ("v", b.cnt["v"])
            else:
                fn = AF.Gelu_apprx_tanh if act1 == "gelu_tanh" else AF.Gelu
                kw = {}
                if h_bias_col is not None:
                    kw["bias"] = h_bias_col[:, ft:ft + 1]
                b.wait("s", "t", b.cnt["t"])
                b.emit("s", lambda E, po=po, ft=ft, fn=fn, kw=kw: E.activation(
                    hT[:, ft * 512:(ft + 1) * 512], po[:, :], fn, **kw))
                rel[("ph", ft % 2)] = ("s", b.cnt["s"])
                rel[("hT",)] = ("s", b.cnt["s"])
        b.wsem("t", sw2)
        e, c = rel[("hT",)]
        b.wait("t", e, c)
        for tl in range(4):
            tt = tb * 4 + tl
            rs, srr = res[tt % 2], sr[tt % 2]
            ot, sdd = outb[tt % 2], sd[tt % 2]
            if ("rs", tt % 2) in rel:
                e, c = rel[("rs", tt % 2)]
                b.wait("y", e, c)
            b.dma("y", srr, lambda E, d=rs, s=res_src, tt=tt: E.dma_start(
                d[:, :], s.ap()[tt * 128:(tt + 1) * 128, :]))
            if extra_res is not None:
                b.dma("y", se[tt % 2], lambda E, d=ex[tt % 2], s=extra_res, tt=tt: E.dma_start(
                    d[:, :], s.ap()[tt * 128:(tt + 1) * 128, :]))
            for fh in range(2):
                po = pout[fh]
                if ("pout", fh) in rel:
                    e, c = rel[("pout", fh)]
                    b.wait("t", e, c)
                for hc in range(nhc):
                    b.emit("t", lambda E, po=po, hc=hc, tl=tl, fh=fh: E.matmul(
                        po[:, :], hT[:, hc * 512 + tl * 128: hc * 512 + tl * 128 + 128],
                        w2[:, hc * 1024 + fh * 512: hc * 1024 + fh * 512 + 512],
                        start=(hc == 0), stop=(out_bias_row is None and hc == nhc - 1)))
                if out_bias_row is not None:
                    b.emit("t", lambda E, po=po, fh=fh, ob=b.cb["ones_b"]: E.matmul(
                        po[:, :], ob[0:1, 0:128], br[0:1, fh * 512:(fh + 1) * 512],
                        start=False, stop=True))
                b.wait("v", "t", b.cnt["t"])
                b.wsem("v", srr)
                b.wsem("v", sdd)
                if extra_res is not None:
                    b.wsem("v", se[tt % 2])
                if gamma_bc is None:
                    b.emit("v", lambda E, ot=ot, rs=rs, po=po, fh=fh: E.tensor_tensor(
                        ot[:, fh * 512:(fh + 1) * 512], rs[:, fh * 512:(fh + 1) * 512],
                        po[:, :], ALU.add))
                else:
                    b.emit("v", lambda E, po=po, fh=fh, g=gamma_bc: E.tensor_tensor(
                        rtmp[:, :], po[:, :], g[:, fh * 512:(fh + 1) * 512], ALU.mult))
                    b.emit("v", lambda E, ot=ot, rs=rs, fh=fh: E.tensor_tensor(
                        ot[:, fh * 512:(fh + 1) * 512], rs[:, fh * 512:(fh + 1) * 512],
                        rtmp[:, :], ALU.add))
                if extra_res is not None:
                    b.emit("v", lambda E, ot=ot, fh=fh, x_=ex[tt % 2]: E.tensor_tensor(
                        ot[:, fh * 512:(fh + 1) * 512], ot[:, fh * 512:(fh + 1) * 512],
                        x_[:, fh * 512:(fh + 1) * 512], ALU.add))
                rel[("pout", fh)] = ("v", b.cnt["v"])
            rel[("rs", tt % 2)] = ("v", b.cnt["v"])
            b.wait("y", "v", b.cnt["v"])
            b.dma("y", sdd, lambda E, s_=ot, d=dst, tt=tt: E.dma_start(
                d.ap()[tt * 128:(tt + 1) * 128, :], s_[:, :]))
        rel[("x", tb % 2)] = ("t", b.cnt["t"])
    b.barrier()
    b.sfree([sw1, sw2] + sx + sr + (se or []) + sd)
    st.close()


def emit_attn_softmax(b, qT_scr, kT_scr, vn_scr, attT_scr, Lk, mode,
                      masks_sb, scorer_col=None):
    st = ExitStack()
    nkb = Lk // 128
    qTp = b.sb("sa_q", [128, T], BF16, st)
    kTp = b.sb("sa_k", [128, Lk], BF16, st)
    vp = [b.sb(f"sa_v{i}", [128, nkb * 65], BF16, st) for i in range(2)]
    eb = [b.sb(f"sa_e{i}", [128, 4 * 512], BF16, st) for i in range(2)]
    on_sb = [b.sb(f"sa_on{i}", [128, 512], F32, st) for i in range(2)]
    d_sb = b.sb("sa_d", [128, 512], F32, st)
    obh = [b.sb(f"sa_ob{i}", [64, T], BF16, st) for i in range(2)]
    PS = [b.ps(f"sa_ps{i}", [128, 512], F32, st) for i in range(4)]
    PO = [b.ps(f"sa_po{i}", [65, 512], F32, st) for i in range(2)]
    PB = b.ps("sa_pb", [64, 512], F32, st)
    sl = {k: b.ssem() for k in ("q", "k", "v0", "v1", "ob0", "ob1")}
    rel = {}
    for p in range(NH // 2):
        if ("pair",) in rel:
            e, c = rel[("pair",)]
            b.wait("y", e, c)
        b.dma("y", sl["q"], lambda E, d=qTp, s=qT_scr, p=p: E.dma_start(
            d[:, :], s.ap()[p * 128:(p + 1) * 128, :]))
        b.dma("y", sl["k"], lambda E, d=kTp, s=kT_scr, p=p: E.dma_start(
            d[:, :], s.ap()[p * 128:(p + 1) * 128, :]))
        for hh in range(2):
            vb = vp[hh]
            if ("pair",) in rel:
                e, c = rel[("pair",)]
                b.wait("v", e, c)
            b.emit("v", lambda E, d=vb, n=nkb: E.memset(
                d.ap().rearrange("p (a f) -> p a f", a=n)[:, :, 64:65], 1.0))
            b.wait("y", "v", b.cnt["v"])
            b.dma("y", sl["v" + str(hh)], lambda E, d=vb, s=vn_scr, p=p, hh=hh, n=nkb: E.dma_start(
                d.ap().rearrange("p (a f) -> p a f", a=n)[:, :, 0:64],
                s.ap()[:, p * 128 + hh * 64: p * 128 + hh * 64 + 64].rearrange(
                    "(a p) f -> p a f", p=128)))
        for e_ in ("t", "v", "s"):
            for k_ in ("q", "k", "v0", "v1"):
                b.wsem(e_, sl[k_])
        epis = []
        for qc in range(NTC):
            if mode == "self":
                kbs = list(range(0, 4 * qc + 4))
            else:
                kbs = list(range(0, min(nkb, 4 * qc + 5)))
            pend = []
            first = [True, True]

            def emit_sv(item):
                hh, kb, slot, scnt, vcnt = item
                b.wait("t", "s", scnt)
                if vcnt:
                    b.wait("t", "v", vcnt)
                if first[hh] and ("po", hh) in rel:
                    e, c = rel[("po", hh)]
                    b.wait("t", e, c)
                b.emit("t", lambda E, hh=hh, kb=kb, slot=slot,
                       fst=first[hh], lst=(kb == kbs[-1]): E.matmul(
                    PO[hh][:, :], vp[hh][:, kb * 65:kb * 65 + 65],
                    eb[hh][:, slot * 512:(slot + 1) * 512], start=fst, stop=lst))
                first[hh] = False
                rel[("eb", hh, slot)] = ("t", b.cnt["t"])

            for i, kb in enumerate(kbs):
                if i == 1 and epis:
                    epis.pop(0)()
                for hh in range(2):
                    hb = 64 * hh
                    i2 = hh * 2 + (i % 2)
                    pS = PS[i2]
                    slot = i % 4
                    jdiag = kb - 4 * qc
                    if ("ps", i2) in rel:
                        e, c = rel[("ps", i2)]
                        b.wait("t", e, c)
                    b.emit("t", lambda E, pS=pS, hb=hb, kb=kb, qc=qc: E.matmul(
                        pS[:, :], kTp[hb:hb + 64, kb * 128:kb * 128 + 128],
                        qTp[hb:hb + 64, qc * 512:qc * 512 + 512],
                        start=True, stop=True))
                    b.wait("s", "t", b.cnt["t"])
                    if ("eb", hh, slot) in rel:
                        e, c = rel[("eb", hh, slot)]
                        b.wait("s", e, c)
                    if scorer_col is None:
                        bias = ccol(b, -16.0)
                    else:
                        bias = scorer_col[:, kb:kb + 1]
                    b.emit("s", lambda E, pS=pS, hh=hh, slot=slot, bias=bias: E.activation(
                        eb[hh][:, slot * 512:(slot + 1) * 512], pS[:, :], AF.Exp,
                        bias=bias, scale=0.125), dep=4)
                    rel[("ps", i2)] = ("s", b.cnt["s"])
                    scnt = b.cnt["s"]
                    vcnt = 0
                    if jdiag >= 0:
                        b.wait("v", "s", scnt)
                        b.emit("v", lambda E, hh=hh, slot=slot, ms=masks_sb, j=jdiag: E.tensor_tensor(
                            eb[hh][:, slot * 512:(slot + 1) * 512],
                            eb[hh][:, slot * 512:(slot + 1) * 512],
                            ms[:, j * 512:(j + 1) * 512], ALU.mult), dep=4)
                        vcnt = b.cnt["v"]
                    pend.append((hh, kb, slot, scnt, vcnt))
                while len(pend) > 4:
                    emit_sv(pend.pop(0))
            for item in pend:
                emit_sv(item)
            svdone = b.cnt["t"]

            def mk_epi(qc=qc, svdone=svdone):
                def epi():
                    for hh in range(2):
                        po = PO[hh]
                        b.wait("v", "t", svdone)
                        b.emit("v", lambda E, po=po, hh=hh: E.reciprocal(
                            d_sb[64 * hh:64 * hh + 1, :], po[64:65, :]))
                        vdone = b.cnt["v"]
                        b.wait("t", "v", vdone)
                        if ("pb",) in rel:
                            e, c = rel[("pb",)]
                            b.wait("t", e, c)
                        b.emit("t", lambda E, hh=hh, of=b.cb["ones_f"]: E.matmul(
                            PB[:, :], of[64 * hh:64 * hh + 1, 0:64],
                            d_sb[64 * hh:64 * hh + 1, :], start=True, stop=True))
                        b.wait("s", "t", b.cnt["t"])
                        b.emit("s", lambda E, po=po, hh=hh: E.copy(on_sb[hh][0:64, :], po[0:64, :]))
                        b.wait("v", "s", b.cnt["s"])
                        b.wsem("v", sl["ob" + str(hh)])
                        b.emit("v", lambda E, hh=hh, g0=qc * 512: E.tensor_tensor(
                            obh[hh][0:64, g0:g0 + 512], on_sb[hh][0:64, :], PB[0:64, :], ALU.mult))
                        rel[("pb",)] = ("v", b.cnt["v"])
                        rel[("po", hh)] = ("v", b.cnt["v"])
                return epi
            epis.append(mk_epi())
        for ep in epis:
            ep()
        epis = []
        for hh in range(2):
            b.wait("y", "v", b.cnt["v"])
            b.dma("y", sl["ob" + str(hh)], lambda E, d=attT_scr, s=obh[hh], p=p, hh=hh: E.dma_start(
                d.ap()[(p * 2 + hh) * 64:(p * 2 + hh) * 64 + 64, :], s[:, :]))
        rel[("pair",)] = ("t", b.cnt["t"])
    b.barrier()
    b.sfree(list(sl.values()))
    st.close()


def emit_rope(b, src_scr, dst_scr, cos_sb, sin_sb, swpT_sb):
    st = ExitStack()
    qp = [b.sb(f"ro_q{i}", [128, T], BF16, st) for i in range(2)]
    stg = [b.sb(f"ro_s{i}", [128, T], BF16, st) for i in range(2)]
    tm1 = b.sb("ro_t1", [128, 512], F32, st)
    tm2 = b.sb("ro_t2", [128, 512], F32, st)
    PW = [b.ps(f"ro_p{i}", [128, 512], F32, st) for i in range(2)]
    sq = b.ssems(2)
    so = b.ssems(2)
    rel = {}
    for p in range(NH // 2):
        q, sqq = qp[p % 2], sq[p % 2]
        sg, soo = stg[p % 2], so[p % 2]
        if ("q", p % 2) in rel:
            e, c = rel[("q", p % 2)]
            b.wait("y", e, c)
        b.dma("y", sqq, lambda E, d=q, s=src_scr, p=p: E.dma_start(
            d[:, :], s.ap()[p * 128:(p + 1) * 128, :]))
        b.wsem("t", sqq)
        b.wsem("v", sqq)
        b.wsem("s", soo)
        b.wsem("v", soo)
        for qc in range(NTC):
            po = PW[qc % 2]
            if ("pw", qc % 2) in rel:
                e, c = rel[("pw", qc % 2)]
                b.wait("t", e, c)
            b.emit("t", lambda E, po=po, q=q, qc=qc: E.matmul(
                po[:, :], swpT_sb[:, :], q[:, qc * 512:qc * 512 + 512],
                start=True, stop=True))
            b.wait("v", "t", b.cnt["t"])
            b.emit("v", lambda E, q=q, qc=qc: E.tensor_tensor(
                tm1[:, :], q[:, qc * 512:qc * 512 + 512],
                cos_sb[:, qc * 512:qc * 512 + 512], ALU.mult))
            b.emit("v", lambda E, po=po, qc=qc: E.tensor_tensor(
                tm2[:, :], po[:, :], sin_sb[:, qc * 512:qc * 512 + 512], ALU.mult))
            rel[("pw", qc % 2)] = ("v", b.cnt["v"])
            b.emit("v", lambda E, sg=sg, qc=qc: E.tensor_tensor(
                sg[:, qc * 512:qc * 512 + 512], tm1[:, :], tm2[:, :], ALU.add))
        rel[("q", p % 2)] = ("v", b.cnt["v"])
        b.wait("y", "v", b.cnt["v"])
        b.dma("y", soo, lambda E, sg=sg, d=dst_scr, p=p: E.dma_start(
            d.ap()[p * 128:(p + 1) * 128, :], sg[:, :]))
    b.barrier()
    b.sfree(sq + so)
    st.close()


def emit_attn_s5(b, qT_scr, kT_scr, qrT_scr, krT_scr, vn_scr, attT_scr,
                 masks_sb, identf_sb):
    st = ExitStack()
    qTp = b.sb("s5_q", [128, T], BF16, st)
    kTp = b.sb("s5_k", [128, T], BF16, st)
    qrp = b.sb("s5_qr", [128, T], BF16, st)
    krp = b.sb("s5_kr", [128, T], BF16, st)
    vnp = [b.sb(f"s5_v{i}", [128, NT * 64], BF16, st) for i in range(2)]
    db = [b.sb(f"s5_db{i}", [128, 2 * 512], BF16, st) for i in range(2)]
    eb = [b.sb(f"s5_e{i}", [128, 4 * 512], BF16, st) for i in range(2)]
    dp = [[b.sb(f"s5_dp{i}{j}", [128, 8], F32, st) for j in range(2)] for i in range(2)]
    dc = [[b.sb(f"s5_dc{i}{j}", [128, 1], F32, st) for j in range(2)] for i in range(2)]
    denR = [b.sb(f"s5_dr{i}", [1, T], F32, st) for i in range(2)]
    on_sb = [b.sb(f"s5_on{i}", [128, 512], F32, st) for i in range(2)]
    obh = [b.sb(f"s5_ob{i}", [64, T], BF16, st) for i in range(2)]
    PS = [b.ps(f"s5_ps{i}", [128, 512], F32, st) for i in range(4)]
    PO = [b.ps(f"s5_po{i}", [65, 512], F32, st) for i in range(2)]
    PB = b.ps("s5_pb", [64, 512], F32, st)
    sl = {k: b.ssem() for k in ("q", "k", "qr", "kr", "v0", "v1", "ob0", "ob1")}
    rel = {}
    for p in range(NH // 2):
        if ("pair",) in rel:
            e, c = rel[("pair",)]
            b.wait("y", e, c)
        for key, buf, s_ in (("q", qTp, qT_scr), ("k", kTp, kT_scr),
                             ("qr", qrp, qrT_scr), ("kr", krp, krT_scr)):
            b.dma("y", sl[key], lambda E, d=buf, s=s_, p=p: E.dma_start(
                d[:, :], s.ap()[p * 128:(p + 1) * 128, :]))
        for hh in range(2):
            b.dma("y", sl["v" + str(hh)], lambda E, d=vnp[hh], s=vn_scr, p=p, hh=hh: E.dma_start(
                d.ap().rearrange("p (a f) -> p a f", a=NT),
                s.ap()[:, p * 128 + hh * 64: p * 128 + hh * 64 + 64].rearrange(
                    "(a p) f -> p a f", p=128)))
        for e_ in ("t", "v", "s"):
            for k_ in ("q", "k", "qr", "kr", "v0", "v1"):
                b.wsem(e_, sl[k_])
        b.wait("s", "t", b.cnt["t"])
        tails = []
        for qt in range(NT):
            qp = qt % 2
            for kc in range(NTC):
                for hh in range(2):
                    hb = 64 * hh
                    i2 = hh * 2 + (kc % 2)
                    ps = PS[i2]
                    slot = kc % 2
                    if ("ps", i2) in rel:
                        e, c = rel[("ps", i2)]
                        b.wait("t", e, c)
                    b.emit("t", lambda E, ps=ps, hb=hb, qt=qt, kc=kc: E.matmul(
                        ps[:, :], qTp[hb:hb + 64, qt * 128:qt * 128 + 128],
                        kTp[hb:hb + 64, kc * 512:kc * 512 + 512],
                        start=True, stop=True))
                    b.wait("s", "t", b.cnt["t"])
                    if ("db", hh, slot) in rel:
                        e, c = rel[("db", hh, slot)]
                        b.wait("s", e, c)
                    b.emit("s", lambda E, ps=ps, hh=hh, slot=slot: E.activation(
                        db[hh][:, slot * 512:(slot + 1) * 512], ps[:, :],
                        AF.Exp, scale=0.125), dep=4)
                    rel[("ps", i2)] = ("s", b.cnt["s"])
                    b.wait("v", "s", b.cnt["s"])
                    if ("dp", hh, qp) in rel:
                        e, c = rel[("dp", hh, qp)]
                        b.wait("v", e, c)
                    b.emit("v", lambda E, hh=hh, slot=slot, qp=qp, kc=kc: E.reduce_sum(
                        dp[hh][qp][:, kc:kc + 1],
                        db[hh][:, slot * 512:(slot + 1) * 512], AX.X), dep=4)
                    rel[("db", hh, slot)] = ("v", b.cnt["v"])
            def mk_tail(qt=qt, qp=qp):
                def tail():
                    for hh in range(2):
                        b.emit("v", lambda E, hh=hh, qp=qp: E.reduce_sum(
                            dc[hh][qp][:, 0:1], dp[hh][qp][:, :], AX.X))
                        b.emit("v", lambda E, hh=hh, qp=qp: E.reciprocal(
                            dc[hh][qp][:, 0:1], dc[hh][qp][:, 0:1]))
                        rel[("dp", hh, qp)] = ("v", b.cnt["v"])
                        vdone = b.cnt["v"]
                        b.wait("t", "v", vdone)
                        if ("ptb", hh, qp) in rel:
                            e, c = rel[("ptb", hh, qp)]
                            b.wait("t", e, c)
                        b.emit("t", lambda E, hh=hh, qp=qp: E.transpose(
                            PO[hh][0:1, 0:128], dc[hh][qp][:, 0:1], identf_sb[:, :]))
                        b.wait("s", "t", b.cnt["t"])
                        b.emit("s", lambda E, hh=hh, qt=qt: E.copy(
                            denR[hh][0:1, qt * 128:qt * 128 + 128], PO[hh][0:1, 0:128]))
                        rel[("ptb", hh, qp)] = ("s", b.cnt["s"])
                return tail
            tails.append(mk_tail())
            if len(tails) > 1:
                tails.pop(0)()
        tails.pop(0)()
        b.wait("t", "s", b.cnt["s"])
        epis = []
        for qc in range(NTC):
            kbs = list(range(0, 4 * qc + 4))
            pend = []
            first = [True, True]

            def emit_sv(item):
                hh, kb, slot, scnt, vcnt = item
                b.wait("t", "s", scnt)
                if vcnt:
                    b.wait("t", "v", vcnt)
                if first[hh] and ("po", hh) in rel:
                    e, c = rel[("po", hh)]
                    b.wait("t", e, c)
                b.emit("t", lambda E, hh=hh, kb=kb, slot=slot,
                       fst=first[hh], lst=(kb == kbs[-1]): E.matmul(
                    PO[hh][0:64, :], vnp[hh][:, kb * 64:kb * 64 + 64],
                    eb[hh][:, slot * 512:(slot + 1) * 512], start=fst, stop=lst))
                first[hh] = False
                rel[("eb", hh, slot)] = ("t", b.cnt["t"])

            for i, kb in enumerate(kbs):
                if i == 1 and epis:
                    epis.pop(0)()
                for hh in range(2):
                    hb = 64 * hh
                    i2 = hh * 2 + (i % 2)
                    pS = PS[i2]
                    slot = i % 4
                    jdiag = kb - 4 * qc
                    if ("ps", i2) in rel:
                        e, c = rel[("ps", i2)]
                        b.wait("t", e, c)
                    b.emit("t", lambda E, pS=pS, hb=hb, kb=kb, qc=qc: E.matmul(
                        pS[:, :], krp[hb:hb + 64, kb * 128:kb * 128 + 128],
                        qrp[hb:hb + 64, qc * 512:qc * 512 + 512],
                        start=True, stop=True))
                    b.wait("s", "t", b.cnt["t"])
                    if ("eb", hh, slot) in rel:
                        e, c = rel[("eb", hh, slot)]
                        b.wait("s", e, c)
                    b.emit("s", lambda E, pS=pS, hh=hh, slot=slot: E.activation(
                        eb[hh][:, slot * 512:(slot + 1) * 512], pS[:, :],
                        AF.Exp, scale=0.125), dep=4)
                    rel[("ps", i2)] = ("s", b.cnt["s"])
                    scnt = b.cnt["s"]
                    vcnt = 0
                    if jdiag >= 0:
                        b.wait("v", "s", scnt)
                        b.emit("v", lambda E, hh=hh, slot=slot, ms=masks_sb, j=jdiag: E.tensor_tensor(
                            eb[hh][:, slot * 512:(slot + 1) * 512],
                            eb[hh][:, slot * 512:(slot + 1) * 512],
                            ms[:, j * 512:(j + 1) * 512], ALU.mult), dep=4)
                        vcnt = b.cnt["v"]
                    pend.append((hh, kb, slot, scnt, vcnt))
                while len(pend) > 4:
                    emit_sv(pend.pop(0))
            for item in pend:
                emit_sv(item)
            svdone = b.cnt["t"]

            def mk_epi(qc=qc, svdone=svdone):
                def epi():
                    for hh in range(2):
                        po = PO[hh]
                        b.wait("t", "v", b.cnt["v"])
                        if ("pb",) in rel:
                            e, c = rel[("pb",)]
                            b.wait("t", e, c)
                        b.wait("t", "t", svdone) if False else None
                        b.emit("t", lambda E, hh=hh, qc=qc, of=b.cb["ones_f"]: E.matmul(
                            PB[:, :], of[0:1, 0:64], denR[hh][0:1, qc * 512:qc * 512 + 512],
                            start=True, stop=True))
                        b.wait("s", "t", b.cnt["t"])
                        b.emit("s", lambda E, po=po, hh=hh: E.copy(on_sb[hh][0:64, :], po[0:64, :]))
                        b.wait("v", "s", b.cnt["s"])
                        b.wsem("v", sl["ob" + str(hh)])
                        b.emit("v", lambda E, hh=hh, g0=qc * 512: E.tensor_tensor(
                            obh[hh][0:64, g0:g0 + 512], on_sb[hh][0:64, :], PB[0:64, :], ALU.mult))
                        rel[("pb",)] = ("v", b.cnt["v"])
                        rel[("po", hh)] = ("v", b.cnt["v"])
                return epi
            epis.append(mk_epi())
        for ep in epis:
            ep()
        for hh in range(2):
            b.wait("y", "v", b.cnt["v"])
            b.dma("y", sl["ob" + str(hh)], lambda E, d=attT_scr, s=obh[hh], p=p, hh=hh: E.dma_start(
                d.ap()[(p * 2 + hh) * 64:(p * 2 + hh) * 64 + 64, :], s[:, :]))
        rel[("pair",)] = ("t", b.cnt["t"])
    b.barrier()
    b.sfree(list(sl.values()))
    st.close()


def emit_attn_global(b, qT_scr, kn_scr, vn_scr, attT_scr, eps=1e-6):
    st = ExitStack()
    qTp = b.sb("ga_q", [128, T], BF16, st)
    knp = b.sb("ga_kn", [128, NT * 128], BF16, st)
    vp = [b.sb(f"ga_v{i}", [128, NT * 65], BF16, st) for i in range(2)]
    state_b = b.sb("ga_stb", [128, 65], BF16, st)
    on_sb = b.sb("ga_on", [128, 512], F32, st)
    d_sb = b.sb("ga_d", [128, 512], F32, st)
    obh = b.sb("ga_ob", [64, T], BF16, st)
    PO = b.ps("ga_po", [65, 512], F32, st)
    PK = b.ps("ga_pk", [128, 65], F32, st)
    PB = b.ps("ga_pb", [64, 512], F32, st)
    sl = {k: b.ssem() for k in ("q", "kn", "v0", "v1", "ob")}
    rel = {}
    for p in range(NH // 2):
        if ("pair",) in rel:
            e, c = rel[("pair",)]
            b.wait("y", e, c)
        b.dma("y", sl["q"], lambda E, d=qTp, s=qT_scr, p=p: E.dma_start(
            d[:, :], s.ap()[p * 128:(p + 1) * 128, :]))
        b.dma("y", sl["kn"], lambda E, d=knp, s=kn_scr, p=p: E.dma_start(
            d.ap().rearrange("p (a f) -> p a f", a=NT),
            s.ap()[:, p * 128:(p + 1) * 128].rearrange("(a p) f -> p a f", p=128)))
        for hh in range(2):
            vb = vp[hh]
            if ("pair",) in rel:
                e, c = rel[("pair",)]
                b.wait("v", e, c)
            b.emit("v", lambda E, d=vb: E.memset(
                d.ap().rearrange("p (a f) -> p a f", a=NT)[:, :, 64:65], 1.0))
            b.wait("y", "v", b.cnt["v"])
            b.dma("y", sl["v" + str(hh)], lambda E, d=vb, s=vn_scr, p=p, hh=hh: E.dma_start(
                d.ap().rearrange("p (a f) -> p a f", a=NT)[:, :, 0:64],
                s.ap()[:, p * 128 + hh * 64: p * 128 + hh * 64 + 64].rearrange(
                    "(a p) f -> p a f", p=128)))
        for e_ in ("t", "v", "s"):
            for k_ in ("q", "kn", "v0", "v1"):
                b.wsem(e_, sl[k_])
        for hh in range(2):
            hb = 64 * hh
            vb = vp[hh]
            b.wait("t", "v", b.cnt["v"])
            b.wait("t", "s", b.cnt["s"])
            for kb in range(NT):
                b.emit("t", lambda E, kb=kb, hb=hb, vb=vb: E.matmul(
                    PK[hb:hb + 64, :], knp[:, kb * 128 + hb: kb * 128 + hb + 64],
                    vb[:, kb * 65:kb * 65 + 65], start=(kb == 0), stop=(kb == NT - 1)))
            b.wait("s", "t", b.cnt["t"])
            b.emit("s", lambda E, hb=hb: E.copy(state_b[hb:hb + 64, :], PK[hb:hb + 64, :]))
            b.wait("t", "s", b.cnt["s"])
            for qc in range(NTC):
                g0 = qc * 512
                b.emit("t", lambda E, hb=hb, g0=g0: E.matmul(
                    PO[:, :], state_b[hb:hb + 64, :], qTp[hb:hb + 64, g0:g0 + 512],
                    start=True, stop=True))
                b.wait("v", "t", b.cnt["t"])
                b.emit("v", lambda E, eps=eps: E.tensor_scalar(
                    d_sb[64:65, :], PO[64:65, :], float(eps), None, ALU.add))
                b.emit("v", lambda E: E.reciprocal(d_sb[64:65, :], d_sb[64:65, :]))
                b.wait("t", "v", b.cnt["v"])
                b.emit("t", lambda E, of=b.cb["ones_f"]: E.matmul(
                    PB[:, :], of[64:65, 0:64], d_sb[64:65, :], start=True, stop=True))
                b.wait("s", "t", b.cnt["t"])
                b.emit("s", lambda E: E.copy(on_sb[0:64, :], PO[0:64, :]))
                b.wait("v", "s", b.cnt["s"])
                b.wsem("v", sl["ob"])
                b.emit("v", lambda E, g0=g0: E.tensor_tensor(
                    obh[0:64, g0:g0 + 512], on_sb[0:64, :], PB[0:64, :], ALU.mult))
                b.wait("t", "v", b.cnt["v"])
            b.wait("y", "v", b.cnt["v"])
            b.dma("y", sl["ob"], lambda E, d=attT_scr, s=obh, p=p, hh=hh: E.dma_start(
                d.ap()[(p * 2 + hh) * 64:(p * 2 + hh) * 64 + 64, :], s[:, :]))
        rel[("pair",)] = ("t", b.cnt["t"])
    b.barrier()
    b.sfree(list(sl.values()))
    st.close()


def emit_moe(b, xT_scr, io, dst, res_src, identf_sb, rs_col):
    TB = 256
    NB = T // TB
    st = ExitStack()
    w1 = b.sb("mo_w1", [128, 8 * 4096], BF16, st)
    w2 = b.sb("mo_w2", [128, 32 * 1024], BF16, st)
    ws1 = b.sb("mo_s1", [128, 8 * 512], BF16, st)
    ws2 = b.sb("mo_s2", [128, 4 * 1024], BF16, st)
    gw = b.sb("mo_gw", [128, 8 * 8], BF16, st)
    b1c = b.sb("mo_b1", [128, 32], F32, st)
    bs1c = b.sb("mo_bs1", [128, 4], F32, st)
    b2b = b.sb("mo_b2", [8, 1024], BF16, st)
    bs2r = b.sb("mo_bs2", [1, 1024], BF16, st)
    hT = b.sb("mo_h", [128, 32 * TB], BF16, st)
    hsT = b.sb("mo_hs", [128, 4 * TB], BF16, st)
    xTb = [b.sb(f"mo_x{i}", [128, 8 * TB], BF16, st) for i in range(2)]
    sg_sb = b.sb("mo_sg", [128, 8], F32, st)
    gsc = b.sb("mo_gs", [128, 16], F32, st)
    gts = [b.sb(f"mo_gt{i}", [8, TB], F32, st) for i in range(2)]
    gtb = [b.sb(f"mo_gtb{i}", [8, TB], BF16, st) for i in range(2)]
    grow = [b.sb(f"mo_gr{i}", [1, 8 * TB], BF16, st) for i in range(2)]
    res = b.sb("mo_r", [128, C], F32, st)
    outb = b.sb("mo_o", [128, C], F32, st)
    pg = b.ps("mo_pg", [128, 512], F32, st)
    pgt = b.ps("mo_pgt", [8, 128], F32, st)
    ph = [b.ps(f"mo_ph{i}", [128, TB], F32, st) for i in range(2)]
    pb_ = [b.ps(f"mo_pb{i}", [128, TB], F32, st) for i in range(2)]
    pout = [b.ps(f"mo_po{i}", [128, 512], F32, st) for i in range(2)]
    sw = b.ssem()
    sx = b.ssems(2)
    sr = b.ssems(2)
    sd = b.ssems(2)
    sgr = b.ssems(2)
    b.dma("g", sw, _w_cast_load(w1, io["L1_ffn_W1cat"], 4096))
    b.dma("g", sw, _w_cast_load(w2, io["L1_ffn_W2cat"], 1024))
    b.dma("g", sw, _w_cast_load(ws1, io["L1_ffn_Ws1"], 512))
    b.dma("g", sw, _w_cast_load(ws2, io["L1_ffn_Ws2"], 1024))
    b.dma("g", sw, _w_cast_load(gw, io["L1_ffn_gate_W"], 8))
    b.dma("y", sw, lambda E, d=b1c, s=io["L1_ffn_b1cols"]: E.dma_start(d[:, :], s.ap()))
    b.dma("y", sw, lambda E, d=bs1c, s=io["L1_ffn_bs1cols"]: E.dma_start(d[:, :], s.ap()))
    b.dma("g", sw, lambda E, d=b2b, s=io["L1_ffn_b2"]: E.dma_start(d[:, :], s.ap()))
    b.dma("g", sw, lambda E, d=bs2r, s=io["L1_ffn_bs2"]: E.dma_start(
        d[0:1, :], row_ap(s, 0, C)))
    b.barrier()
    rel = {}
    ntl = TB // 128

    def load_x(tb):
        sxx = sx[tb % 2]
        if ("x", tb % 2) in rel:
            e, c = rel[("x", tb % 2)]
            b.wait("y", e, c)
        b.dma("y", sxx, lambda E, d=xTb[tb % 2], s=xT_scr, tb=tb: E.dma_start(
            d.ap().rearrange("p (a f) -> p a f", a=8),
            s.ap()[:, tb * TB:(tb + 1) * TB].rearrange("(a p) f -> p a f", p=128)))
        b.wsem("t", sxx)

    def emit_gates(tb):
        pr = tb % 2
        xb = xTb[pr]
        for tl in range(ntl):
            if ("gt",) in rel:
                e, c = rel[("gt",)]
                b.wait("t", e, c)
            b.wait("t", "v", b.cnt["v"])
            for a in range(8):
                b.emit("t", lambda E, xb=xb, a=a, tl=tl: E.matmul(
                    pg[:, 0:8], xb[:, a * TB + tl * 128: a * TB + tl * 128 + 128],
                    gw[:, a * 8:(a + 1) * 8], start=(a == 0), stop=(a == 7)))
            b.wait("s", "t", b.cnt["t"])
            b.wait("s", "v", b.cnt["v"])
            b.emit("s", lambda E: E.activation(sg_sb[:, :], pg[:, 0:8], AF.Sigmoid))
            b.wait("v", "s", b.cnt["s"])
            b.emit("v", lambda E, rs=rs_col: E.tensor_scalar(
                sg_sb[:, :], sg_sb[:, :], rs[:, 0:1], None, ALU.mult))
            b.emit("v", lambda E: E.reduce_max(gsc[:, 0:1], sg_sb[:, :], AX.X))
            b.emit("v", lambda E: E.tensor_scalar(
                gsc[:, 2:10], sg_sb[:, :], gsc[:, 0:1], None, ALU.is_ge))
            b.emit("v", lambda E: E.scalar_tensor_tensor(
                gsc[:, 2:10], gsc[:, 2:10], -1e30, sg_sb[:, :], ALU.mult, ALU.add))
            b.emit("v", lambda E: E.reduce_max(gsc[:, 1:2], gsc[:, 2:10], AX.X))
            b.emit("v", lambda E: E.tensor_tensor(
                gsc[:, 10:11], gsc[:, 0:1], gsc[:, 1:2], ALU.add))
            b.emit("v", lambda E: E.reciprocal(gsc[:, 10:11], gsc[:, 10:11]))
            b.emit("v", lambda E: E.tensor_scalar(
                gsc[:, 2:10], sg_sb[:, :], gsc[:, 1:2], None, ALU.is_ge))
            b.emit("v", lambda E: E.tensor_tensor(
                gsc[:, 2:10], gsc[:, 2:10], sg_sb[:, :], ALU.mult))
            b.emit("v", lambda E: E.tensor_scalar(
                sg_sb[:, :], gsc[:, 2:10], gsc[:, 10:11], None, ALU.mult))
            b.wait("t", "v", b.cnt["v"])
            b.emit("t", lambda E: E.transpose(pgt[:, :], sg_sb[:, :], identf_sb[:, :]))
            b.wait("s", "t", b.cnt["t"])
            b.emit("s", lambda E, pr=pr, tl=tl: E.copy(
                gts[pr][0:8, tl * 128:(tl + 1) * 128], pgt[0:8, :]))
            rel[("gt",)] = ("s", b.cnt["s"])
        b.wait("v", "s", b.cnt["s"])
        b.wsem("v", sgr[pr])
        b.emit("v", lambda E, pr=pr: E.tensor_copy(gtb[pr][:, :], gts[pr][:, :]))
        b.wait("y", "v", b.cnt["v"])
        b.dma("y", sgr[pr], lambda E, pr=pr: E.dma_start(grow[pr][0:1, :], gtb[pr][0:8, :]))

    load_x(0)
    emit_gates(0)
    for tb in range(NB):
        pr = tb % 2
        xb = xTb[pr]
        b.wsem("t", sgr[pr])
        b.wsem("t", sw)
        if tb + 1 < NB:
            load_x(tb + 1)
        hpend = []
        for ft in range(32):
            po = ph[ft % 2]
            if ("ph", ft % 2) in rel:
                e, c = rel[("ph", ft % 2)]
                b.wait("t", e, c)
            for a in range(8):
                b.emit("t", lambda E, po=po, xb=xb, a=a, ft=ft: E.matmul(
                    po[:, :], w1[:, a * 4096 + ft * 128: a * 4096 + ft * 128 + 128],
                    xb[:, a * TB:(a + 1) * TB], start=(a == 0), stop=(a == 7)))
            b.wait("s", "t", b.cnt["t"])
            b.emit("s", lambda E, po=po, ft=ft: E.activation(
                hT[:, ft * TB:(ft + 1) * TB], po[:, :], AF.Gelu,
                bias=b1c[:, ft:ft + 1]), dep=2)
            hpend.append((ft, b.cnt["s"]))
            rel[("ph", ft % 2)] = ("s", b.cnt["s"])
            while len(hpend) > 2:
                f2, scnt = hpend.pop(0)
                pbx = pb_[f2 % 2]
                if ("pbx", f2 % 2) in rel:
                    e, c = rel[("pbx", f2 % 2)]
                    b.wait("t", e, c)
                b.emit("t", lambda E, pbx=pbx, pr=pr, ex=f2 // 4, ob=b.cb["ones_b"]: E.matmul(
                    pbx[:, :], ob[0:1, 0:128], grow[pr][0:1, ex * TB:(ex + 1) * TB],
                    start=True, stop=True))
                b.wait("v", "t", b.cnt["t"])
                b.wait("v", "s", scnt)
                b.emit("v", lambda E, f2=f2, pbx=pbx: E.tensor_tensor(
                    hT[:, f2 * TB:(f2 + 1) * TB], hT[:, f2 * TB:(f2 + 1) * TB],
                    pbx[:, :], ALU.mult), dep=2)
                rel[("pbx", f2 % 2)] = ("v", b.cnt["v"])
                rel[("hT",)] = ("v", b.cnt["v"])
        for f2, scnt in hpend:
            pbx = pb_[f2 % 2]
            if ("pbx", f2 % 2) in rel:
                e, c = rel[("pbx", f2 % 2)]
                b.wait("t", e, c)
            b.emit("t", lambda E, pbx=pbx, pr=pr, ex=f2 // 4, ob=b.cb["ones_b"]: E.matmul(
                pbx[:, :], ob[0:1, 0:128], grow[pr][0:1, ex * TB:(ex + 1) * TB],
                start=True, stop=True))
            b.wait("v", "t", b.cnt["t"])
            b.wait("v", "s", scnt)
            b.emit("v", lambda E, f2=f2, pbx=pbx: E.tensor_tensor(
                hT[:, f2 * TB:(f2 + 1) * TB], hT[:, f2 * TB:(f2 + 1) * TB],
                pbx[:, :], ALU.mult), dep=2)
            rel[("pbx", f2 % 2)] = ("v", b.cnt["v"])
            rel[("hT",)] = ("v", b.cnt["v"])
        for fs in range(4):
            po = ph[fs % 2]
            e, c = rel[("ph", fs % 2)]
            b.wait("t", e, c)
            for a in range(8):
                b.emit("t", lambda E, po=po, xb=xb, a=a, fs=fs: E.matmul(
                    po[:, :], ws1[:, a * 512 + fs * 128: a * 512 + fs * 128 + 128],
                    xb[:, a * TB:(a + 1) * TB], start=(a == 0), stop=(a == 7)))
            b.wait("s", "t", b.cnt["t"])
            b.emit("s", lambda E, po=po, fs=fs: E.activation(
                hsT[:, fs * TB:(fs + 1) * TB], po[:, :], AF.Gelu,
                bias=bs1c[:, fs:fs + 1]), dep=2)
            rel[("ph", fs % 2)] = ("s", b.cnt["s"])
            rel[("hsT",)] = ("s", b.cnt["s"])
        if tb + 1 < NB:
            emit_gates(tb + 1)
        e, c = rel[("hT",)]
        b.wait("t", e, c)
        e, c = rel[("hsT",)]
        b.wait("t", e, c)
        for tl in range(ntl):
            tt = tb * ntl + tl
            srr = sr[tt % 2]
            sdd = sd[tt % 2]
            if ("rs",) in rel:
                e, c = rel[("rs",)]
                b.wait("y", e, c)
            b.dma("y", srr, lambda E, d=res, s=res_src, tt=tt: E.dma_start(
                d[:, :], s.ap()[tt * 128:(tt + 1) * 128, :]))
            for fh in range(2):
                po = pout[fh]
                if ("pout", fh) in rel:
                    e, c = rel[("pout", fh)]
                    b.wait("t", e, c)
                for hc in range(32):
                    b.emit("t", lambda E, po=po, hc=hc, tl=tl, fh=fh: E.matmul(
                        po[:, :], hT[:, hc * TB + tl * 128: hc * TB + tl * 128 + 128],
                        w2[:, hc * 1024 + fh * 512: hc * 1024 + fh * 512 + 512],
                        start=(hc == 0), stop=False))
                for sc_ in range(4):
                    b.emit("t", lambda E, po=po, sc_=sc_, tl=tl, fh=fh: E.matmul(
                        po[:, :], hsT[:, sc_ * TB + tl * 128: sc_ * TB + tl * 128 + 128],
                        ws2[:, sc_ * 1024 + fh * 512: sc_ * 1024 + fh * 512 + 512],
                        start=False, stop=False))
                b.emit("t", lambda E, po=po, pr=pr, tl=tl, fh=fh: E.matmul(
                    po[:, :], gtb[pr][0:8, tl * 128:(tl + 1) * 128],
                    b2b[0:8, fh * 512:(fh + 1) * 512], start=False, stop=False))
                b.emit("t", lambda E, po=po, fh=fh, ob=b.cb["ones_b"]: E.matmul(
                    po[:, :], ob[0:1, 0:128], bs2r[0:1, fh * 512:(fh + 1) * 512],
                    start=False, stop=True))
                b.wait("v", "t", b.cnt["t"])
                b.wsem("v", srr)
                b.wsem("v", sdd)
                b.emit("v", lambda E, po=po, fh=fh: E.tensor_tensor(
                    outb[:, fh * 512:(fh + 1) * 512], res[:, fh * 512:(fh + 1) * 512],
                    po[:, :], ALU.add))
                rel[("pout", fh)] = ("v", b.cnt["v"])
            rel[("rs",)] = ("v", b.cnt["v"])
            b.wait("y", "v", b.cnt["v"])
            b.dma("y", sdd, lambda E, d=dst, tt=tt: E.dma_start(
                d.ap()[tt * 128:(tt + 1) * 128, :], outb[:, :]))
            b.wsem("v", sdd)
        rel[("x", pr)] = ("t", b.cnt["t"])
    b.barrier()
    b.sfree([sw] + sgr + sx + sr + sd)
    st.close()



def emit_s1(b, io, scr, src, dst):
    st = ExitStack()
    xT = b.sb("xT1", [128, 8 * T], BF16, st)
    emit_ln_t(b, src, xT)
    emit_proj_T(b, xT, io["L0_attn_a_wqkv"], 0, C, scr["qT"], act=AF.Relu)
    emit_proj_T(b, xT, io["L0_attn_a_wqkv"], C, C, scr["kT"], act=AF.Relu)
    emit_proj_N(b, xT, io["L0_attn_a_wqkv"], C, C, scr["kn"], act=AF.Relu)
    emit_proj_N(b, xT, io["L0_attn_a_wqkv"], 2 * C, C, scr["vn"], act=AF.Copy)
    st.close()
    st2 = ExitStack()
    masks = b.sb("m_le", [128, 4 * 512], BF16, st2)
    ss = b.ssem()
    b.dma("y", ss, lambda E, d=masks, s=io["masks_le"]: E.dma_start(d[:, :], s.ap()))
    b.barrier()
    b.sfree([ss])
    emit_lin_attn_causal(b, scr["qT"], scr["kT"], scr["kn"], scr["vn"], masks, scr["aT"])
    st2.close()
    emit_out_proj(b, scr["aT"], io["L0_attn_a_wo"], C, src, dst)


S2_WEFF_MARK = {}


def emit_s2(b, io, scr, src, dst, first):
    if first:
        S2_WEFF_MARK["fc"] = emit_weff(b, io["L0_ffn_w_fc"], 4096, 1024,
                                       scr["weff_fc"], barrier=False)
        S2_WEFF_MARK["pj"] = emit_weff(b, io["L0_ffn_w_proj"], 1024, 4096,
                                       scr["weff_proj"], barrier=False)
    gate_col = emit_bcast_scalar(b, io["L0_ffn_gate"])
    st = ExitStack()
    xT = b.sb("xT2", [128, 8 * T], BF16, st)
    emit_ln_t(b, src, xT, mirror_scr=scr["xTs"])
    st.close()
    emit_ffn(b, scr["xTs"], _ld_transposed(scr["weff_fc"], 4096, 8),
             _ld_transposed(scr["weff_proj"], 1024, 32), dst, src,
             "relu2", hwidth=4096,
             lowrank=dict(vT=io["L0_ffn_vT"], fbT=io["L0_ffn_fbT"],
                          uT=io["L0_ffn_uT"], gate_col=gate_col))


def emit_s3(b, io, scr, src, dst, first):
    if first:
        st = ExitStack()
        encT = b.sb("encT", [128, 8 * 2048], BF16, st)
        ss = b.ssem()
        b.dma("g", ss, _w_cast_load(encT, io["L0_attn_b_encT"], 2048))
        kcols = b.sb("xkvbk", [128, 8], F32, st)
        b.dma("y", ss, lambda E, d=kcols, s=io["L0_attn_b_x_kv_b_kcols"]: E.dma_start(d[:, :], s.ap()))
        b.barrier()
        b.sfree([ss])
        emit_proj_T(b, encT, io["L0_attn_b_x_kv_w"], 0, C, scr["ekT"],
                    bias_col=kcols, ntok=2048)
        emit_proj_N(b, encT, io["L0_attn_b_x_kv_w"], C, C, scr["ev"],
                    bias_row=io["L0_attn_b_x_kv_b"], bias_off=1024, ntok=2048)
        st.close()
    st = ExitStack()
    xT = b.sb("xT3", [128, 8 * T], BF16, st)
    emit_ln_t(b, src, xT, norm_f32_dst=scr["xh1"])
    stg = ExitStack()
    g1 = emit_bcast_row(b, io["L0_attn_b_ln1_g"], C, dst=b.sb("g1", [128, C], F32, stg))
    b1 = emit_bcast_row(b, io["L0_attn_b_ln1_b"], C, dst=b.sb("b1", [128, C], F32, stg))
    emit_ln_t(b, scr["xh1"], xT, g_bc=g1, b_bc=b1)
    stg.close()
    stg = ExitStack()
    cabq = b.sb("cabq", [128, 8], F32, stg)
    cabk = b.sb("cabk", [128, 8], F32, stg)
    ss = b.ssem()
    b.dma("y", ss, lambda E, d=cabq, s=io["L0_attn_b_c_attn_b_qcols"]: E.dma_start(d[:, :], s.ap()))
    b.dma("y", ss, lambda E, d=cabk, s=io["L0_attn_b_c_attn_b_kcols"]: E.dma_start(d[:, :], s.ap()))
    b.barrier()
    b.sfree([ss])
    emit_proj_T(b, xT, io["L0_attn_b_c_attn_w"], 0, C, scr["qT"], bias_col=cabq)
    emit_proj_T(b, xT, io["L0_attn_b_c_attn_w"], C, C, scr["kT"], bias_col=cabk)
    emit_proj_N(b, xT, io["L0_attn_b_c_attn_w"], 2 * C, C, scr["vn"],
                bias_row=io["L0_attn_b_c_attn_b"], bias_off=2048)
    stg.close()
    stg = ExitStack()
    masks = b.sb("m_le3", [128, 4 * 512], BF16, stg)
    ss = b.ssem()
    b.dma("y", ss, lambda E, d=masks, s=io["masks_le"]: E.dma_start(d[:, :], s.ap()))
    b.barrier()
    b.sfree([ss])
    emit_attn_softmax(b, scr["qT"], scr["kT"], scr["vn"], scr["aT"], T, "self", masks)
    stg.close()
    emit_out_proj(b, scr["aT"], io["L0_attn_b_attn_proj_w"], C, scr["xh1"],
                  scr["xh2"], bias_row=io["L0_attn_b_attn_proj_b"])
    stg = ExitStack()
    g2 = emit_bcast_row(b, io["L0_attn_b_lnx_g"], C, dst=b.sb("g2", [128, C], F32, stg))
    b2 = emit_bcast_row(b, io["L0_attn_b_lnx_b"], C, dst=b.sb("b2", [128, C], F32, stg))
    emit_ln_t(b, scr["xh2"], xT, g_bc=g2, b_bc=b2)
    stg.close()
    stg = ExitStack()
    qab = b.sb("qab", [128, 8], F32, stg)
    ss = b.ssem()
    b.dma("y", ss, lambda E, d=qab, s=io["L0_attn_b_q_attn_b_cols"]: E.dma_start(d[:, :], s.ap()))
    b.barrier()
    b.sfree([ss])
    emit_proj_T(b, xT, io["L0_attn_b_q_attn_w"], 0, C, scr["qT"], bias_col=qab)
    stg.close()
    stg = ExitStack()
    masks2 = b.sb("m_le2", [128, 5 * 512], BF16, stg)
    scor = b.sb("scor", [128, 16], F32, stg)
    ss = b.ssem()
    b.dma("y", ss, lambda E, d=masks2, s=io["masks_le2"]: E.dma_start(d[:, :], s.ap()))
    b.dma("y", ss, lambda E, d=scor, s=io["L0_attn_b_scorer_cols"]: E.dma_start(d[:, :], s.ap()))
    b.barrier()
    b.wsem("v", ss)
    b.emit("v", lambda E, d=scor: E.tensor_scalar(d[:, :], d[:, :], -16.0, None, ALU.add))
    b.barrier()
    b.sfree([ss])
    emit_attn_softmax(b, scr["qT"], scr["ekT"], scr["ev"], scr["aT"], 2048,
                      "cross", masks2, scorer_col=scor)
    stg.close()
    emit_out_proj(b, scr["aT"], io["L0_attn_b_x_proj_w"], C, scr["xh2"],
                  scr["xh1"], bias_row=io["L0_attn_b_x_proj_b"])
    stg = ExitStack()
    g3 = emit_bcast_row(b, io["L0_attn_b_ln2_g"], C, dst=b.sb("g3", [128, C], F32, stg))
    b3 = emit_bcast_row(b, io["L0_attn_b_ln2_b"], C, dst=b.sb("b3", [128, C], F32, stg))
    emit_ln_t(b, scr["xh1"], xT, g_bc=g3, b_bc=b3, mirror_scr=scr["xTs"])
    stg.close()
    st.close()
    stg = ExitStack()
    fcb = b.sb("fcb", [128, 32], F32, stg)
    ss = b.ssem()
    b.dma("y", ss, lambda E, d=fcb, s=io["L0_attn_b_fc_b_cols"]: E.dma_start(d[:, :], s.ap()))
    b.barrier()
    b.sfree([ss])
    emit_ffn(b, scr["xTs"], _ld_natural(io["L0_attn_b_fc_w"], 4096),
             _ld_natural(io["L0_attn_b_mlp_proj_w"], 1024), dst, scr["xh1"],
             "gelu_tanh", hwidth=4096, h_bias_col=fcb,
             out_bias_row=io["L0_attn_b_mlp_proj_b"], extra_res=src)
    stg.close()


def emit_s5(b, io, scr, src, dst):
    st = ExitStack()
    xT = b.sb("xT5", [128, 8 * T], BF16, st)
    emit_ln_t(b, src, xT)
    emit_proj_T(b, xT, io["L1_attn_a_Wqkv"], 0, C, scr["qT"])
    emit_proj_T(b, xT, io["L1_attn_a_Wqkv"], C, C, scr["kT"])
    emit_proj_N(b, xT, io["L1_attn_a_Wqkv"], 2 * C, C, scr["vn"])
    st.close()
    st = ExitStack()
    cosr = b.sb("cosr", [128, T], F32, st)
    sinr = b.sb("sinr", [128, T], F32, st)
    swpT = b.sb("swpT", [128, 128], BF16, st)
    ss = b.ssem()
    b.dma("y", ss, lambda E, d=cosr, s=io["rope_cos"]: E.dma_start(d[:, :], s.ap()))
    b.dma("y", ss, lambda E, d=sinr, s=io["rope_sin"]: E.dma_start(d[:, :], s.ap()))
    b.dma("y", ss, lambda E, d=swpT, s=io["rope_swpT"]: E.dma_start(d[:, :], s.ap()))
    b.barrier()
    b.sfree([ss])
    emit_rope(b, scr["qT"], scr["qrT"], cosr, sinr, swpT)
    emit_rope(b, scr["kT"], scr["krT"], cosr, sinr, swpT)
    st.close()
    st = ExitStack()
    masks = b.sb("m_le5", [128, 4 * 512], BF16, st)
    identf = b.sb("idf5", [128, 128], F32, st)
    ss = b.ssem()
    b.dma("y", ss, lambda E, d=masks, s=io["masks_le"]: E.dma_start(d[:, :], s.ap()))
    b.dma("y", ss, lambda E, d=identf, s=io["identf"]: E.dma_start(d[:, :], s.ap()))
    b.barrier()
    b.sfree([ss])
    emit_attn_s5(b, scr["qT"], scr["kT"], scr["qrT"], scr["krT"], scr["vn"],
                 scr["aT"], masks, identf)
    st.close()
    emit_out_proj(b, scr["aT"], io["L1_attn_a_Wout"], C, src, dst,
                  bias_row=io["L1_attn_a_bout"])


def emit_s6(b, io, scr, src, dst):
    rs_col = emit_bcast_scalar(b, io["L1_ffn_route_scale"])
    st = ExitStack()
    xT = b.sb("xT6", [128, 8 * T], BF16, st)
    emit_ln_t(b, src, xT, mirror_scr=scr["xTs"])
    st.close()
    st = ExitStack()
    identf = b.sb("idf6", [128, 128], F32, st)
    ss = b.ssem()
    b.dma("y", ss, lambda E, d=identf, s=io["identf"]: E.dma_start(d[:, :], s.ap()))
    b.barrier()
    b.sfree([ss])
    emit_moe(b, scr["xTs"], io, dst, src, identf, rs_col)
    st.close()


def emit_s7(b, io, scr, src, dst):
    st = ExitStack()
    gam = emit_bcast_row(b, io["L1_attn_b_gamma"], C, dst=b.sb("gam", [128, C], F32, st))
    sc1 = emit_bcast_row(b, io["L1_attn_b_scale1"], C, dst=b.sb("sc1", [128, C], F32, st))
    sc2 = emit_bcast_row(b, io["L1_attn_b_scale2"], C, dst=b.sb("sc2", [128, C], F32, st))
    stx = ExitStack()
    xT = b.sb("xT7", [128, 8 * T], BF16, stx)
    emit_ln_t(b, src, xT, norm_f32_dst=scr["xh1"])
    emit_ln_t(b, scr["xh1"], xT, rms_scale_bc=sc1)
    emit_proj_T(b, xT, io["L1_attn_b_qkv_wT"], 0, C, scr["qT"],
                act=AF.Square, act2=(AF.Exp, -0.5))
    emit_proj_N(b, xT, io["L1_attn_b_qkv_wT"], C, C, scr["kn"],
                act=AF.Square, act2=(AF.Exp, -0.5))
    emit_proj_N(b, xT, io["L1_attn_b_qkv_wT"], 2 * C, C, scr["vn"])
    emit_attn_global(b, scr["qT"], scr["kn"], scr["vn"], scr["aT"])
    emit_out_proj(b, scr["aT"], io["L1_attn_b_proj_wT"], C, scr["xh1"],
                  scr["xh2"], bias_row=io["L1_attn_b_proj_b"], gamma_bc=gam)
    emit_ln_t(b, scr["xh2"], xT, rms_scale_bc=sc2, mirror_scr=scr["xTs"])
    stx.close()
    stg = ExitStack()
    c1b = b.sb("c1b", [128, 16], F32, stg)
    ss = b.ssem()
    b.dma("y", ss, lambda E, d=c1b, s=io["L1_attn_b_conv1_b_cols"]: E.dma_start(d[:, :], s.ap()))
    b.barrier()
    b.sfree([ss])
    emit_ffn(b, scr["xTs"], _ld_natural(io["L1_attn_b_conv1_wT"], 2048),
             _ld_natural(io["L1_attn_b_conv2_wT"], 1024), dst, scr["xh2"],
             "gelu", hwidth=2048, h_bias_col=c1b,
             out_bias_row=io["L1_attn_b_conv2_b"], extra_res=src, gamma_bc=gam)
    stg.close()
    st.close()


SUB_INPUTS = {
    "s1": ["L0_attn_a_wqkv", "L0_attn_a_wo"],
    "s2": ["L0_ffn_w_fc", "L0_ffn_w_proj", "L0_ffn_vT", "L0_ffn_fbT",
           "L0_ffn_uT", "L0_ffn_gate"],
    "s3": ["L0_attn_b_encT", "L0_attn_b_scorer_cols", "L0_attn_b_ln1_g",
           "L0_attn_b_ln1_b", "L0_attn_b_c_attn_w", "L0_attn_b_c_attn_b",
           "L0_attn_b_c_attn_b_qcols", "L0_attn_b_c_attn_b_kcols",
           "L0_attn_b_attn_proj_w", "L0_attn_b_attn_proj_b",
           "L0_attn_b_lnx_g", "L0_attn_b_lnx_b", "L0_attn_b_q_attn_w",
           "L0_attn_b_q_attn_b_cols", "L0_attn_b_x_kv_w", "L0_attn_b_x_kv_b",
           "L0_attn_b_x_kv_b_kcols", "L0_attn_b_x_proj_w", "L0_attn_b_x_proj_b",
           "L0_attn_b_ln2_g", "L0_attn_b_ln2_b", "L0_attn_b_fc_w",
           "L0_attn_b_fc_b_cols", "L0_attn_b_mlp_proj_w", "L0_attn_b_mlp_proj_b"],
    "s5": ["L1_attn_a_Wqkv", "L1_attn_a_Wout", "L1_attn_a_bout",
           "rope_cos", "rope_sin", "rope_swpT", "identf"],
    "s6": ["L1_ffn_gate_W", "L1_ffn_W1cat", "L1_ffn_b1cols", "L1_ffn_W2cat",
           "L1_ffn_b2", "L1_ffn_Ws1", "L1_ffn_bs1cols", "L1_ffn_Ws2",
           "L1_ffn_bs2", "L1_ffn_route_scale", "identf"],
    "s7": ["L1_attn_b_scale1", "L1_attn_b_scale2", "L1_attn_b_gamma",
           "L1_attn_b_qkv_wT", "L1_attn_b_proj_wT", "L1_attn_b_proj_b",
           "L1_attn_b_conv1_wT", "L1_attn_b_conv1_b_cols", "L1_attn_b_conv2_wT",
           "L1_attn_b_conv2_b"],
}
SUB_INPUTS["s4"] = SUB_INPUTS["s2"]
SUB_INPUTS["s8"] = SUB_INPUTS["s6"]

INPUT_SHAPES = {
    "L0_attn_a_wqkv": [C, 3 * C], "L0_attn_a_wo": [C, C],
    "L0_ffn_w_fc": [4096, C], "L0_ffn_w_proj": [C, 4096],
    "L0_ffn_vT": [C, 16], "L0_ffn_fbT": [16, 16], "L0_ffn_uT": [16, 4096],
    "L0_ffn_gate": [1, 1],
    "L0_attn_b_encT": [C, 2048], "L0_attn_b_scorer_cols": [128, 16],
    "L0_attn_b_ln1_g": [C], "L0_attn_b_ln1_b": [C],
    "L0_attn_b_c_attn_w": [C, 3 * C], "L0_attn_b_c_attn_b": [3 * C],
    "L0_attn_b_c_attn_b_qcols": [128, 8], "L0_attn_b_c_attn_b_kcols": [128, 8],
    "L0_attn_b_attn_proj_w": [C, C], "L0_attn_b_attn_proj_b": [C],
    "L0_attn_b_lnx_g": [C], "L0_attn_b_lnx_b": [C],
    "L0_attn_b_q_attn_w": [C, C], "L0_attn_b_q_attn_b_cols": [128, 8],
    "L0_attn_b_x_kv_w": [C, 2 * C], "L0_attn_b_x_kv_b": [2 * C],
    "L0_attn_b_x_kv_b_kcols": [128, 8],
    "L0_attn_b_x_proj_w": [C, C], "L0_attn_b_x_proj_b": [C],
    "L0_attn_b_ln2_g": [C], "L0_attn_b_ln2_b": [C],
    "L0_attn_b_fc_w": [C, 4096], "L0_attn_b_fc_b_cols": [128, 32],
    "L0_attn_b_mlp_proj_w": [4096, C], "L0_attn_b_mlp_proj_b": [C],
    "L1_attn_a_Wqkv": [C, 3 * C], "L1_attn_a_Wout": [C, C], "L1_attn_a_bout": [C],
    "rope_cos": [128, T], "rope_sin": [128, T],
    "L1_ffn_gate_W": [C, 8], "L1_ffn_W1cat": [C, 4096], "L1_ffn_b1cols": [128, 32],
    "L1_ffn_W2cat": [4096, C], "L1_ffn_b2": [8, C], "L1_ffn_Ws1": [C, 512],
    "L1_ffn_bs1cols": [128, 4], "L1_ffn_Ws2": [512, C], "L1_ffn_bs2": [C],
    "L1_ffn_route_scale": [1, 1],
    "L1_attn_b_scale1": [C], "L1_attn_b_scale2": [C], "L1_attn_b_gamma": [C],
    "L1_attn_b_qkv_wT": [C, 3 * C], "L1_attn_b_proj_wT": [C, C],
    "L1_attn_b_proj_b": [C], "L1_attn_b_conv1_wT": [C, 2048],
    "L1_attn_b_conv1_b_cols": [128, 16], "L1_attn_b_conv2_wT": [2048, C],
    "L1_attn_b_conv2_b": [C],
    "identf": [128, 128],
}
BF16_INPUTS = {"identb", "masks_le", "masks_le2", "rope_swpT"}

EMITTERS = {
    "s1": lambda b, io, scr, src, dst, first: emit_s1(b, io, scr, src, dst),
    "s2": emit_s2,
    "s3": lambda b, io, scr, src, dst, first: emit_s3(b, io, scr, src, dst, first),
    "s4": emit_s2,
    "s5": lambda b, io, scr, src, dst, first: emit_s5(b, io, scr, src, dst),
    "s6": lambda b, io, scr, src, dst, first: emit_s6(b, io, scr, src, dst),
    "s7": lambda b, io, scr, src, dst, first: emit_s7(b, io, scr, src, dst),
    "s8": lambda b, io, scr, src, dst, first: emit_s6(b, io, scr, src, dst),
}


def build_program(parts, debug_scr=False):
    nc = bass.Bass("TRN2", target_bir_lowering=False)
    io = {}
    names = {"identb", "masks_le", "masks_le2", "rope_swpT"}
    for p in parts:
        names |= set(SUB_INPUTS[p])
    io["identb"] = nc.dram_tensor("identb", [128, 128], BF16, kind="ExternalInput")
    io["masks_le"] = nc.dram_tensor("masks_le", [128, 4 * 512], BF16, kind="ExternalInput")
    if "s3" in parts:
        io["masks_le2"] = nc.dram_tensor("masks_le2", [128, 5 * 512], BF16, kind="ExternalInput")
    if "s5" in parts:
        io["rope_swpT"] = nc.dram_tensor("rope_swpT", [128, 128], BF16, kind="ExternalInput")
    for n in sorted(names):
        if n in io or n in ("identb", "masks_le", "masks_le2", "rope_swpT"):
            continue
        io[n] = nc.dram_tensor(n, INPUT_SHAPES[n], F32, kind="ExternalInput")
    io["x"] = nc.dram_tensor("x", [T, C], F32, kind="ExternalInput")
    io["y"] = nc.dram_tensor("y", [T, C], F32, kind="ExternalOutput")

    scr = {}
    kind = "ExternalOutput" if debug_scr else "Internal"

    def mkscr(name, shape, dt=BF16):
        scr[name] = nc.dram_tensor("scr_" + name, shape, dt, kind=kind)

    mkscr("qT", [C, T]); mkscr("kT", [C, T]); mkscr("kn", [T, C])
    mkscr("vn", [T, C]); mkscr("aT", [C, T]); mkscr("xTs", [C, T])
    if "s5" in parts:
        mkscr("qrT", [C, T]); mkscr("krT", [C, T])
    if "s2" in parts or "s4" in parts:
        mkscr("weff_fc", [4096, C]); mkscr("weff_proj", [C, 4096])
    if "s3" in parts:
        mkscr("ekT", [C, 2048]); mkscr("ev", [2048, C])
    if "s3" in parts or "s7" in parts:
        mkscr("xh1", [T, C], F32); mkscr("xh2", [T, C], F32)
    nx = sum(1 for p in parts) - 1
    xbuf = []
    for i in range(max(0, len(parts) - 1)):
        xbuf.append(nc.dram_tensor(f"xcur{i}", [T, C], F32))

    S2_WEFF_MARK.clear()
    b = B(nc)
    b.cb = emit_consts(b, io)
    seen = set()
    cur = io["x"]
    for idx, p in enumerate(parts):
        dst = io["y"] if idx == len(parts) - 1 else xbuf[idx]
        first = p[:2] not in seen
        base = {"s4": "s2", "s8": "s6"}.get(p, p)
        first = base not in seen
        seen.add(base)
        EMITTERS[p](b, io, scr, cur, dst, first)
        cur = dst
    b.barrier()
    b.finalize()
    return nc



def _cols(v, n):
    return np.ascontiguousarray(v.reshape(n, 128).T.astype(np.float32))


def host_consts():
    identb = np.eye(128).astype(BF)
    identf = np.eye(128).astype(np.float32)
    j = np.arange(4)[:, None, None]
    kk_ = np.arange(128)[None, :, None]
    qq = np.arange(512)[None, None, :]
    m = (128 * j + kk_ <= qq).astype(BF)
    masks_le = np.ascontiguousarray(m.transpose(1, 0, 2).reshape(128, 4 * 512))
    j5 = np.arange(5)[:, None, None]
    m2 = (128 * j5 + kk_ <= qq + 2).astype(BF)
    masks_le2 = np.ascontiguousarray(m2.transpose(1, 0, 2).reshape(128, 5 * 512))
    i_ = np.arange(32, dtype=np.float64)
    theta = np.power(10000.0, -2.0 * (i_ - 1.0) / 64.0)
    ang = np.arange(T, dtype=np.float64)[:, None] * theta[None, :]
    cosE = np.cos(ang); sinE = np.sin(ang)
    cosT = np.repeat(cosE.T, 2, axis=0).astype(np.float32)
    sinT = np.repeat(sinE.T, 2, axis=0).astype(np.float32)
    rope_cos = np.ascontiguousarray(np.tile(cosT, (2, 1)))
    rope_sin = np.ascontiguousarray(np.tile(sinT, (2, 1)))
    S = np.zeros((64, 64), np.float32)
    for ii in range(32):
        S[2 * ii, 2 * ii + 1] = -1.0
        S[2 * ii + 1, 2 * ii] = 1.0
    S128 = np.zeros((128, 128), np.float32)
    S128[0:64, 0:64] = S
    S128[64:128, 64:128] = S
    swpT = np.ascontiguousarray(S128.T).astype(BF)
    return dict(identb=identb, identf=identf, masks_le=masks_le,
                masks_le2=masks_le2, rope_cos=rope_cos, rope_sin=rope_sin,
                rope_swpT=swpT)


def host_inputs(inputs):
    f = np.ascontiguousarray
    d = dict(host_consts())
    d["L0_attn_a_wqkv"] = inputs["L0_attn_a_wqkv"]
    d["L0_attn_a_wo"] = inputs["L0_attn_a_wo"]
    d["L0_ffn_w_fc"] = inputs["L0_ffn_w_fc"]
    d["L0_ffn_w_proj"] = inputs["L0_ffn_w_proj"]
    d["L0_ffn_vT"] = f(inputs["L0_ffn_v"].T)
    d["L0_ffn_fbT"] = f(inputs["L0_ffn_fast_b"].T)
    d["L0_ffn_uT"] = f(inputs["L0_ffn_u"].T)
    d["L0_ffn_gate"] = f(np.asarray(inputs["L0_ffn_gate"]).reshape(1, 1))
    d["L0_attn_b_encT"] = f(np.asarray(inputs["L0_attn_b_encoder_hidden_states"])[0].T)
    d["L0_attn_b_scorer_cols"] = _cols(np.asarray(inputs["L0_attn_b_scorer_logits"])[0], 16)
    for nm in ("ln1_g", "ln1_b", "c_attn_w", "c_attn_b", "attn_proj_w",
               "attn_proj_b", "lnx_g", "lnx_b", "q_attn_w", "x_kv_w", "x_kv_b",
               "x_proj_w", "x_proj_b", "ln2_g", "ln2_b", "fc_w",
               "mlp_proj_w", "mlp_proj_b"):
        d["L0_attn_b_" + nm] = inputs["L0_attn_b_" + nm]
    cab = np.asarray(inputs["L0_attn_b_c_attn_b"])
    d["L0_attn_b_c_attn_b_qcols"] = _cols(cab[0:1024], 8)
    d["L0_attn_b_c_attn_b_kcols"] = _cols(cab[1024:2048], 8)
    d["L0_attn_b_q_attn_b_cols"] = _cols(np.asarray(inputs["L0_attn_b_q_attn_b"]), 8)
    d["L0_attn_b_x_kv_b_kcols"] = _cols(np.asarray(inputs["L0_attn_b_x_kv_b"])[0:1024], 8)
    d["L0_attn_b_fc_b_cols"] = _cols(np.asarray(inputs["L0_attn_b_fc_b"]), 32)
    d["L1_attn_a_Wqkv"] = inputs["L1_attn_a_Wqkv"]
    d["L1_attn_a_Wout"] = inputs["L1_attn_a_Wout"]
    d["L1_attn_a_bout"] = inputs["L1_attn_a_bout"]
    d["L1_ffn_gate_W"] = inputs["L1_ffn_gate_W"]
    d["L1_ffn_W1cat"] = f(np.asarray(inputs["L1_ffn_W1"]).transpose(1, 0, 2).reshape(C, 8 * 512))
    d["L1_ffn_b1cols"] = _cols(np.asarray(inputs["L1_ffn_b1"]).reshape(-1), 32)
    d["L1_ffn_W2cat"] = f(np.asarray(inputs["L1_ffn_W2"]).reshape(8 * 512, C))
    d["L1_ffn_b2"] = inputs["L1_ffn_b2"]
    d["L1_ffn_Ws1"] = inputs["L1_ffn_Ws1"]
    d["L1_ffn_bs1cols"] = _cols(np.asarray(inputs["L1_ffn_bs1"]), 4)
    d["L1_ffn_Ws2"] = inputs["L1_ffn_Ws2"]
    d["L1_ffn_bs2"] = inputs["L1_ffn_bs2"]
    d["L1_ffn_route_scale"] = f(np.asarray(inputs["L1_ffn_route_scale"]).reshape(1, 1))
    d["L1_attn_b_scale1"] = inputs["L1_attn_b_scale1"]
    d["L1_attn_b_scale2"] = inputs["L1_attn_b_scale2"]
    d["L1_attn_b_gamma"] = inputs["L1_attn_b_gamma"]
    d["L1_attn_b_qkv_wT"] = f(np.asarray(inputs["L1_attn_b_qkv_w"]).T)
    d["L1_attn_b_proj_wT"] = f(np.asarray(inputs["L1_attn_b_proj_w"]).T)
    d["L1_attn_b_proj_b"] = inputs["L1_attn_b_proj_b"]
    d["L1_attn_b_conv1_wT"] = f(np.asarray(inputs["L1_attn_b_conv1_w"]).T)
    d["L1_attn_b_conv1_b_cols"] = _cols(np.asarray(inputs["L1_attn_b_conv1_b"]), 16)
    d["L1_attn_b_conv2_wT"] = f(np.asarray(inputs["L1_attn_b_conv2_w"]).T)
    d["L1_attn_b_conv2_b"] = inputs["L1_attn_b_conv2_b"]
    return {k: (np.ascontiguousarray(v) if isinstance(v, np.ndarray) else
                np.ascontiguousarray(np.asarray(v))) for k, v in d.items()}


ALL_PARTS = ["s1", "s2", "s3", "s4", "s5", "s6", "s7", "s8"]
GROUPS = [["s1", "s2", "s3", "s4", "s5", "s6", "s7", "s8"]]


def _in_map_for(nc, host, x):
    want = set()
    for a in nc.m.functions[0].allocations:
        try:
            if a.kind == "ExternalInput":
                want.add(a.memorylocations[0].name)
        except AttributeError:
            pass
    m = {}
    for n in want:
        if n == "x":
            m[n] = x
        elif n in host:
            m[n] = host[n]
    return m


def kernel(**inputs):
    host = host_inputs(inputs)
    x = np.ascontiguousarray(np.asarray(inputs["L0_attn_a_x"], np.float32))
    cur = [np.ascontiguousarray(x[c]) for c in range(N_CORE)]
    for grp in GROUPS:
        nc = build_program(grp)
        in_maps = [_in_map_for(nc, host, cur[c]) for c in range(N_CORE)]
        res = run_bass_kernel_spmd(nc, in_maps, core_ids=list(range(N_CORE)))
        cur = [np.asarray(res.results[c]["y"], np.float32) for c in range(N_CORE)]
    return np.stack(cur)


# version 11
# speedup vs baseline: 1.0181x; 1.0181x over previous
import math
import numpy as np
import ml_dtypes
from contextlib import ExitStack

import concourse.bass as bass
import concourse.mybir as mybir
from concourse.bass_utils import run_bass_kernel_spmd

F32 = mybir.dt.float32
BF16 = mybir.dt.bfloat16
AF = mybir.ActivationFunctionType
ALU = mybir.AluOpType
AX = mybir.AxisListType
BF = ml_dtypes.bfloat16

N_CORE = 8
T = 4096
C = 1024
NT = T // 128
NTC = T // 512
NH = 16
HD = 64
RNE = 12582912.0

ENGINES = ("t", "v", "s", "g", "y")
DEBUG_LN_BARRIER = False


class Slot:
    __slots__ = ("sem", "cnt", "name")

    def __init__(self, sem, name):
        self.sem = sem
        self.cnt = 0
        self.name = name


class B:
    def __init__(self, nc):
        self.nc = nc
        self.st = ExitStack()
        self.sems = {}
        self.cnt = {e: 0 for e in ENGINES}
        self.hw = {e: {} for e in ENGINES}
        self.steps = {e: [] for e in ENGINES}
        self.nalloc = 0
        self.spool = []
        self.sall = []
        for e in ENGINES:
            self.sems[e] = self.st.enter_context(nc.semaphore("sem_" + e))

    def sb(self, name, shape, dt, stack=None):
        self.nalloc += 1
        return (stack or self.st).enter_context(
            self.nc.sbuf_tensor(f"{name}_{self.nalloc}", shape, dt))

    def ps(self, name, shape, dt=F32, stack=None):
        self.nalloc += 1
        return (stack or self.st).enter_context(
            self.nc.psum_tensor(f"{name}_{self.nalloc}", shape, dt))

    def ssem(self):
        if self.spool:
            return self.spool.pop()
        self.nalloc += 1
        s = Slot(self.st.enter_context(self.nc.semaphore(f"dsem_{self.nalloc}")),
                 f"dsem_{self.nalloc}")
        self.sall.append(s)
        return s

    def ssems(self, n):
        return [self.ssem() for _ in range(n)]

    def sfree(self, slots):
        for s in slots:
            self.spool.append(s)

    def wait(self, eng, on, val=None):
        if val is None:
            val = self.cnt[on]
        if val <= 0 or self.hw[eng].get(on, 0) >= val:
            return
        self.hw[eng][on] = val
        sem = self.sems[on]
        step = lambda E, sem=sem, val=val: E.wait_ge(sem, val)
        step.meta = ("wait", on, val)
        self.steps[eng].append(step)

    def wsem(self, eng, slot, val=None):
        if val is None:
            val = slot.cnt
        key = ("slot", id(slot))
        if val <= 0 or self.hw[eng].get(key, 0) >= val:
            return
        self.hw[eng][key] = val
        step = lambda E, sem=slot.sem, val=val: E.wait_ge(sem, val)
        step.meta = ("wait", key, val)
        self.steps[eng].append(step)

    def emit(self, eng, fn, dep=1):
        if eng in ("v", "s", "g"):
            tgt = self.cnt[eng] + 1 - dep
            if tgt > 0:
                self.wait(eng, eng, tgt)
        self.cnt[eng] += 1
        sem = self.sems[eng]

        def step(E, fn=fn, sem=sem):
            fn(E).then_inc(sem, 1)
        step.meta = ("inc", eng, 1)
        self.steps[eng].append(step)
        return self.cnt[eng]

    def dma(self, eng, slot, fn):
        slot.cnt += 16

        def step(E, fn=fn, sem=slot.sem):
            fn(E).then_inc(sem, 16)
        step.meta = ("inc", ("slot", id(slot)), 16)
        self.steps[eng].append(step)
        return slot.cnt

    def check_deadlock(self):
        pos = {e: 0 for e in ENGINES}
        sem = {}
        while True:
            progress = False
            for e in ENGINES:
                while pos[e] < len(self.steps[e]):
                    m = getattr(self.steps[e][pos[e]], "meta", None)
                    if m is None:
                        pos[e] += 1
                        progress = True
                        continue
                    kind, key, val = m
                    if kind == "inc":
                        sem[key] = sem.get(key, 0) + val
                        pos[e] += 1
                        progress = True
                    else:
                        if sem.get(key, 0) >= val:
                            pos[e] += 1
                            progress = True
                        else:
                            break
            if all(pos[e] >= len(self.steps[e]) for e in ENGINES):
                return
            if not progress:
                rep = []
                for e in ENGINES:
                    if pos[e] < len(self.steps[e]):
                        m = getattr(self.steps[e][pos[e]], "meta", None)
                        rep.append(f"{e}@{pos[e]}/{len(self.steps[e])} blocked on {m} have={sem.get(m[1], 0) if m else '?'}")
                raise RuntimeError("DEADLOCK:\n" + "\n".join(rep))

    def barrier(self, engines=None):
        engines = engines or ENGINES
        snap = dict(self.cnt)
        for e in engines:
            for o in engines:
                self.wait(e, o, snap[o])
            if engines is ENGINES:
                for s in self.sall:
                    self.wsem(e, s, s.cnt)

    def finalize(self):
        self.check_deadlock()
        for h in list(self.sems.values()) + [s.sem for s in self.sall]:
            self.nc.gpsimd.sem_clear(h)
        self.nc.all_engine_barrier()
        with self.nc.Block() as block:
            for name, deco in (("y", block.sync), ("t", block.tensor),
                               ("v", block.vector), ("s", block.scalar),
                               ("g", block.gpsimd)):
                steps = self.steps[name]

                def body(E, steps=steps):
                    for s_ in steps:
                        s_(E)
                deco(body)
        self.st.close()





def row_ap(t, off, w):
    return bass.AP(t, off, [[1, 1], [1, w]])

CONST_VALS = [1e-5, -16.0, 1e-6, 0.0, 1.0]
DEBUG_LN_NOSTATS = False


def emit_consts(b, io):
    cb = {}
    cb["identb"] = b.sb("c_identb", [128, 128], BF16)
    cb["ones_f"] = b.sb("c_ones_f", [128, 128], F32)
    cb["ones_b"] = b.sb("c_ones_b", [1, 512], BF16)
    cb["cpool"] = b.sb("c_cpool", [128, max(8, len(CONST_VALS))], F32)
    ss = b.ssem()
    b.dma("y", ss, lambda E, d=cb["identb"], s=io["identb"]: E.dma_start(d[:, :], s.ap()))
    b.emit("v", lambda E, d=cb["ones_f"]: E.memset(d[:, :], 1.0))
    b.emit("v", lambda E, d=cb["ones_b"]: E.memset(d[:, :], 1.0))
    for i, v in enumerate(CONST_VALS):
        b.emit("v", lambda E, d=cb["cpool"], i=i, v=v: E.memset(d[:, i:i + 1], float(v)))
    b.barrier()
    b.sfree([ss])
    return cb


def ccol(b, v):
    return b.cb["cpool"][:, CONST_VALS.index(v):CONST_VALS.index(v) + 1]



def emit_ln_t(b, src, xT, g_bc=None, b_bc=None, rms_scale_bc=None,
              norm_f32_dst=None, eps=1e-5, mirror_scr=None):
    st = ExitStack()
    NXF = 6
    xf = [b.sb(f"ln_xf{i}", [128, C], F32, st) for i in range(NXF)]
    xn = [b.sb(f"ln_xn{i}", [128, C], BF16, st) for i in range(3)]
    xg = [b.sb(f"ln_xg{i}", [128, C], F32, st) for i in range(2)] if norm_f32_dst is not None else None
    stats = [b.sb(f"ln_st{i}", [128, 8], F32, st) for i in range(4)]
    sq = [b.sb(f"ln_sq{i}", [128, C], F32, st) for i in range(2)]
    pt = [b.ps(f"ln_pt{i}", [128, 1024], BF16, st) for i in range(2)]
    sx = b.ssems(NXF)
    sg_ = b.ssems(2) if norm_f32_dst is not None else None
    smir = b.ssem() if mirror_scr is not None else None
    rel = {}
    sdoneA = {}
    vdoneB = {}
    sdoneC = {}
    tdoneE = {}

    def stageA(i):
        fx, sxx = xf[i % NXF], sx[i % NXF]
        sl = stats[i % 4]
        if ("xf", i % NXF) in rel:
            e, c = rel[("xf", i % NXF)]
            b.wait("y", e, c)
        b.dma("y", sxx, lambda E, d=fx, s=src, i=i: E.dma_start(
            d[:, :], s.ap()[i * 128:(i + 1) * 128, :]))
        b.wsem("v", sxx)
        if ("st", i % 4) in rel:
            e, c = rel[("st", i % 4)]
            b.wait("v", e, c)
            b.wait("s", e, c)
        if rms_scale_bc is None:
            b.emit("v", lambda E, d=sl, s_=fx: E.reduce_sum(d[:, 0:1], s_[:, :], AX.X), dep=3)
        b.wsem("s", sxx)
        if ("sq", i % 2) in rel:
            e, c = rel[("sq", i % 2)]
            b.wait("s", e, c)
        b.emit("s", lambda E, d=sl, s_=fx, q=sq[i % 2]: E.activation(
            q[:, :], s_[:, :], AF.Square, accum_out=d[:, 1:2]), dep=3)
        rel[("sq", i % 2)] = ("s", b.cnt["s"])
        sdoneA[i] = b.cnt["s"]

    def stageB(i):
        sl = stats[i % 4]
        b.wait("v", "s", sdoneA[i])
        if rms_scale_bc is None:
            b.emit("v", lambda E, d=sl: E.tensor_scalar(d[:, 2:3], d[:, 0:1], 1.0 / C, None, ALU.mult))
            b.emit("v", lambda E, d=sl: E.tensor_tensor(d[:, 3:4], d[:, 2:3], d[:, 2:3], ALU.mult))
            b.emit("v", lambda E, d=sl: E.tensor_scalar(d[:, 4:5], d[:, 1:2], 1.0 / C, d[:, 3:4], ALU.mult, ALU.subtract))
        else:
            b.emit("v", lambda E, d=sl: E.memset(d[:, 2:3], 0.0))
        vdoneB[i] = b.cnt["v"]

    def stageC(i):
        sl = stats[i % 4]
        b.wait("s", "v", vdoneB[i])
        if rms_scale_bc is None:
            b.emit("s", lambda E, d=sl: E.activation(d[:, 5:6], d[:, 4:5], AF.Ln, bias=ccol(b, float(eps))), dep=2)
            b.emit("s", lambda E, d=sl: E.activation(d[:, 6:7], d[:, 5:6], AF.Exp, scale=-0.5))
        else:
            b.emit("s", lambda E, d=sl: E.activation(d[:, 5:6], d[:, 1:2], AF.Ln), dep=2)
            b.emit("s", lambda E, d=sl: E.activation(d[:, 6:7], d[:, 5:6], AF.Exp, scale=-0.5))
        sdoneC[i] = b.cnt["s"]

    def stageD(i):
        fx = xf[i % NXF]
        sl = stats[i % 4]
        xo = xn[i % 3]
        b.wait("v", "s", sdoneC[i])
        if rms_scale_bc is not None:
            b.emit("v", lambda E, d=sl: E.tensor_scalar(d[:, 6:7], d[:, 6:7], math.sqrt(C) + 1e-6, None, ALU.mult))
        if ("xn", i % 3) in rel:
            e, c = rel[("xn", i % 3)]
            b.wait("v", e, c)
        b.emit("v", lambda E, d=xo, s_=fx, sl=sl: E.tensor_scalar(
            d[:, :], s_[:, :], sl[:, 2:3], sl[:, 6:7], ALU.subtract, ALU.mult))
        if rms_scale_bc is not None:
            b.emit("v", lambda E, d=xo, g=rms_scale_bc: E.tensor_tensor(d[:, :], d[:, :], g[:, :], ALU.mult))
        if g_bc is not None:
            b.emit("v", lambda E, d=xo, g=g_bc: E.tensor_tensor(d[:, :], d[:, :], g[:, :], ALU.mult))
        if b_bc is not None:
            b.emit("v", lambda E, d=xo, g=b_bc: E.tensor_tensor(d[:, :], d[:, :], g[:, :], ALU.add))
        vdone = b.cnt["v"]
        rel[("xf", i % NXF)] = ("v", vdone)
        rel[("st", i % 4)] = ("v", vdone)
        if norm_f32_dst is not None:
            fg, sgx = xg[i % 2], sg_[i % 2]
            b.wsem("v", sgx)
            b.emit("v", lambda E, d=fg, s_=xo: E.tensor_copy(d[:, :], s_[:, :]))
            b.wait("y", "v", b.cnt["v"])
            b.dma("y", sgx, lambda E, s_=fg, d=norm_f32_dst, i=i: E.dma_start(
                d.ap()[i * 128:(i + 1) * 128, :], s_[:, :]))
        po = pt[i % 2]
        b.wait("t", "v", vdone)
        if ("pt", i % 2) in rel:
            e, c = rel[("pt", i % 2)]
            b.wait("t", e, c)
        for cch in range(8):
            b.emit("t", lambda E, po=po, s_=xo, cch=cch, ib=b.cb["identb"]: E.transpose(
                po[:, cch * 128:(cch + 1) * 128], s_[:, cch * 128:(cch + 1) * 128], ib[:, :]))
        rel[("xn", i % 3)] = ("t", b.cnt["t"])
        tdoneE[i] = b.cnt["t"]

    def stageF(i):
        po = pt[i % 2]
        b.wait("s", "t", tdoneE[i])
        b.emit("s", lambda E, po=po, xT=xT, i=i: E.copy(
            xT.ap().rearrange("p (c n) -> p c n", c=8)[:, :, i * 128:(i + 1) * 128],
            po.ap().rearrange("p (c n) -> p c n", c=8)), dep=2)
        rel[("pt", i % 2)] = ("s", b.cnt["s"])
        if mirror_scr is not None:
            b.wait("y", "s", b.cnt["s"])
            b.dma("y", smir, lambda E, d=mirror_scr, s_=xT, i=i: E.dma_start(
                d.ap()[:, i * 128:(i + 1) * 128].rearrange("(a p) f -> p a f", p=128),
                s_.ap().rearrange("p (c n) -> p c n", c=8)[:, :, i * 128:(i + 1) * 128]))

    for i in range(NT + 4):
        if i < NT:
            stageA(i)
        if 0 <= i - 1 < NT:
            stageB(i - 1)
        if 0 <= i - 2 < NT:
            stageC(i - 2)
        if 0 <= i - 3 < NT:
            stageD(i - 3)
        if 0 <= i - 4 < NT:
            stageF(i - 4)
    b.barrier()
    b.sfree(sx + (sg_ or []) + ([smir] if smir else []))
    st.close()


def _w_cast_load(dst_sb, w_hbm, cols, rows=None, col0=0, row0=0):
    R = rows if rows is not None else w_hbm.shape[0]
    nchunk = R // 128

    def fn(E, d=dst_sb, s=w_hbm, cols=cols, nchunk=nchunk, col0=col0, row0=row0):
        src = s.ap()[row0:row0 + nchunk * 128, col0:col0 + cols].rearrange(
            "(a p) f -> p a f", p=128)
        dd = d.ap().rearrange("p (a f) -> p a f", a=nchunk)
        return E.dma_start(dd, src)
    return fn


def emit_proj_T(b, xT, w_hbm, col0, fout, dst_scr, act=AF.Copy,
                bias_col=None, evac_scale=1.0, ntok=T, act2=None):
    st = ExitStack()
    ntc = ntok // 512
    wl = [b.sb(f"pT_w{i}", [128, 8 * 128], BF16, st) for i in range(2)]
    stg = [b.sb(f"pT_s{i}", [128, ntok], BF16, st) for i in range(2)]
    tmp2 = b.sb("pT_t2", [128, 512], F32, st) if act2 is not None else None
    pp = [b.ps(f"pT_p{i}", [128, 512], F32, st) for i in range(2)]
    sw = b.ssems(2)
    so = b.ssems(2)
    rel = {}
    for ft in range(fout // 128):
        w, sww = wl[ft % 2], sw[ft % 2]
        sg, soo = stg[ft % 2], so[ft % 2]
        if ("w", ft % 2) in rel:
            e, c = rel[("w", ft % 2)]
            b.wait("g", e, c)
        b.dma("g", sww, _w_cast_load(w, w_hbm, 128, col0=col0 + ft * 128))
        b.wsem("s", soo)
        for tc in range(ntc):
            po = pp[tc % 2]
            b.wsem("t", sww)
            if ("pp", tc % 2) in rel:
                e, c = rel[("pp", tc % 2)]
                b.wait("t", e, c)
            for a in range(8):
                b.emit("t", lambda E, po=po, w=w, xT=xT, a=a, tc=tc, ntok=ntok: E.matmul(
                    po[:, :], w[:, a * 128:(a + 1) * 128],
                    xT[:, a * ntok + tc * 512: a * ntok + tc * 512 + 512],
                    start=(a == 0), stop=(a == 7)))
            b.wait("s", "t", b.cnt["t"])
            kw = {}
            if bias_col is not None:
                kw["bias"] = bias_col[:, ft:ft + 1]
            if act == AF.Copy and bias_col is not None:
                b.wait("v", "t", b.cnt["t"])
                b.wsem("v", soo)
                b.emit("v", lambda E, po=po, sg=sg, tc=tc, bc=bias_col, ft=ft: E.tensor_scalar(
                    sg[:, tc * 512:(tc + 1) * 512], po[:, :], bc[:, ft:ft + 1], None, ALU.add))
                rel[("pp", tc % 2)] = ("v", b.cnt["v"])
                continue
            if act2 is None:
                b.emit("s", lambda E, po=po, sg=sg, tc=tc, act=act, kw=kw, sc=evac_scale: E.activation(
                    sg[:, tc * 512:(tc + 1) * 512], po[:, :], act, scale=sc, **kw), dep=2)
            else:
                b.emit("s", lambda E, po=po, act=act, kw=kw, sc=evac_scale: E.activation(
                    tmp2[:, :], po[:, :], act, scale=sc, **kw))
                a2f, a2s = act2
                b.emit("s", lambda E, sg=sg, tc=tc, a2f=a2f, a2s=a2s: E.activation(
                    sg[:, tc * 512:(tc + 1) * 512], tmp2[:, :], a2f, scale=a2s))
            rel[("pp", tc % 2)] = ("s", b.cnt["s"])
        b.wait("y", "s", b.cnt["s"])
        b.wait("y", "v", b.cnt["v"])
        b.dma("y", soo, lambda E, sg=sg, d=dst_scr, ft=ft: E.dma_start(
            d.ap()[ft * 128:(ft + 1) * 128, :], sg[:, :]))
        rel[("w", ft % 2)] = ("t", b.cnt["t"])
    b.barrier()
    b.sfree(sw + so)
    st.close()


def emit_proj_N(b, xT, w_hbm, col0, fout, dst_scr, act=AF.Copy,
                bias_row=None, bias_off=0, evac_scale=1.0, ntok=T, act2=None,
                dst_col0=0):
    st = ExitStack()
    nt_ = ntok // 128
    wr = [b.sb(f"pN_w{i}", [128, 8 * 512], BF16, st) for i in range(2)]
    stg = [b.sb(f"pN_s{i}", [128, 512], BF16, st) for i in range(4)]
    tmp2 = b.sb("pN_t2", [128, 512], F32, st) if act2 is not None else None
    pp = [b.ps(f"pN_p{i}", [128, 512], F32, st) for i in range(2)]
    sw = b.ssems(2)
    so = b.ssems(4)
    br = None
    if bias_row is not None:
        br = b.sb("pN_br", [1, fout], BF16, st)
        sb_ = b.ssem()
        b.dma("g", sb_, lambda E, d=br, s=bias_row, o=bias_off, f=fout: E.dma_start(
            d[0:1, :], row_ap(s, o, f)))
        b.wsem("t", sb_)
        b.sfree([sb_])
    rel = {}
    for fc in range(fout // 512):
        w, sww = wr[fc % 2], sw[fc % 2]
        if ("w", fc % 2) in rel:
            e, c = rel[("w", fc % 2)]
            b.wait("g", e, c)
        b.dma("g", sww, _w_cast_load(w, w_hbm, 512, col0=col0 + fc * 512))
        for tt in range(nt_):
            po = pp[tt % 2]
            sg, soo = stg[tt % 4], so[tt % 4]
            b.wsem("t", sww)
            if ("pp", tt % 2) in rel:
                e, c = rel[("pp", tt % 2)]
                b.wait("t", e, c)
            for a in range(8):
                b.emit("t", lambda E, po=po, w=w, xT=xT, a=a, tt=tt, nt=ntok: E.matmul(
                    po[:, :], xT[:, a * nt + tt * 128: a * nt + tt * 128 + 128],
                    w[:, a * 512:(a + 1) * 512],
                    start=(a == 0), stop=(bias_row is None and a == 7)))
            if bias_row is not None:
                b.emit("t", lambda E, po=po, br=br, fc=fc, ob=b.cb["ones_b"]: E.matmul(
                    po[:, :], ob[0:1, 0:128], br[0:1, fc * 512:(fc + 1) * 512],
                    start=False, stop=True))
            b.wait("s", "t", b.cnt["t"])
            b.wsem("s", soo)
            if act2 is None:
                b.emit("s", lambda E, po=po, sg=sg, act=act, sc=evac_scale: E.activation(
                    sg[:, :], po[:, :], act, scale=sc), dep=2)
            else:
                b.emit("s", lambda E, po=po, act=act, sc=evac_scale: E.activation(
                    tmp2[:, :], po[:, :], act, scale=sc))
                a2f, a2s = act2
                b.emit("s", lambda E, sg=sg, a2f=a2f, a2s=a2s: E.activation(
                    sg[:, :], tmp2[:, :], a2f, scale=a2s))
            rel[("pp", tt % 2)] = ("s", b.cnt["s"])
            b.wait("y", "s", b.cnt["s"])
            b.dma("y", soo, lambda E, sg=sg, d=dst_scr, tt=tt, fc=fc, o=dst_col0: E.dma_start(
                d.ap()[tt * 128:(tt + 1) * 128, o + fc * 512:o + (fc + 1) * 512], sg[:, :]))
        rel[("w", fc % 2)] = ("t", b.cnt["t"])
    b.barrier()
    b.sfree(sw + so)
    st.close()


def emit_lin_attn_causal(b, qT_scr, kT_scr, kn_scr, vn_scr, masks_sb, attT_scr, eps=1e-5):
    st = ExitStack()
    qTp = b.sb("la_q", [128, T], BF16, st)
    kTp = b.sb("la_k", [128, T], BF16, st)
    knp = b.sb("la_kn", [128, NT * 128], BF16, st)
    vp = [b.sb(f"la_v{i}", [128, NT * 65], BF16, st) for i in range(2)]
    state_f = b.sb("la_stf", [128, 65], F32, st)
    state_b = b.sb("la_stb", [128, 65], BF16, st)
    on_sb = [b.sb(f"la_on{i}", [128, 512], F32, st) for i in range(2)]
    d_sb = b.sb("la_d", [128, 512], F32, st)
    obh = [b.sb(f"la_ob{i}", [64, T], BF16, st) for i in range(2)]
    msk = [b.sb(f"la_m{i}", [128, 512], BF16, st) for i in range(4)]
    PO = [[b.ps(f"la_po{i}{j}", [65, 512], F32, st) for j in range(2)] for i in range(2)]
    PS = [b.ps(f"la_ps{i}", [128, 512], F32, st) for i in range(2)]
    PK = b.ps("la_pk", [128, 65], F32, st)
    PB = b.ps("la_pb", [64, 512], F32, st)
    sl = {k: b.ssem() for k in ("q", "k", "kn", "v0", "v1", "ob0", "ob1")}
    rel = {}
    for p in range(NH // 2):
        if ("pair",) in rel:
            e, c = rel[("pair",)]
            b.wait("y", e, c)
        b.dma("y", sl["q"], lambda E, d=qTp, s=qT_scr, p=p: E.dma_start(
            d[:, :], s.ap()[p * 128:(p + 1) * 128, :]))
        b.dma("y", sl["k"], lambda E, d=kTp, s=kT_scr, p=p: E.dma_start(
            d[:, :], s.ap()[p * 128:(p + 1) * 128, :]))
        b.dma("y", sl["kn"], lambda E, d=knp, s=kn_scr, p=p: E.dma_start(
            d.ap().rearrange("p (a f) -> p a f", a=NT),
            s.ap()[:, p * 128:(p + 1) * 128].rearrange("(a p) f -> p a f", p=128)))
        for hh in range(2):
            vb = vp[hh]
            if ("pair",) in rel:
                e, c = rel[("pair",)]
                b.wait("v", e, c)
            b.emit("v", lambda E, d=vb: E.memset(
                d.ap().rearrange("p (a f) -> p a f", a=NT)[:, :, 64:65], 1.0))
            b.wait("y", "v", b.cnt["v"])
            b.dma("y", sl["v" + str(hh)], lambda E, d=vb, s=vn_scr, p=p, hh=hh: E.dma_start(
                d.ap().rearrange("p (a f) -> p a f", a=NT)[:, :, 0:64],
                s.ap()[:, p * 128 + hh * 64: p * 128 + hh * 64 + 64].rearrange(
                    "(a p) f -> p a f", p=128)))
        for e_ in ("t", "v", "s"):
            for k_ in ("q", "k", "kn", "v0", "v1"):
                b.wsem(e_, sl[k_])
        b.wait("v", "t", b.cnt["t"])
        b.emit("v", lambda E, d=state_f: E.memset(d[:, :], 0.0))
        b.emit("v", lambda E, d=state_b: E.memset(d[:, :], 0.0))
        epis = []
        for gi in range(NTC):
            g0 = gi * 512
            gp = gi % 2
            b.wait("t", "v", b.cnt["v"])
            for hh in range(2):
                hb = 64 * hh
                if ("pog", hh, gp) in rel:
                    e, c = rel[("pog", hh, gp)]
                    b.wait("t", e, c)
                b.emit("t", lambda E, hb=hb, hh=hh, gp=gp, g0=g0: E.matmul(
                    PO[hh][gp][:, :], state_b[hb:hb + 64, :], qTp[hb:hb + 64, g0:g0 + 512],
                    start=True, stop=False))
            if epis:
                epis.pop(0)()
            for j in range(4):
                kb = gi * 4 + j
                for hh in range(2):
                    hb = 64 * hh
                    pS = PS[hh]
                    mk = msk[hh * 2 + j % 2]
                    if ("ps", hh) in rel:
                        e, c = rel[("ps", hh)]
                        b.wait("t", e, c)
                    b.emit("t", lambda E, pS=pS, hb=hb, kb=kb, g0=g0: E.matmul(
                        pS[:, :], kTp[hb:hb + 64, kb * 128:kb * 128 + 128],
                        qTp[hb:hb + 64, g0:g0 + 512], start=True, stop=True))
                    b.wait("v", "t", b.cnt["t"])
                    if ("mk", hh, j % 2) in rel:
                        e, c = rel[("mk", hh, j % 2)]
                        b.wait("v", e, c)
                    b.emit("v", lambda E, pS=pS, mk=mk, ms=masks_sb, j=j: E.tensor_tensor(
                        mk[:, :], pS[:, :], ms[:, j * 512:(j + 1) * 512], ALU.mult), dep=2)
                    rel[("ps", hh)] = ("v", b.cnt["v"])
                    b.wait("t", "v", b.cnt["v"])
                    b.emit("t", lambda E, hh=hh, gp=gp, mk=mk, kb=kb, j=j: E.matmul(
                        PO[hh][gp][:, :], vp[hh][:, kb * 65:kb * 65 + 65], mk[:, :],
                        start=False, stop=(j == 3)))
                    rel[("mk", hh, j % 2)] = ("t", b.cnt["t"])
            b.wait("t", "v", b.cnt["v"])
            for hh in range(2):
                hb = 64 * hh
                for j in range(4):
                    kb = gi * 4 + j
                    b.emit("t", lambda E, hb=hb, hh=hh, kb=kb, j=j: E.matmul(
                        PK[hb:hb + 64, :], knp[:, kb * 128 + hb: kb * 128 + hb + 64],
                        vp[hh][:, kb * 65:kb * 65 + 65], start=(j == 0), stop=(j == 3)))
            pkdone = b.cnt["t"]
            b.wait("v", "t", pkdone)
            for hh in range(2):
                hb = 64 * hh
                b.emit("v", lambda E, hb=hb: E.tensor_tensor(
                    state_f[hb:hb + 64, :], state_f[hb:hb + 64, :], PK[hb:hb + 64, :], ALU.add))
                b.emit("v", lambda E, hb=hb: E.tensor_copy(
                    state_b[hb:hb + 64, :], state_f[hb:hb + 64, :]))
            svdone = b.cnt["t"]

            def mk_epi(gi=gi, gp=gp, g0=g0, svdone=svdone):
                def epi():
                    for hh in range(2):
                        hb2 = 64 * hh
                        b.wait("v", "t", svdone)
                        b.emit("v", lambda E, hh=hh, gp=gp, hb2=hb2, eps=eps: E.tensor_scalar(
                            d_sb[hb2:hb2 + 1, :], PO[hh][gp][64:65, :], float(eps), None, ALU.add))
                        b.emit("v", lambda E, hb2=hb2: E.reciprocal(
                            d_sb[hb2:hb2 + 1, :], d_sb[hb2:hb2 + 1, :]))
                        vdone = b.cnt["v"]
                        b.wait("t", "v", vdone)
                        if ("pb",) in rel:
                            e, c = rel[("pb",)]
                            b.wait("t", e, c)
                        b.emit("t", lambda E, hb2=hb2, of=b.cb["ones_f"]: E.matmul(
                            PB[:, :], of[hb2:hb2 + 1, 0:64], d_sb[hb2:hb2 + 1, :],
                            start=True, stop=True))
                        b.wait("s", "t", b.cnt["t"])
                        b.emit("s", lambda E, hh=hh, gp=gp: E.copy(
                            on_sb[hh][0:64, :], PO[hh][gp][0:64, :]))
                        b.wait("v", "s", b.cnt["s"])
                        b.wsem("v", sl["ob" + str(hh)])
                        b.emit("v", lambda E, hh=hh, g0=g0: E.tensor_tensor(
                            obh[hh][0:64, g0:g0 + 512], on_sb[hh][0:64, :], PB[0:64, :], ALU.mult))
                        rel[("pb",)] = ("v", b.cnt["v"])
                        rel[("pog", hh, gp)] = ("v", b.cnt["v"])
                return epi
            epis.append(mk_epi())
        for ep in epis:
            ep()
        for hh in range(2):
            b.wait("y", "v", b.cnt["v"])
            b.dma("y", sl["ob" + str(hh)], lambda E, d=attT_scr, s=obh[hh], p=p, hh=hh: E.dma_start(
                d.ap()[(p * 2 + hh) * 64:(p * 2 + hh) * 64 + 64, :], s[:, :]))
        rel[("pair",)] = ("t", b.cnt["t"])
    b.barrier()
    b.sfree(list(sl.values()))
    st.close()


def emit_out_proj(b, inT_scr, w_hbm, fout, res_src, dst, bias_row=None,
                  extra_res=None, gamma_bc=None, cin=C):
    st = ExitStack()
    nch = cin // 128
    w_sb = b.sb("op_w", [128, nch * fout], BF16, st)
    aT = [b.sb(f"op_a{i}", [128, nch * 512], BF16, st) for i in range(2)]
    res = [b.sb(f"op_r{i}", [128, fout], F32, st) for i in range(2)]
    ex = [b.sb(f"op_e{i}", [128, fout], F32, st) for i in range(2)] if extra_res is not None else None
    out = [b.sb(f"op_o{i}", [128, fout], F32, st) for i in range(2)]
    tmp = b.sb("op_t", [128, 512], F32, st)
    br = b.sb("op_br", [1, fout], BF16, st) if bias_row is not None else None
    pp = [b.ps(f"op_p{i}", [128, 512], F32, st) for i in range(2)]
    sw = b.ssem()
    sa = b.ssems(2)
    sr = b.ssems(2)
    se = b.ssems(2) if extra_res is not None else None
    sd = b.ssems(2)
    b.dma("g", sw, _w_cast_load(w_sb, w_hbm, fout))
    if bias_row is not None:
        b.dma("g", sw, lambda E, d=br, s=bias_row: E.dma_start(
            d[0:1, :], row_ap(s, 0, fout)))
    rel = {}
    for tc in range(NTC):
        a, saa = aT[tc % 2], sa[tc % 2]
        if ("a", tc % 2) in rel:
            e, c = rel[("a", tc % 2)]
            b.wait("y", e, c)
        b.dma("y", saa, lambda E, d=a, s=inT_scr, tc=tc, nch=nch: E.dma_start(
            d.ap().rearrange("p (a f) -> p a f", a=nch),
            s.ap()[:, tc * 512:(tc + 1) * 512].rearrange("(a p) f -> p a f", p=128)))
        for tl in range(4):
            tt = tc * 4 + tl
            rs, srr = res[tt % 2], sr[tt % 2]
            ot, sdd = out[tt % 2], sd[tt % 2]
            if ("rs", tt % 2) in rel:
                e, c = rel[("rs", tt % 2)]
                b.wait("y", e, c)
            b.dma("y", srr, lambda E, d=rs, s=res_src, tt=tt: E.dma_start(
                d[:, :], s.ap()[tt * 128:(tt + 1) * 128, :]))
            if extra_res is not None:
                b.dma("y", se[tt % 2], lambda E, d=ex[tt % 2], s=extra_res, tt=tt: E.dma_start(
                    d[:, :], s.ap()[tt * 128:(tt + 1) * 128, :]))
            for fc in range(fout // 512):
                po = pp[fc % 2]
                b.wsem("t", sw)
                b.wsem("t", saa)
                if ("pp", fc % 2) in rel:
                    e, c = rel[("pp", fc % 2)]
                    b.wait("t", e, c)
                for ac in range(nch):
                    b.emit("t", lambda E, po=po, a=a, ac=ac, tl=tl, fc=fc, fout=fout: E.matmul(
                        po[:, :], a[:, ac * 512 + tl * 128: ac * 512 + tl * 128 + 128],
                        w_sb[:, ac * fout + fc * 512: ac * fout + fc * 512 + 512],
                        start=(ac == 0), stop=(bias_row is None and ac == nch - 1)))
                if bias_row is not None:
                    b.emit("t", lambda E, po=po, br=br, fc=fc, ob=b.cb["ones_b"]: E.matmul(
                        po[:, :], ob[0:1, 0:128], br[0:1, fc * 512:(fc + 1) * 512],
                        start=False, stop=True))
                tdone = b.cnt["t"]
                b.wait("v", "t", tdone)
                b.wsem("v", srr)
                if extra_res is not None:
                    b.wsem("v", se[tt % 2])
                b.wsem("v", sdd)
                if gamma_bc is None:
                    b.emit("v", lambda E, ot=ot, rs=rs, po=po, fc=fc: E.tensor_tensor(
                        ot[:, fc * 512:(fc + 1) * 512], rs[:, fc * 512:(fc + 1) * 512],
                        po[:, :], ALU.add))
                else:
                    b.emit("v", lambda E, po=po, g=gamma_bc, fc=fc: E.tensor_tensor(
                        tmp[:, :], po[:, :], g[:, fc * 512:(fc + 1) * 512], ALU.mult))
                    b.emit("v", lambda E, ot=ot, rs=rs, fc=fc: E.tensor_tensor(
                        ot[:, fc * 512:(fc + 1) * 512], rs[:, fc * 512:(fc + 1) * 512],
                        tmp[:, :], ALU.add))
                if extra_res is not None:
                    b.emit("v", lambda E, ot=ot, s_=ex[tt % 2], fc=fc: E.tensor_tensor(
                        ot[:, fc * 512:(fc + 1) * 512], ot[:, fc * 512:(fc + 1) * 512],
                        s_[:, fc * 512:(fc + 1) * 512], ALU.add))
                rel[("pp", fc % 2)] = ("v", b.cnt["v"])
            rel[("rs", tt % 2)] = ("v", b.cnt["v"])
            b.wait("y", "v", b.cnt["v"])
            b.dma("y", sdd, lambda E, s_=ot, d=dst, tt=tt: E.dma_start(
                d.ap()[tt * 128:(tt + 1) * 128, :], s_[:, :]))
        rel[("a", tc % 2)] = ("t", b.cnt["t"])
    b.barrier()
    b.sfree([sw] + sa + sr + (se or []) + sd)
    st.close()




def emit_bcast_row(b, vec_hbm, width, dst=None, off=0, scale=None):
    out = dst if dst is not None else b.sb("bc_out", [128, width], F32)
    st = ExitStack()
    row = b.sb("bc_row", [1, width], F32, st)
    pb = b.ps("bc_p", [128, 512], F32, st)
    ss = b.ssem()
    b.dma("y", ss, lambda E, d=row, s=vec_hbm, off=off, w=width: E.dma_start(
        d[0:1, :], row_ap(s, off, w)))
    b.wsem("t", ss)
    for fc in range((width + 511) // 512):
        w = min(512, width - fc * 512)
        b.wait("t", "s", b.cnt["s"])
        b.emit("t", lambda E, pb=pb, row=row, fc=fc, w=w, of=b.cb["ones_f"]: E.matmul(
            pb[:, 0:w], of[0:1, :], row[0:1, fc * 512:fc * 512 + w],
            start=True, stop=True))
        b.wait("s", "t", b.cnt["t"])
        if scale is None:
            b.emit("s", lambda E, pb=pb, out=out, fc=fc, w=w: E.copy(
                out[:, fc * 512:fc * 512 + w], pb[:, 0:w]))
        else:
            b.emit("s", lambda E, pb=pb, out=out, fc=fc, w=w, sc=scale: E.mul(
                out[:, fc * 512:fc * 512 + w], pb[:, 0:w], sc))
    b.barrier()
    b.sfree([ss])
    st.close()
    return out


def emit_bcast_scalar(b, sc_hbm, nparts=128):
    out = b.sb("bs_o", [128, 1], F32)
    st = ExitStack()
    cell = b.sb("bs_c", [1, 1], F32, st)
    pb = b.ps("bs_p", [128, 1], F32, st)
    ss = b.ssem()
    b.dma("y", ss, lambda E, d=cell, s=sc_hbm: E.dma_start(d[0:1, 0:1], s.ap()))
    b.wsem("t", ss)
    b.emit("t", lambda E, of=b.cb["ones_f"]: E.matmul(
        pb[:, :], of[0:1, :], cell[0:1, 0:1], start=True, stop=True))
    b.wait("s", "t", b.cnt["t"])
    b.emit("s", lambda E: E.copy(out[:, :], pb[:, :]))
    b.barrier()
    b.sfree([ss])
    st.close()
    return out


def emit_weff(b, w_hbm, rows, cols, dst_scr, barrier=True):
    st = ExitStack()
    wt = [b.sb(f"we_w{i}", [128, cols], F32, st) for i in range(2)]
    tm = b.sb("we_t", [128, cols], F32, st)
    stg = [b.sb(f"we_s{i}", [128, cols], BF16, st) for i in range(2)]
    sc = b.sb("we_sc", [128, 4], F32, st)
    sl_ = b.ssems(2)
    so = b.ssems(2)
    rel = {}
    for i in range(rows // 128):
        w, sw = wt[i % 2], sl_[i % 2]
        sg, soo = stg[i % 2], so[i % 2]
        if ("w", i % 2) in rel:
            e, c = rel[("w", i % 2)]
            b.wait("y", e, c)
        b.dma("y", sw, lambda E, d=w, s=w_hbm, i=i: E.dma_start(
            d[:, :], s.ap()[i * 128:(i + 1) * 128, :]))
        b.wsem("v", sw)
        b.emit("v", lambda E, w=w: E.reduce_sum(sc[:, 0:1], w[:, :], AX.X,
                                                apply_absolute_value=True))
        b.emit("v", lambda E, cols=cols: E.tensor_scalar(
            sc[:, 1:2], sc[:, 0:1], 1.0 / cols, 1e-5, ALU.mult, ALU.max))
        b.emit("v", lambda E: E.reciprocal(sc[:, 2:3], sc[:, 1:2]))
        b.emit("v", lambda E, w=w: E.tensor_scalar(
            tm[:, :], w[:, :], sc[:, 2:3], None, ALU.mult))
        b.emit("v", lambda E: E.tensor_scalar(
            tm[:, :], tm[:, :], RNE, -RNE, ALU.add, ALU.add))
        b.emit("v", lambda E: E.tensor_scalar(
            tm[:, :], tm[:, :], -1.0, 1.0, ALU.max, ALU.min))
        b.wsem("v", soo)
        b.emit("v", lambda E, sg=sg: E.tensor_scalar(
            sg[:, :], tm[:, :], sc[:, 1:2], None, ALU.mult))
        rel[("w", i % 2)] = ("v", b.cnt["v"])
        b.wait("y", "v", b.cnt["v"])
        b.dma("y", soo, lambda E, sg=sg, d=dst_scr, i=i: E.dma_start(
            d.ap()[i * 128:(i + 1) * 128, :], sg[:, :]))
    b.barrier()
    b.sfree(sl_ + so)
    st.close()
    return (list(so), b.cnt["v"], None)


def _ld_natural(w_hbm, cols, col0=0):
    def fn(b, dst_sb, slot):
        b.dma("g", slot, _w_cast_load(dst_sb, w_hbm, cols, col0=col0))
    return fn


def _ld_transposed(w_scr, rows, nchunk):
    def fn(b, dst_sb, slot):
        for a in range(nchunk):
            b.dma("y", slot, lambda E, d=dst_sb, s=w_scr, a=a, rows=rows: E.dma_start_transpose(
                d.ap().rearrange("p (a f) -> p a f", a=nchunk)[:, a, :],
                s.ap()[:, a * 128:(a + 1) * 128]))
    return fn


def emit_ffn(b, xT_scr, w1_ld, w2_ld, dst, res_src, act1, hwidth=4096,
             h_bias_col=None, out_bias_row=None, out_bias_off=0,
             extra_res=None, lowrank=None, gamma_bc=None, io=None):
    st = ExitStack()
    nhc = hwidth // 128
    w1 = b.sb("ff_w1", [128, 8 * hwidth], BF16, st)
    w2 = b.sb("ff_w2", [128, nhc * 1024], BF16, st)
    hT = b.sb("ff_h", [128, nhc * 512], BF16, st)
    xTb = [b.sb(f"ff_x{i}", [128, 8 * 512], BF16, st) for i in range(2)]
    rtmp = b.sb("ff_rt", [128, 512], F32, st)
    res = [b.sb(f"ff_r{i}", [128, C], F32, st) for i in range(2)]
    ex = [b.sb(f"ff_e{i}", [128, C], F32, st) for i in range(2)] if extra_res is not None else None
    outb = [b.sb(f"ff_o{i}", [128, C], F32, st) for i in range(2)]
    br = b.sb("ff_br", [1, C], BF16, st) if out_bias_row is not None else None
    ph = [b.ps(f"ff_ph{i}", [128, 512], F32, st) for i in range(2)]
    pout = [b.ps(f"ff_po{i}", [128, 512], F32, st) for i in range(2)]
    sw1, sw2 = b.ssem(), b.ssem()
    sx = b.ssems(2)
    sr = b.ssems(2)
    se = b.ssems(2) if extra_res is not None else None
    sd = b.ssems(2)
    w1_ld(b, w1, sw1)
    w2_ld(b, w2, sw2)
    if out_bias_row is not None:
        b.dma("g", sw2, lambda E, d=br, s=out_bias_row, o=out_bias_off: E.dma_start(
            d[0:1, :], row_ap(s, o, C)))
    lr = None
    if lowrank is not None:
        lr = {}
        lr["vT"] = b.sb("ff_vT", [128, 8 * 16], BF16, st)
        lr["fbT"] = b.sb("ff_fb", [16, 16], BF16, st)
        lr["uT"] = b.sb("ff_uT", [16, hwidth], BF16, st)
        lr["t1"] = b.sb("ff_t1", [16, 512], BF16, st)
        lr["t2"] = b.sb("ff_t2", [16, 512], BF16, st)
        lr["plr"] = b.ps("ff_plr", [16, 512], F32, st)
        b.dma("g", sw1, _w_cast_load(lr["vT"], lowrank["vT"], 16))
        b.dma("g", sw1, lambda E, d=lr["fbT"], s=lowrank["fbT"]: E.dma_start(
            d[:, :], s.ap()))
        b.dma("g", sw1, lambda E, d=lr["uT"], s=lowrank["uT"]: E.dma_start(
            d[:, :], s.ap()))
        gate_col = lowrank["gate_col"]
    b.barrier()
    rel = {}
    for tb in range(NTC):
        xb, sxx = xTb[tb % 2], sx[tb % 2]
        if ("x", tb % 2) in rel:
            e, c = rel[("x", tb % 2)]
            b.wait("y", e, c)
        b.dma("y", sxx, lambda E, d=xb, s=xT_scr, tb=tb: E.dma_start(
            d.ap().rearrange("p (a f) -> p a f", a=8),
            s.ap()[:, tb * 512:(tb + 1) * 512].rearrange("(a p) f -> p a f", p=128)))
        b.wsem("t", sxx)
        if lowrank is not None:
            plr = lr["plr"]
            b.wait("t", "v", b.cnt["v"])
            b.wait("t", "s", b.cnt["s"])
            for a in range(8):
                b.emit("t", lambda E, plr=plr, xb=xb, a=a: E.matmul(
                    plr[:, :], lr["vT"][:, a * 16:(a + 1) * 16], xb[:, a * 512:(a + 1) * 512],
                    start=(a == 0), stop=(a == 7)))
            b.wait("s", "t", b.cnt["t"])
            b.emit("s", lambda E: E.copy(lr["t1"][:, :], plr[:, :]))
            b.wait("t", "s", b.cnt["s"])
            b.emit("t", lambda E: E.matmul(plr[:, :], lr["fbT"][:, :], lr["t1"][:, :],
                                           start=True, stop=True))
            b.wait("v", "t", b.cnt["t"])
            b.emit("v", lambda E, gc=gate_col: E.tensor_scalar(
                lr["t2"][:, :], plr[:, :], gc[0:16, 0:1], None, ALU.mult))
            b.wait("t", "v", b.cnt["v"])
        for ft in range(nhc):
            po = ph[ft % 2]
            if ("ph", ft % 2) in rel:
                e, c = rel[("ph", ft % 2)]
                b.wait("t", e, c)
            nlast = 8 if lowrank is None else 9
            for a in range(8):
                b.emit("t", lambda E, po=po, xb=xb, a=a, ft=ft, hw=hwidth: E.matmul(
                    po[:, :], w1[:, a * hw + ft * 128: a * hw + ft * 128 + 128],
                    xb[:, a * 512:(a + 1) * 512],
                    start=(a == 0), stop=(a == nlast - 1)))
            if lowrank is not None:
                b.emit("t", lambda E, po=po, ft=ft: E.matmul(
                    po[:, :], lr["uT"][:, ft * 128:ft * 128 + 128], lr["t2"][:, :],
                    start=False, stop=True))
            if act1 == "relu2":
                b.wait("s", "t", b.cnt["t"])
                b.wait("s", "v", b.cnt["v"])
                b.emit("s", lambda E, po=po: E.activation(rtmp[:, :], po[:, :], AF.Relu))
                b.wait("v", "s", b.cnt["s"])
                b.emit("v", lambda E, ft=ft: E.tensor_tensor(
                    hT[:, ft * 512:(ft + 1) * 512], rtmp[:, :], rtmp[:, :], ALU.mult))
                rel[("ph", ft % 2)] = ("v", b.cnt["v"])
                rel[("hT",)] = ("v", b.cnt["v"])
            else:
                fn = AF.Gelu_apprx_tanh if act1 == "gelu_tanh" else AF.Gelu
                kw = {}
                if h_bias_col is not None:
                    kw["bias"] = h_bias_col[:, ft:ft + 1]
                b.wait("s", "t", b.cnt["t"])
                b.emit("s", lambda E, po=po, ft=ft, fn=fn, kw=kw: E.activation(
                    hT[:, ft * 512:(ft + 1) * 512], po[:, :], fn, **kw))
                rel[("ph", ft % 2)] = ("s", b.cnt["s"])
                rel[("hT",)] = ("s", b.cnt["s"])
        b.wsem("t", sw2)
        e, c = rel[("hT",)]
        b.wait("t", e, c)
        for tl in range(4):
            tt = tb * 4 + tl
            rs, srr = res[tt % 2], sr[tt % 2]
            ot, sdd = outb[tt % 2], sd[tt % 2]
            if ("rs", tt % 2) in rel:
                e, c = rel[("rs", tt % 2)]
                b.wait("y", e, c)
            b.dma("y", srr, lambda E, d=rs, s=res_src, tt=tt: E.dma_start(
                d[:, :], s.ap()[tt * 128:(tt + 1) * 128, :]))
            if extra_res is not None:
                b.dma("y", se[tt % 2], lambda E, d=ex[tt % 2], s=extra_res, tt=tt: E.dma_start(
                    d[:, :], s.ap()[tt * 128:(tt + 1) * 128, :]))
            for fh in range(2):
                po = pout[fh]
                if ("pout", fh) in rel:
                    e, c = rel[("pout", fh)]
                    b.wait("t", e, c)
                for hc in range(nhc):
                    b.emit("t", lambda E, po=po, hc=hc, tl=tl, fh=fh: E.matmul(
                        po[:, :], hT[:, hc * 512 + tl * 128: hc * 512 + tl * 128 + 128],
                        w2[:, hc * 1024 + fh * 512: hc * 1024 + fh * 512 + 512],
                        start=(hc == 0), stop=(out_bias_row is None and hc == nhc - 1)))
                if out_bias_row is not None:
                    b.emit("t", lambda E, po=po, fh=fh, ob=b.cb["ones_b"]: E.matmul(
                        po[:, :], ob[0:1, 0:128], br[0:1, fh * 512:(fh + 1) * 512],
                        start=False, stop=True))
                b.wait("v", "t", b.cnt["t"])
                b.wsem("v", srr)
                b.wsem("v", sdd)
                if extra_res is not None:
                    b.wsem("v", se[tt % 2])
                if gamma_bc is None:
                    b.emit("v", lambda E, ot=ot, rs=rs, po=po, fh=fh: E.tensor_tensor(
                        ot[:, fh * 512:(fh + 1) * 512], rs[:, fh * 512:(fh + 1) * 512],
                        po[:, :], ALU.add))
                else:
                    b.emit("v", lambda E, po=po, fh=fh, g=gamma_bc: E.tensor_tensor(
                        rtmp[:, :], po[:, :], g[:, fh * 512:(fh + 1) * 512], ALU.mult))
                    b.emit("v", lambda E, ot=ot, rs=rs, fh=fh: E.tensor_tensor(
                        ot[:, fh * 512:(fh + 1) * 512], rs[:, fh * 512:(fh + 1) * 512],
                        rtmp[:, :], ALU.add))
                if extra_res is not None:
                    b.emit("v", lambda E, ot=ot, fh=fh, x_=ex[tt % 2]: E.tensor_tensor(
                        ot[:, fh * 512:(fh + 1) * 512], ot[:, fh * 512:(fh + 1) * 512],
                        x_[:, fh * 512:(fh + 1) * 512], ALU.add))
                rel[("pout", fh)] = ("v", b.cnt["v"])
            rel[("rs", tt % 2)] = ("v", b.cnt["v"])
            b.wait("y", "v", b.cnt["v"])
            b.dma("y", sdd, lambda E, s_=ot, d=dst, tt=tt: E.dma_start(
                d.ap()[tt * 128:(tt + 1) * 128, :], s_[:, :]))
        rel[("x", tb % 2)] = ("t", b.cnt["t"])
    b.barrier()
    b.sfree([sw1, sw2] + sx + sr + (se or []) + sd)
    st.close()


def emit_attn_softmax(b, qT_scr, kT_scr, vn_scr, attT_scr, Lk, mode,
                      masks_sb, scorer_col=None):
    st = ExitStack()
    nkb = Lk // 128
    qTp = b.sb("sa_q", [128, T], BF16, st)
    kTp = b.sb("sa_k", [128, Lk], BF16, st)
    vp = [b.sb(f"sa_v{i}", [128, nkb * 65], BF16, st) for i in range(2)]
    eb = [b.sb(f"sa_e{i}", [128, 4 * 512], BF16, st) for i in range(2)]
    on_sb = [b.sb(f"sa_on{i}", [128, 512], F32, st) for i in range(2)]
    d_sb = b.sb("sa_d", [128, 512], F32, st)
    obh = [b.sb(f"sa_ob{i}", [64, T], BF16, st) for i in range(2)]
    PS = [b.ps(f"sa_ps{i}", [128, 512], F32, st) for i in range(4)]
    PO = [b.ps(f"sa_po{i}", [65, 512], F32, st) for i in range(2)]
    PB = b.ps("sa_pb", [64, 512], F32, st)
    sl = {k: b.ssem() for k in ("q", "k", "v0", "v1", "ob0", "ob1")}
    rel = {}
    for p in range(NH // 2):
        if ("pair",) in rel:
            e, c = rel[("pair",)]
            b.wait("y", e, c)
        b.dma("y", sl["q"], lambda E, d=qTp, s=qT_scr, p=p: E.dma_start(
            d[:, :], s.ap()[p * 128:(p + 1) * 128, :]))
        b.dma("y", sl["k"], lambda E, d=kTp, s=kT_scr, p=p: E.dma_start(
            d[:, :], s.ap()[p * 128:(p + 1) * 128, :]))
        for hh in range(2):
            vb = vp[hh]
            if ("pair",) in rel:
                e, c = rel[("pair",)]
                b.wait("v", e, c)
            b.emit("v", lambda E, d=vb, n=nkb: E.memset(
                d.ap().rearrange("p (a f) -> p a f", a=n)[:, :, 64:65], 1.0))
            b.wait("y", "v", b.cnt["v"])
            b.dma("y", sl["v" + str(hh)], lambda E, d=vb, s=vn_scr, p=p, hh=hh, n=nkb: E.dma_start(
                d.ap().rearrange("p (a f) -> p a f", a=n)[:, :, 0:64],
                s.ap()[:, p * 128 + hh * 64: p * 128 + hh * 64 + 64].rearrange(
                    "(a p) f -> p a f", p=128)))
        for e_ in ("t", "v", "s"):
            for k_ in ("q", "k", "v0", "v1"):
                b.wsem(e_, sl[k_])
        epis = []
        for qc in range(NTC):
            if mode == "self":
                kbs = list(range(0, 4 * qc + 4))
            else:
                kbs = list(range(0, min(nkb, 4 * qc + 5)))
            pend = []
            first = [True, True]

            def emit_sv(item):
                hh, kb, slot, scnt, vcnt = item
                b.wait("t", "s", scnt)
                if vcnt:
                    b.wait("t", "v", vcnt)
                if first[hh] and ("po", hh) in rel:
                    e, c = rel[("po", hh)]
                    b.wait("t", e, c)
                b.emit("t", lambda E, hh=hh, kb=kb, slot=slot,
                       fst=first[hh], lst=(kb == kbs[-1]): E.matmul(
                    PO[hh][:, :], vp[hh][:, kb * 65:kb * 65 + 65],
                    eb[hh][:, slot * 512:(slot + 1) * 512], start=fst, stop=lst))
                first[hh] = False
                rel[("eb", hh, slot)] = ("t", b.cnt["t"])

            for i, kb in enumerate(kbs):
                if i == 1 and epis:
                    epis.pop(0)()
                for hh in range(2):
                    hb = 64 * hh
                    i2 = hh * 2 + (i % 2)
                    pS = PS[i2]
                    slot = i % 4
                    jdiag = kb - 4 * qc
                    if ("ps", i2) in rel:
                        e, c = rel[("ps", i2)]
                        b.wait("t", e, c)
                    b.emit("t", lambda E, pS=pS, hb=hb, kb=kb, qc=qc: E.matmul(
                        pS[:, :], kTp[hb:hb + 64, kb * 128:kb * 128 + 128],
                        qTp[hb:hb + 64, qc * 512:qc * 512 + 512],
                        start=True, stop=True))
                    b.wait("s", "t", b.cnt["t"])
                    if ("eb", hh, slot) in rel:
                        e, c = rel[("eb", hh, slot)]
                        b.wait("s", e, c)
                    if scorer_col is None:
                        bias = ccol(b, -16.0)
                    else:
                        bias = scorer_col[:, kb:kb + 1]
                    b.emit("s", lambda E, pS=pS, hh=hh, slot=slot, bias=bias: E.activation(
                        eb[hh][:, slot * 512:(slot + 1) * 512], pS[:, :], AF.Exp,
                        bias=bias, scale=0.125), dep=4)
                    rel[("ps", i2)] = ("s", b.cnt["s"])
                    scnt = b.cnt["s"]
                    vcnt = 0
                    if jdiag >= 0:
                        b.wait("v", "s", scnt)
                        b.emit("v", lambda E, hh=hh, slot=slot, ms=masks_sb, j=jdiag: E.tensor_tensor(
                            eb[hh][:, slot * 512:(slot + 1) * 512],
                            eb[hh][:, slot * 512:(slot + 1) * 512],
                            ms[:, j * 512:(j + 1) * 512], ALU.mult), dep=4)
                        vcnt = b.cnt["v"]
                    pend.append((hh, kb, slot, scnt, vcnt))
                while len(pend) > 4:
                    emit_sv(pend.pop(0))
            for item in pend:
                emit_sv(item)
            svdone = b.cnt["t"]

            def mk_epi(qc=qc, svdone=svdone):
                def epi():
                    for hh in range(2):
                        po = PO[hh]
                        b.wait("v", "t", svdone)
                        b.emit("v", lambda E, po=po, hh=hh: E.reciprocal(
                            d_sb[64 * hh:64 * hh + 1, :], po[64:65, :]))
                        vdone = b.cnt["v"]
                        b.wait("t", "v", vdone)
                        if ("pb",) in rel:
                            e, c = rel[("pb",)]
                            b.wait("t", e, c)
                        b.emit("t", lambda E, hh=hh, of=b.cb["ones_f"]: E.matmul(
                            PB[:, :], of[64 * hh:64 * hh + 1, 0:64],
                            d_sb[64 * hh:64 * hh + 1, :], start=True, stop=True))
                        b.wait("s", "t", b.cnt["t"])
                        b.emit("s", lambda E, po=po, hh=hh: E.copy(on_sb[hh][0:64, :], po[0:64, :]))
                        b.wait("v", "s", b.cnt["s"])
                        b.wsem("v", sl["ob" + str(hh)])
                        b.emit("v", lambda E, hh=hh, g0=qc * 512: E.tensor_tensor(
                            obh[hh][0:64, g0:g0 + 512], on_sb[hh][0:64, :], PB[0:64, :], ALU.mult))
                        rel[("pb",)] = ("v", b.cnt["v"])
                        rel[("po", hh)] = ("v", b.cnt["v"])
                return epi
            epis.append(mk_epi())
        for ep in epis:
            ep()
        epis = []
        for hh in range(2):
            b.wait("y", "v", b.cnt["v"])
            b.dma("y", sl["ob" + str(hh)], lambda E, d=attT_scr, s=obh[hh], p=p, hh=hh: E.dma_start(
                d.ap()[(p * 2 + hh) * 64:(p * 2 + hh) * 64 + 64, :], s[:, :]))
        rel[("pair",)] = ("t", b.cnt["t"])
    b.barrier()
    b.sfree(list(sl.values()))
    st.close()


def emit_rope(b, src_scr, dst_scr, cos_sb, sin_sb, swpT_sb):
    st = ExitStack()
    qp = [b.sb(f"ro_q{i}", [128, T], BF16, st) for i in range(2)]
    stg = [b.sb(f"ro_s{i}", [128, T], BF16, st) for i in range(2)]
    tm1 = b.sb("ro_t1", [128, 512], F32, st)
    tm2 = b.sb("ro_t2", [128, 512], F32, st)
    PW = [b.ps(f"ro_p{i}", [128, 512], F32, st) for i in range(2)]
    sq = b.ssems(2)
    so = b.ssems(2)
    rel = {}
    for p in range(NH // 2):
        q, sqq = qp[p % 2], sq[p % 2]
        sg, soo = stg[p % 2], so[p % 2]
        if ("q", p % 2) in rel:
            e, c = rel[("q", p % 2)]
            b.wait("y", e, c)
        b.dma("y", sqq, lambda E, d=q, s=src_scr, p=p: E.dma_start(
            d[:, :], s.ap()[p * 128:(p + 1) * 128, :]))
        b.wsem("t", sqq)
        b.wsem("v", sqq)
        b.wsem("s", soo)
        b.wsem("v", soo)
        for qc in range(NTC):
            po = PW[qc % 2]
            if ("pw", qc % 2) in rel:
                e, c = rel[("pw", qc % 2)]
                b.wait("t", e, c)
            b.emit("t", lambda E, po=po, q=q, qc=qc: E.matmul(
                po[:, :], swpT_sb[:, :], q[:, qc * 512:qc * 512 + 512],
                start=True, stop=True))
            b.wait("v", "t", b.cnt["t"])
            b.emit("v", lambda E, q=q, qc=qc: E.tensor_tensor(
                tm1[:, :], q[:, qc * 512:qc * 512 + 512],
                cos_sb[:, qc * 512:qc * 512 + 512], ALU.mult))
            b.emit("v", lambda E, po=po, qc=qc: E.tensor_tensor(
                tm2[:, :], po[:, :], sin_sb[:, qc * 512:qc * 512 + 512], ALU.mult))
            rel[("pw", qc % 2)] = ("v", b.cnt["v"])
            b.emit("v", lambda E, sg=sg, qc=qc: E.tensor_tensor(
                sg[:, qc * 512:qc * 512 + 512], tm1[:, :], tm2[:, :], ALU.add))
        rel[("q", p % 2)] = ("v", b.cnt["v"])
        b.wait("y", "v", b.cnt["v"])
        b.dma("y", soo, lambda E, sg=sg, d=dst_scr, p=p: E.dma_start(
            d.ap()[p * 128:(p + 1) * 128, :], sg[:, :]))
    b.barrier()
    b.sfree(sq + so)
    st.close()


def emit_attn_s5(b, qT_scr, kT_scr, qrT_scr, krT_scr, vn_scr, attT_scr,
                 masks_sb, identf_sb):
    st = ExitStack()
    qTp = b.sb("s5_q", [128, T], BF16, st)
    kTp = b.sb("s5_k", [128, T], BF16, st)
    qrp = b.sb("s5_qr", [128, T], BF16, st)
    krp = b.sb("s5_kr", [128, T], BF16, st)
    vnp = [b.sb(f"s5_v{i}", [128, NT * 64], BF16, st) for i in range(2)]
    db = [b.sb(f"s5_db{i}", [128, 2 * 512], BF16, st) for i in range(2)]
    eb = [b.sb(f"s5_e{i}", [128, 4 * 512], BF16, st) for i in range(2)]
    dp = [[b.sb(f"s5_dp{i}{j}", [128, 8], F32, st) for j in range(2)] for i in range(2)]
    dc = [[b.sb(f"s5_dc{i}{j}", [128, 1], F32, st) for j in range(2)] for i in range(2)]
    denR = [b.sb(f"s5_dr{i}", [1, T], F32, st) for i in range(2)]
    on_sb = [b.sb(f"s5_on{i}", [128, 512], F32, st) for i in range(2)]
    obh = [b.sb(f"s5_ob{i}", [64, T], BF16, st) for i in range(2)]
    PS = [b.ps(f"s5_ps{i}", [128, 512], F32, st) for i in range(4)]
    PO = [b.ps(f"s5_po{i}", [65, 512], F32, st) for i in range(2)]
    PB = b.ps("s5_pb", [64, 512], F32, st)
    sl = {k: b.ssem() for k in ("q", "k", "qr", "kr", "v0", "v1", "ob0", "ob1")}
    rel = {}
    for p in range(NH // 2):
        if ("pair",) in rel:
            e, c = rel[("pair",)]
            b.wait("y", e, c)
        for key, buf, s_ in (("q", qTp, qT_scr), ("k", kTp, kT_scr),
                             ("qr", qrp, qrT_scr), ("kr", krp, krT_scr)):
            b.dma("y", sl[key], lambda E, d=buf, s=s_, p=p: E.dma_start(
                d[:, :], s.ap()[p * 128:(p + 1) * 128, :]))
        for hh in range(2):
            b.dma("y", sl["v" + str(hh)], lambda E, d=vnp[hh], s=vn_scr, p=p, hh=hh: E.dma_start(
                d.ap().rearrange("p (a f) -> p a f", a=NT),
                s.ap()[:, p * 128 + hh * 64: p * 128 + hh * 64 + 64].rearrange(
                    "(a p) f -> p a f", p=128)))
        for e_ in ("t", "v", "s"):
            for k_ in ("q", "k", "qr", "kr", "v0", "v1"):
                b.wsem(e_, sl[k_])
        b.wait("s", "t", b.cnt["t"])
        tails = []
        for qt in range(NT):
            qp = qt % 2
            for kc in range(NTC):
                for hh in range(2):
                    hb = 64 * hh
                    i2 = hh * 2 + (kc % 2)
                    ps = PS[i2]
                    slot = kc % 2
                    if ("ps", i2) in rel:
                        e, c = rel[("ps", i2)]
                        b.wait("t", e, c)
                    b.emit("t", lambda E, ps=ps, hb=hb, qt=qt, kc=kc: E.matmul(
                        ps[:, :], qTp[hb:hb + 64, qt * 128:qt * 128 + 128],
                        kTp[hb:hb + 64, kc * 512:kc * 512 + 512],
                        start=True, stop=True))
                    b.wait("s", "t", b.cnt["t"])
                    if ("db", hh, slot) in rel:
                        e, c = rel[("db", hh, slot)]
                        b.wait("s", e, c)
                    b.emit("s", lambda E, ps=ps, hh=hh, slot=slot: E.activation(
                        db[hh][:, slot * 512:(slot + 1) * 512], ps[:, :],
                        AF.Exp, scale=0.125), dep=4)
                    rel[("ps", i2)] = ("s", b.cnt["s"])
                    b.wait("v", "s", b.cnt["s"])
                    if ("dp", hh, qp) in rel:
                        e, c = rel[("dp", hh, qp)]
                        b.wait("v", e, c)
                    b.emit("v", lambda E, hh=hh, slot=slot, qp=qp, kc=kc: E.reduce_sum(
                        dp[hh][qp][:, kc:kc + 1],
                        db[hh][:, slot * 512:(slot + 1) * 512], AX.X), dep=4)
                    rel[("db", hh, slot)] = ("v", b.cnt["v"])
            def mk_tail(qt=qt, qp=qp):
                def tail():
                    for hh in range(2):
                        b.emit("v", lambda E, hh=hh, qp=qp: E.reduce_sum(
                            dc[hh][qp][:, 0:1], dp[hh][qp][:, :], AX.X))
                        b.emit("v", lambda E, hh=hh, qp=qp: E.reciprocal(
                            dc[hh][qp][:, 0:1], dc[hh][qp][:, 0:1]))
                        rel[("dp", hh, qp)] = ("v", b.cnt["v"])
                        vdone = b.cnt["v"]
                        b.wait("t", "v", vdone)
                        if ("ptb", hh, qp) in rel:
                            e, c = rel[("ptb", hh, qp)]
                            b.wait("t", e, c)
                        b.emit("t", lambda E, hh=hh, qp=qp: E.transpose(
                            PO[hh][0:1, 0:128], dc[hh][qp][:, 0:1], identf_sb[:, :]))
                        b.wait("s", "t", b.cnt["t"])
                        b.emit("s", lambda E, hh=hh, qt=qt: E.copy(
                            denR[hh][0:1, qt * 128:qt * 128 + 128], PO[hh][0:1, 0:128]))
                        rel[("ptb", hh, qp)] = ("s", b.cnt["s"])
                return tail
            tails.append(mk_tail())
            if len(tails) > 1:
                tails.pop(0)()
        tails.pop(0)()
        b.wait("t", "s", b.cnt["s"])
        epis = []
        for qc in range(NTC):
            kbs = list(range(0, 4 * qc + 4))
            pend = []
            first = [True, True]

            def emit_sv(item):
                hh, kb, slot, scnt, vcnt = item
                b.wait("t", "s", scnt)
                if vcnt:
                    b.wait("t", "v", vcnt)
                if first[hh] and ("po", hh) in rel:
                    e, c = rel[("po", hh)]
                    b.wait("t", e, c)
                b.emit("t", lambda E, hh=hh, kb=kb, slot=slot,
                       fst=first[hh], lst=(kb == kbs[-1]): E.matmul(
                    PO[hh][0:64, :], vnp[hh][:, kb * 64:kb * 64 + 64],
                    eb[hh][:, slot * 512:(slot + 1) * 512], start=fst, stop=lst))
                first[hh] = False
                rel[("eb", hh, slot)] = ("t", b.cnt["t"])

            for i, kb in enumerate(kbs):
                if i == 1 and epis:
                    epis.pop(0)()
                for hh in range(2):
                    hb = 64 * hh
                    i2 = hh * 2 + (i % 2)
                    pS = PS[i2]
                    slot = i % 4
                    jdiag = kb - 4 * qc
                    if ("ps", i2) in rel:
                        e, c = rel[("ps", i2)]
                        b.wait("t", e, c)
                    b.emit("t", lambda E, pS=pS, hb=hb, kb=kb, qc=qc: E.matmul(
                        pS[:, :], krp[hb:hb + 64, kb * 128:kb * 128 + 128],
                        qrp[hb:hb + 64, qc * 512:qc * 512 + 512],
                        start=True, stop=True))
                    b.wait("s", "t", b.cnt["t"])
                    if ("eb", hh, slot) in rel:
                        e, c = rel[("eb", hh, slot)]
                        b.wait("s", e, c)
                    b.emit("s", lambda E, pS=pS, hh=hh, slot=slot: E.activation(
                        eb[hh][:, slot * 512:(slot + 1) * 512], pS[:, :],
                        AF.Exp, scale=0.125), dep=4)
                    rel[("ps", i2)] = ("s", b.cnt["s"])
                    scnt = b.cnt["s"]
                    vcnt = 0
                    if jdiag >= 0:
                        b.wait("v", "s", scnt)
                        b.emit("v", lambda E, hh=hh, slot=slot, ms=masks_sb, j=jdiag: E.tensor_tensor(
                            eb[hh][:, slot * 512:(slot + 1) * 512],
                            eb[hh][:, slot * 512:(slot + 1) * 512],
                            ms[:, j * 512:(j + 1) * 512], ALU.mult), dep=4)
                        vcnt = b.cnt["v"]
                    pend.append((hh, kb, slot, scnt, vcnt))
                while len(pend) > 4:
                    emit_sv(pend.pop(0))
            for item in pend:
                emit_sv(item)
            svdone = b.cnt["t"]

            def mk_epi(qc=qc, svdone=svdone):
                def epi():
                    for hh in range(2):
                        po = PO[hh]
                        b.wait("t", "v", b.cnt["v"])
                        if ("pb",) in rel:
                            e, c = rel[("pb",)]
                            b.wait("t", e, c)
                        b.wait("t", "t", svdone) if False else None
                        b.emit("t", lambda E, hh=hh, qc=qc, of=b.cb["ones_f"]: E.matmul(
                            PB[:, :], of[0:1, 0:64], denR[hh][0:1, qc * 512:qc * 512 + 512],
                            start=True, stop=True))
                        b.wait("s", "t", b.cnt["t"])
                        b.emit("s", lambda E, po=po, hh=hh: E.copy(on_sb[hh][0:64, :], po[0:64, :]))
                        b.wait("v", "s", b.cnt["s"])
                        b.wsem("v", sl["ob" + str(hh)])
                        b.emit("v", lambda E, hh=hh, g0=qc * 512: E.tensor_tensor(
                            obh[hh][0:64, g0:g0 + 512], on_sb[hh][0:64, :], PB[0:64, :], ALU.mult))
                        rel[("pb",)] = ("v", b.cnt["v"])
                        rel[("po", hh)] = ("v", b.cnt["v"])
                return epi
            epis.append(mk_epi())
        for ep in epis:
            ep()
        for hh in range(2):
            b.wait("y", "v", b.cnt["v"])
            b.dma("y", sl["ob" + str(hh)], lambda E, d=attT_scr, s=obh[hh], p=p, hh=hh: E.dma_start(
                d.ap()[(p * 2 + hh) * 64:(p * 2 + hh) * 64 + 64, :], s[:, :]))
        rel[("pair",)] = ("t", b.cnt["t"])
    b.barrier()
    b.sfree(list(sl.values()))
    st.close()


def emit_attn_global(b, qT_scr, kn_scr, vn_scr, attT_scr, eps=1e-6):
    st = ExitStack()
    qTp = b.sb("ga_q", [128, T], BF16, st)
    knp = b.sb("ga_kn", [128, NT * 128], BF16, st)
    vp = [b.sb(f"ga_v{i}", [128, NT * 65], BF16, st) for i in range(2)]
    state_b = b.sb("ga_stb", [128, 65], BF16, st)
    on_sb = b.sb("ga_on", [128, 512], F32, st)
    d_sb = b.sb("ga_d", [128, 512], F32, st)
    obh = b.sb("ga_ob", [64, T], BF16, st)
    PO = b.ps("ga_po", [65, 512], F32, st)
    PK = b.ps("ga_pk", [128, 65], F32, st)
    PB = b.ps("ga_pb", [64, 512], F32, st)
    sl = {k: b.ssem() for k in ("q", "kn", "v0", "v1", "ob")}
    rel = {}
    for p in range(NH // 2):
        if ("pair",) in rel:
            e, c = rel[("pair",)]
            b.wait("y", e, c)
        b.dma("y", sl["q"], lambda E, d=qTp, s=qT_scr, p=p: E.dma_start(
            d[:, :], s.ap()[p * 128:(p + 1) * 128, :]))
        b.dma("y", sl["kn"], lambda E, d=knp, s=kn_scr, p=p: E.dma_start(
            d.ap().rearrange("p (a f) -> p a f", a=NT),
            s.ap()[:, p * 128:(p + 1) * 128].rearrange("(a p) f -> p a f", p=128)))
        for hh in range(2):
            vb = vp[hh]
            if ("pair",) in rel:
                e, c = rel[("pair",)]
                b.wait("v", e, c)
            b.emit("v", lambda E, d=vb: E.memset(
                d.ap().rearrange("p (a f) -> p a f", a=NT)[:, :, 64:65], 1.0))
            b.wait("y", "v", b.cnt["v"])
            b.dma("y", sl["v" + str(hh)], lambda E, d=vb, s=vn_scr, p=p, hh=hh: E.dma_start(
                d.ap().rearrange("p (a f) -> p a f", a=NT)[:, :, 0:64],
                s.ap()[:, p * 128 + hh * 64: p * 128 + hh * 64 + 64].rearrange(
                    "(a p) f -> p a f", p=128)))
        for e_ in ("t", "v", "s"):
            for k_ in ("q", "kn", "v0", "v1"):
                b.wsem(e_, sl[k_])
        for hh in range(2):
            hb = 64 * hh
            vb = vp[hh]
            b.wait("t", "v", b.cnt["v"])
            b.wait("t", "s", b.cnt["s"])
            for kb in range(NT):
                b.emit("t", lambda E, kb=kb, hb=hb, vb=vb: E.matmul(
                    PK[hb:hb + 64, :], knp[:, kb * 128 + hb: kb * 128 + hb + 64],
                    vb[:, kb * 65:kb * 65 + 65], start=(kb == 0), stop=(kb == NT - 1)))
            b.wait("s", "t", b.cnt["t"])
            b.emit("s", lambda E, hb=hb: E.copy(state_b[hb:hb + 64, :], PK[hb:hb + 64, :]))
            b.wait("t", "s", b.cnt["s"])
            for qc in range(NTC):
                g0 = qc * 512
                b.emit("t", lambda E, hb=hb, g0=g0: E.matmul(
                    PO[:, :], state_b[hb:hb + 64, :], qTp[hb:hb + 64, g0:g0 + 512],
                    start=True, stop=True))
                b.wait("v", "t", b.cnt["t"])
                b.emit("v", lambda E, eps=eps: E.tensor_scalar(
                    d_sb[64:65, :], PO[64:65, :], float(eps), None, ALU.add))
                b.emit("v", lambda E: E.reciprocal(d_sb[64:65, :], d_sb[64:65, :]))
                b.wait("t", "v", b.cnt["v"])
                b.emit("t", lambda E, of=b.cb["ones_f"]: E.matmul(
                    PB[:, :], of[64:65, 0:64], d_sb[64:65, :], start=True, stop=True))
                b.wait("s", "t", b.cnt["t"])
                b.emit("s", lambda E: E.copy(on_sb[0:64, :], PO[0:64, :]))
                b.wait("v", "s", b.cnt["s"])
                b.wsem("v", sl["ob"])
                b.emit("v", lambda E, g0=g0: E.tensor_tensor(
                    obh[0:64, g0:g0 + 512], on_sb[0:64, :], PB[0:64, :], ALU.mult))
                b.wait("t", "v", b.cnt["v"])
            b.wait("y", "v", b.cnt["v"])
            b.dma("y", sl["ob"], lambda E, d=attT_scr, s=obh, p=p, hh=hh: E.dma_start(
                d.ap()[(p * 2 + hh) * 64:(p * 2 + hh) * 64 + 64, :], s[:, :]))
        rel[("pair",)] = ("t", b.cnt["t"])
    b.barrier()
    b.sfree(list(sl.values()))
    st.close()


def emit_moe(b, xT_scr, io, dst, res_src, identf_sb, rs_col):
    TB = 256
    NB = T // TB
    st = ExitStack()
    w1 = b.sb("mo_w1", [128, 8 * 4096], BF16, st)
    w2 = b.sb("mo_w2", [128, 32 * 1024], BF16, st)
    ws1 = b.sb("mo_s1", [128, 8 * 512], BF16, st)
    ws2 = b.sb("mo_s2", [128, 4 * 1024], BF16, st)
    gw = b.sb("mo_gw", [128, 8 * 8], BF16, st)
    b1c = b.sb("mo_b1", [128, 32], F32, st)
    bs1c = b.sb("mo_bs1", [128, 4], F32, st)
    b2b = b.sb("mo_b2", [8, 1024], BF16, st)
    bs2r = b.sb("mo_bs2", [1, 1024], BF16, st)
    hT = b.sb("mo_h", [128, 32 * TB], BF16, st)
    hsT = b.sb("mo_hs", [128, 4 * TB], BF16, st)
    xTb = [b.sb(f"mo_x{i}", [128, 8 * TB], BF16, st) for i in range(2)]
    sg_sb = b.sb("mo_sg", [128, 8], F32, st)
    gsc = b.sb("mo_gs", [128, 16], F32, st)
    gts = [b.sb(f"mo_gt{i}", [8, TB], F32, st) for i in range(2)]
    gtb = [b.sb(f"mo_gtb{i}", [8, TB], BF16, st) for i in range(2)]
    grow = [b.sb(f"mo_gr{i}", [1, 8 * TB], BF16, st) for i in range(2)]
    res = b.sb("mo_r", [128, C], F32, st)
    outb = b.sb("mo_o", [128, C], F32, st)
    pg = b.ps("mo_pg", [128, 512], F32, st)
    pgt = b.ps("mo_pgt", [8, 128], F32, st)
    ph = [b.ps(f"mo_ph{i}", [128, TB], F32, st) for i in range(2)]
    pb_ = [b.ps(f"mo_pb{i}", [128, TB], F32, st) for i in range(2)]
    pout = [b.ps(f"mo_po{i}", [128, 512], F32, st) for i in range(2)]
    sw = b.ssem()
    sx = b.ssems(2)
    sr = b.ssems(2)
    sd = b.ssems(2)
    sgr = b.ssems(2)
    b.dma("g", sw, _w_cast_load(w1, io["L1_ffn_W1cat"], 4096))
    b.dma("g", sw, _w_cast_load(w2, io["L1_ffn_W2cat"], 1024))
    b.dma("g", sw, _w_cast_load(ws1, io["L1_ffn_Ws1"], 512))
    b.dma("g", sw, _w_cast_load(ws2, io["L1_ffn_Ws2"], 1024))
    b.dma("g", sw, _w_cast_load(gw, io["L1_ffn_gate_W"], 8))
    b.dma("y", sw, lambda E, d=b1c, s=io["L1_ffn_b1cols"]: E.dma_start(d[:, :], s.ap()))
    b.dma("y", sw, lambda E, d=bs1c, s=io["L1_ffn_bs1cols"]: E.dma_start(d[:, :], s.ap()))
    b.dma("g", sw, lambda E, d=b2b, s=io["L1_ffn_b2"]: E.dma_start(d[:, :], s.ap()))
    b.dma("g", sw, lambda E, d=bs2r, s=io["L1_ffn_bs2"]: E.dma_start(
        d[0:1, :], row_ap(s, 0, C)))
    b.barrier()
    rel = {}
    ntl = TB // 128

    def load_x(tb):
        sxx = sx[tb % 2]
        if ("x", tb % 2) in rel:
            e, c = rel[("x", tb % 2)]
            b.wait("y", e, c)
        b.dma("y", sxx, lambda E, d=xTb[tb % 2], s=xT_scr, tb=tb: E.dma_start(
            d.ap().rearrange("p (a f) -> p a f", a=8),
            s.ap()[:, tb * TB:(tb + 1) * TB].rearrange("(a p) f -> p a f", p=128)))
        b.wsem("t", sxx)

    def emit_gates(tb):
        pr = tb % 2
        xb = xTb[pr]
        for tl in range(ntl):
            if ("gt",) in rel:
                e, c = rel[("gt",)]
                b.wait("t", e, c)
            b.wait("t", "v", b.cnt["v"])
            for a in range(8):
                b.emit("t", lambda E, xb=xb, a=a, tl=tl: E.matmul(
                    pg[:, 0:8], xb[:, a * TB + tl * 128: a * TB + tl * 128 + 128],
                    gw[:, a * 8:(a + 1) * 8], start=(a == 0), stop=(a == 7)))
            b.wait("s", "t", b.cnt["t"])
            b.wait("s", "v", b.cnt["v"])
            b.emit("s", lambda E: E.activation(sg_sb[:, :], pg[:, 0:8], AF.Sigmoid))
            b.wait("v", "s", b.cnt["s"])
            b.emit("v", lambda E, rs=rs_col: E.tensor_scalar(
                sg_sb[:, :], sg_sb[:, :], rs[:, 0:1], None, ALU.mult))
            b.emit("v", lambda E: E.reduce_max(gsc[:, 0:1], sg_sb[:, :], AX.X))
            b.emit("v", lambda E: E.tensor_scalar(
                gsc[:, 2:10], sg_sb[:, :], gsc[:, 0:1], None, ALU.is_ge))
            b.emit("v", lambda E: E.scalar_tensor_tensor(
                gsc[:, 2:10], gsc[:, 2:10], -1e30, sg_sb[:, :], ALU.mult, ALU.add))
            b.emit("v", lambda E: E.reduce_max(gsc[:, 1:2], gsc[:, 2:10], AX.X))
            b.emit("v", lambda E: E.tensor_tensor(
                gsc[:, 10:11], gsc[:, 0:1], gsc[:, 1:2], ALU.add))
            b.emit("v", lambda E: E.reciprocal(gsc[:, 10:11], gsc[:, 10:11]))
            b.emit("v", lambda E: E.tensor_scalar(
                gsc[:, 2:10], sg_sb[:, :], gsc[:, 1:2], None, ALU.is_ge))
            b.emit("v", lambda E: E.tensor_tensor(
                gsc[:, 2:10], gsc[:, 2:10], sg_sb[:, :], ALU.mult))
            b.emit("v", lambda E: E.tensor_scalar(
                sg_sb[:, :], gsc[:, 2:10], gsc[:, 10:11], None, ALU.mult))
            b.wait("t", "v", b.cnt["v"])
            b.emit("t", lambda E: E.transpose(pgt[:, :], sg_sb[:, :], identf_sb[:, :]))
            b.wait("s", "t", b.cnt["t"])
            b.emit("s", lambda E, pr=pr, tl=tl: E.copy(
                gts[pr][0:8, tl * 128:(tl + 1) * 128], pgt[0:8, :]))
            rel[("gt",)] = ("s", b.cnt["s"])
        b.wait("v", "s", b.cnt["s"])
        b.wsem("v", sgr[pr])
        b.emit("v", lambda E, pr=pr: E.tensor_copy(gtb[pr][:, :], gts[pr][:, :]))
        b.wait("y", "v", b.cnt["v"])
        b.dma("y", sgr[pr], lambda E, pr=pr: E.dma_start(grow[pr][0:1, :], gtb[pr][0:8, :]))

    load_x(0)
    emit_gates(0)
    for tb in range(NB):
        pr = tb % 2
        xb = xTb[pr]
        b.wsem("t", sgr[pr])
        b.wsem("t", sw)
        if tb + 1 < NB:
            load_x(tb + 1)
        hpend = []
        for ft in range(32):
            po = ph[ft % 2]
            if ("ph", ft % 2) in rel:
                e, c = rel[("ph", ft % 2)]
                b.wait("t", e, c)
            for a in range(8):
                b.emit("t", lambda E, po=po, xb=xb, a=a, ft=ft: E.matmul(
                    po[:, :], w1[:, a * 4096 + ft * 128: a * 4096 + ft * 128 + 128],
                    xb[:, a * TB:(a + 1) * TB], start=(a == 0), stop=(a == 7)))
            b.wait("s", "t", b.cnt["t"])
            b.emit("s", lambda E, po=po, ft=ft: E.activation(
                hT[:, ft * TB:(ft + 1) * TB], po[:, :], AF.Gelu,
                bias=b1c[:, ft:ft + 1]), dep=2)
            hpend.append((ft, b.cnt["s"]))
            rel[("ph", ft % 2)] = ("s", b.cnt["s"])
            while len(hpend) > 2:
                f2, scnt = hpend.pop(0)
                pbx = pb_[f2 % 2]
                if ("pbx", f2 % 2) in rel:
                    e, c = rel[("pbx", f2 % 2)]
                    b.wait("t", e, c)
                b.emit("t", lambda E, pbx=pbx, pr=pr, ex=f2 // 4, ob=b.cb["ones_b"]: E.matmul(
                    pbx[:, :], ob[0:1, 0:128], grow[pr][0:1, ex * TB:(ex + 1) * TB],
                    start=True, stop=True))
                b.wait("v", "t", b.cnt["t"])
                b.wait("v", "s", scnt)
                b.emit("v", lambda E, f2=f2, pbx=pbx: E.tensor_tensor(
                    hT[:, f2 * TB:(f2 + 1) * TB], hT[:, f2 * TB:(f2 + 1) * TB],
                    pbx[:, :], ALU.mult), dep=2)
                rel[("pbx", f2 % 2)] = ("v", b.cnt["v"])
                rel[("hT",)] = ("v", b.cnt["v"])
        for f2, scnt in hpend:
            pbx = pb_[f2 % 2]
            if ("pbx", f2 % 2) in rel:
                e, c = rel[("pbx", f2 % 2)]
                b.wait("t", e, c)
            b.emit("t", lambda E, pbx=pbx, pr=pr, ex=f2 // 4, ob=b.cb["ones_b"]: E.matmul(
                pbx[:, :], ob[0:1, 0:128], grow[pr][0:1, ex * TB:(ex + 1) * TB],
                start=True, stop=True))
            b.wait("v", "t", b.cnt["t"])
            b.wait("v", "s", scnt)
            b.emit("v", lambda E, f2=f2, pbx=pbx: E.tensor_tensor(
                hT[:, f2 * TB:(f2 + 1) * TB], hT[:, f2 * TB:(f2 + 1) * TB],
                pbx[:, :], ALU.mult), dep=2)
            rel[("pbx", f2 % 2)] = ("v", b.cnt["v"])
            rel[("hT",)] = ("v", b.cnt["v"])
        for fs in range(4):
            po = ph[fs % 2]
            e, c = rel[("ph", fs % 2)]
            b.wait("t", e, c)
            for a in range(8):
                b.emit("t", lambda E, po=po, xb=xb, a=a, fs=fs: E.matmul(
                    po[:, :], ws1[:, a * 512 + fs * 128: a * 512 + fs * 128 + 128],
                    xb[:, a * TB:(a + 1) * TB], start=(a == 0), stop=(a == 7)))
            b.wait("s", "t", b.cnt["t"])
            b.emit("s", lambda E, po=po, fs=fs: E.activation(
                hsT[:, fs * TB:(fs + 1) * TB], po[:, :], AF.Gelu,
                bias=bs1c[:, fs:fs + 1]), dep=2)
            rel[("ph", fs % 2)] = ("s", b.cnt["s"])
            rel[("hsT",)] = ("s", b.cnt["s"])
        if tb + 1 < NB:
            emit_gates(tb + 1)
        e, c = rel[("hT",)]
        b.wait("t", e, c)
        e, c = rel[("hsT",)]
        b.wait("t", e, c)
        for tl in range(ntl):
            tt = tb * ntl + tl
            srr = sr[tt % 2]
            sdd = sd[tt % 2]
            if ("rs",) in rel:
                e, c = rel[("rs",)]
                b.wait("y", e, c)
            b.dma("y", srr, lambda E, d=res, s=res_src, tt=tt: E.dma_start(
                d[:, :], s.ap()[tt * 128:(tt + 1) * 128, :]))
            for fh in range(2):
                po = pout[fh]
                if ("pout", fh) in rel:
                    e, c = rel[("pout", fh)]
                    b.wait("t", e, c)
                for hc in range(32):
                    b.emit("t", lambda E, po=po, hc=hc, tl=tl, fh=fh: E.matmul(
                        po[:, :], hT[:, hc * TB + tl * 128: hc * TB + tl * 128 + 128],
                        w2[:, hc * 1024 + fh * 512: hc * 1024 + fh * 512 + 512],
                        start=(hc == 0), stop=False))
                for sc_ in range(4):
                    b.emit("t", lambda E, po=po, sc_=sc_, tl=tl, fh=fh: E.matmul(
                        po[:, :], hsT[:, sc_ * TB + tl * 128: sc_ * TB + tl * 128 + 128],
                        ws2[:, sc_ * 1024 + fh * 512: sc_ * 1024 + fh * 512 + 512],
                        start=False, stop=False))
                b.emit("t", lambda E, po=po, pr=pr, tl=tl, fh=fh: E.matmul(
                    po[:, :], gtb[pr][0:8, tl * 128:(tl + 1) * 128],
                    b2b[0:8, fh * 512:(fh + 1) * 512], start=False, stop=False))
                b.emit("t", lambda E, po=po, fh=fh, ob=b.cb["ones_b"]: E.matmul(
                    po[:, :], ob[0:1, 0:128], bs2r[0:1, fh * 512:(fh + 1) * 512],
                    start=False, stop=True))
                b.wait("v", "t", b.cnt["t"])
                b.wsem("v", srr)
                b.wsem("v", sdd)
                b.emit("v", lambda E, po=po, fh=fh: E.tensor_tensor(
                    outb[:, fh * 512:(fh + 1) * 512], res[:, fh * 512:(fh + 1) * 512],
                    po[:, :], ALU.add))
                rel[("pout", fh)] = ("v", b.cnt["v"])
            rel[("rs",)] = ("v", b.cnt["v"])
            b.wait("y", "v", b.cnt["v"])
            b.dma("y", sdd, lambda E, d=dst, tt=tt: E.dma_start(
                d.ap()[tt * 128:(tt + 1) * 128, :], outb[:, :]))
            b.wsem("v", sdd)
        rel[("x", pr)] = ("t", b.cnt["t"])
    b.barrier()
    b.sfree([sw] + sgr + sx + sr + sd)
    st.close()



def emit_s1(b, io, scr, src, dst):
    st = ExitStack()
    xT = b.sb("xT1", [128, 8 * T], BF16, st)
    emit_ln_t(b, src, xT)
    emit_proj_T(b, xT, io["L0_attn_a_wqkv"], 0, C, scr["qT"], act=AF.Relu)
    emit_proj_T(b, xT, io["L0_attn_a_wqkv"], C, C, scr["kT"], act=AF.Relu)
    emit_proj_N(b, xT, io["L0_attn_a_wqkv"], C, C, scr["kn"], act=AF.Relu)
    emit_proj_N(b, xT, io["L0_attn_a_wqkv"], 2 * C, C, scr["vn"], act=AF.Copy)
    st.close()
    st2 = ExitStack()
    masks = b.sb("m_le", [128, 4 * 512], BF16, st2)
    ss = b.ssem()
    b.dma("y", ss, lambda E, d=masks, s=io["masks_le"]: E.dma_start(d[:, :], s.ap()))
    b.barrier()
    b.sfree([ss])
    emit_lin_attn_causal(b, scr["qT"], scr["kT"], scr["kn"], scr["vn"], masks, scr["aT"])
    st2.close()
    emit_out_proj(b, scr["aT"], io["L0_attn_a_wo"], C, src, dst)


S2_WEFF_MARK = {}


def emit_s2(b, io, scr, src, dst, first):
    if first:
        S2_WEFF_MARK["fc"] = emit_weff(b, io["L0_ffn_w_fc"], 4096, 1024,
                                       scr["weff_fc"], barrier=False)
        S2_WEFF_MARK["pj"] = emit_weff(b, io["L0_ffn_w_proj"], 1024, 4096,
                                       scr["weff_proj"], barrier=False)
    gate_col = emit_bcast_scalar(b, io["L0_ffn_gate"])
    st = ExitStack()
    xT = b.sb("xT2", [128, 8 * T], BF16, st)
    emit_ln_t(b, src, xT, mirror_scr=scr["xTs"])
    st.close()
    emit_ffn(b, scr["xTs"], _ld_transposed(scr["weff_fc"], 4096, 8),
             _ld_transposed(scr["weff_proj"], 1024, 32), dst, src,
             "relu2", hwidth=4096,
             lowrank=dict(vT=io["L0_ffn_vT"], fbT=io["L0_ffn_fbT"],
                          uT=io["L0_ffn_uT"], gate_col=gate_col))


def emit_s3(b, io, scr, src, dst, first):
    if first:
        st = ExitStack()
        encT = b.sb("encT", [128, 8 * 2048], BF16, st)
        ss = b.ssem()
        b.dma("g", ss, _w_cast_load(encT, io["L0_attn_b_encT"], 2048))
        kcols = b.sb("xkvbk", [128, 8], F32, st)
        b.dma("y", ss, lambda E, d=kcols, s=io["L0_attn_b_x_kv_b_kcols"]: E.dma_start(d[:, :], s.ap()))
        b.barrier()
        b.sfree([ss])
        emit_proj_T(b, encT, io["L0_attn_b_x_kv_w"], 0, C, scr["ekT"],
                    bias_col=kcols, ntok=2048)
        emit_proj_N(b, encT, io["L0_attn_b_x_kv_w"], C, C, scr["ev"],
                    bias_row=io["L0_attn_b_x_kv_b"], bias_off=1024, ntok=2048)
        st.close()
    st = ExitStack()
    xT = b.sb("xT3", [128, 8 * T], BF16, st)
    emit_ln_t(b, src, xT, norm_f32_dst=scr["xh1"])
    stg = ExitStack()
    g1 = emit_bcast_row(b, io["L0_attn_b_ln1_g"], C, dst=b.sb("g1", [128, C], F32, stg))
    b1 = emit_bcast_row(b, io["L0_attn_b_ln1_b"], C, dst=b.sb("b1", [128, C], F32, stg))
    emit_ln_t(b, scr["xh1"], xT, g_bc=g1, b_bc=b1)
    stg.close()
    stg = ExitStack()
    cabq = b.sb("cabq", [128, 8], F32, stg)
    cabk = b.sb("cabk", [128, 8], F32, stg)
    ss = b.ssem()
    b.dma("y", ss, lambda E, d=cabq, s=io["L0_attn_b_c_attn_b_qcols"]: E.dma_start(d[:, :], s.ap()))
    b.dma("y", ss, lambda E, d=cabk, s=io["L0_attn_b_c_attn_b_kcols"]: E.dma_start(d[:, :], s.ap()))
    b.barrier()
    b.sfree([ss])
    emit_proj_T(b, xT, io["L0_attn_b_c_attn_w"], 0, C, scr["qT"], bias_col=cabq)
    emit_proj_T(b, xT, io["L0_attn_b_c_attn_w"], C, C, scr["kT"], bias_col=cabk)
    emit_proj_N(b, xT, io["L0_attn_b_c_attn_w"], 2 * C, C, scr["vn"],
                bias_row=io["L0_attn_b_c_attn_b"], bias_off=2048)
    stg.close()
    stg = ExitStack()
    masks = b.sb("m_le3", [128, 4 * 512], BF16, stg)
    ss = b.ssem()
    b.dma("y", ss, lambda E, d=masks, s=io["masks_le"]: E.dma_start(d[:, :], s.ap()))
    b.barrier()
    b.sfree([ss])
    emit_attn_softmax(b, scr["qT"], scr["kT"], scr["vn"], scr["aT"], T, "self", masks)
    stg.close()
    emit_out_proj(b, scr["aT"], io["L0_attn_b_attn_proj_w"], C, scr["xh1"],
                  scr["xh2"], bias_row=io["L0_attn_b_attn_proj_b"])
    stg = ExitStack()
    g2 = emit_bcast_row(b, io["L0_attn_b_lnx_g"], C, dst=b.sb("g2", [128, C], F32, stg))
    b2 = emit_bcast_row(b, io["L0_attn_b_lnx_b"], C, dst=b.sb("b2", [128, C], F32, stg))
    emit_ln_t(b, scr["xh2"], xT, g_bc=g2, b_bc=b2)
    stg.close()
    stg = ExitStack()
    qab = b.sb("qab", [128, 8], F32, stg)
    ss = b.ssem()
    b.dma("y", ss, lambda E, d=qab, s=io["L0_attn_b_q_attn_b_cols"]: E.dma_start(d[:, :], s.ap()))
    b.barrier()
    b.sfree([ss])
    emit_proj_T(b, xT, io["L0_attn_b_q_attn_w"], 0, C, scr["qT"], bias_col=qab)
    stg.close()
    stg = ExitStack()
    masks2 = b.sb("m_le2", [128, 5 * 512], BF16, stg)
    scor = b.sb("scor", [128, 16], F32, stg)
    ss = b.ssem()
    b.dma("y", ss, lambda E, d=masks2, s=io["masks_le2"]: E.dma_start(d[:, :], s.ap()))
    b.dma("y", ss, lambda E, d=scor, s=io["L0_attn_b_scorer_cols"]: E.dma_start(d[:, :], s.ap()))
    b.barrier()
    b.wsem("v", ss)
    b.emit("v", lambda E, d=scor: E.tensor_scalar(d[:, :], d[:, :], -16.0, None, ALU.add))
    b.barrier()
    b.sfree([ss])
    emit_attn_softmax(b, scr["qT"], scr["ekT"], scr["ev"], scr["aT"], 2048,
                      "cross", masks2, scorer_col=scor)
    stg.close()
    emit_out_proj(b, scr["aT"], io["L0_attn_b_x_proj_w"], C, scr["xh2"],
                  scr["xh1"], bias_row=io["L0_attn_b_x_proj_b"])
    stg = ExitStack()
    g3 = emit_bcast_row(b, io["L0_attn_b_ln2_g"], C, dst=b.sb("g3", [128, C], F32, stg))
    b3 = emit_bcast_row(b, io["L0_attn_b_ln2_b"], C, dst=b.sb("b3", [128, C], F32, stg))
    emit_ln_t(b, scr["xh1"], xT, g_bc=g3, b_bc=b3, mirror_scr=scr["xTs"])
    stg.close()
    st.close()
    stg = ExitStack()
    fcb = b.sb("fcb", [128, 32], F32, stg)
    ss = b.ssem()
    b.dma("y", ss, lambda E, d=fcb, s=io["L0_attn_b_fc_b_cols"]: E.dma_start(d[:, :], s.ap()))
    b.barrier()
    b.sfree([ss])
    emit_ffn(b, scr["xTs"], _ld_natural(io["L0_attn_b_fc_w"], 4096),
             _ld_natural(io["L0_attn_b_mlp_proj_w"], 1024), dst, scr["xh1"],
             "gelu_tanh", hwidth=4096, h_bias_col=fcb,
             out_bias_row=io["L0_attn_b_mlp_proj_b"], extra_res=src)
    stg.close()


def emit_s5(b, io, scr, src, dst):
    st = ExitStack()
    xT = b.sb("xT5", [128, 8 * T], BF16, st)
    emit_ln_t(b, src, xT)
    emit_proj_T(b, xT, io["L1_attn_a_Wqkv"], 0, C, scr["qT"])
    emit_proj_T(b, xT, io["L1_attn_a_Wqkv"], C, C, scr["kT"])
    emit_proj_N(b, xT, io["L1_attn_a_Wqkv"], 2 * C, C, scr["vn"])
    st.close()
    st = ExitStack()
    cosr = b.sb("cosr", [128, T], F32, st)
    sinr = b.sb("sinr", [128, T], F32, st)
    swpT = b.sb("swpT", [128, 128], BF16, st)
    ss = b.ssem()
    b.dma("y", ss, lambda E, d=cosr, s=io["rope_cos"]: E.dma_start(d[:, :], s.ap()))
    b.dma("y", ss, lambda E, d=sinr, s=io["rope_sin"]: E.dma_start(d[:, :], s.ap()))
    b.dma("y", ss, lambda E, d=swpT, s=io["rope_swpT"]: E.dma_start(d[:, :], s.ap()))
    b.barrier()
    b.sfree([ss])
    emit_rope(b, scr["qT"], scr["qrT"], cosr, sinr, swpT)
    emit_rope(b, scr["kT"], scr["krT"], cosr, sinr, swpT)
    st.close()
    st = ExitStack()
    masks = b.sb("m_le5", [128, 4 * 512], BF16, st)
    identf = b.sb("idf5", [128, 128], F32, st)
    ss = b.ssem()
    b.dma("y", ss, lambda E, d=masks, s=io["masks_le"]: E.dma_start(d[:, :], s.ap()))
    b.dma("y", ss, lambda E, d=identf, s=io["identf"]: E.dma_start(d[:, :], s.ap()))
    b.barrier()
    b.sfree([ss])
    emit_attn_s5(b, scr["qT"], scr["kT"], scr["qrT"], scr["krT"], scr["vn"],
                 scr["aT"], masks, identf)
    st.close()
    emit_out_proj(b, scr["aT"], io["L1_attn_a_Wout"], C, src, dst,
                  bias_row=io["L1_attn_a_bout"])


def emit_s6(b, io, scr, src, dst):
    rs_col = emit_bcast_scalar(b, io["L1_ffn_route_scale"])
    st = ExitStack()
    xT = b.sb("xT6", [128, 8 * T], BF16, st)
    emit_ln_t(b, src, xT, mirror_scr=scr["xTs"])
    st.close()
    st = ExitStack()
    identf = b.sb("idf6", [128, 128], F32, st)
    ss = b.ssem()
    b.dma("y", ss, lambda E, d=identf, s=io["identf"]: E.dma_start(d[:, :], s.ap()))
    b.barrier()
    b.sfree([ss])
    emit_moe(b, scr["xTs"], io, dst, src, identf, rs_col)
    st.close()


def emit_s7(b, io, scr, src, dst):
    st = ExitStack()
    gam = emit_bcast_row(b, io["L1_attn_b_gamma"], C, dst=b.sb("gam", [128, C], F32, st))
    sc1 = emit_bcast_row(b, io["L1_attn_b_scale1"], C, dst=b.sb("sc1", [128, C], F32, st))
    sc2 = emit_bcast_row(b, io["L1_attn_b_scale2"], C, dst=b.sb("sc2", [128, C], F32, st))
    stx = ExitStack()
    xT = b.sb("xT7", [128, 8 * T], BF16, stx)
    emit_ln_t(b, src, xT, norm_f32_dst=scr["xh1"])
    emit_ln_t(b, scr["xh1"], xT, rms_scale_bc=sc1)
    emit_proj_T(b, xT, io["L1_attn_b_qkv_wT"], 0, C, scr["qT"],
                act=AF.Square, act2=(AF.Exp, -0.5))
    emit_proj_N(b, xT, io["L1_attn_b_qkv_wT"], C, C, scr["kn"],
                act=AF.Square, act2=(AF.Exp, -0.5))
    emit_proj_N(b, xT, io["L1_attn_b_qkv_wT"], 2 * C, C, scr["vn"])
    emit_attn_global(b, scr["qT"], scr["kn"], scr["vn"], scr["aT"])
    emit_out_proj(b, scr["aT"], io["L1_attn_b_proj_wT"], C, scr["xh1"],
                  scr["xh2"], bias_row=io["L1_attn_b_proj_b"], gamma_bc=gam)
    emit_ln_t(b, scr["xh2"], xT, rms_scale_bc=sc2, mirror_scr=scr["xTs"])
    stx.close()
    stg = ExitStack()
    c1b = b.sb("c1b", [128, 16], F32, stg)
    ss = b.ssem()
    b.dma("y", ss, lambda E, d=c1b, s=io["L1_attn_b_conv1_b_cols"]: E.dma_start(d[:, :], s.ap()))
    b.barrier()
    b.sfree([ss])
    emit_ffn(b, scr["xTs"], _ld_natural(io["L1_attn_b_conv1_wT"], 2048),
             _ld_natural(io["L1_attn_b_conv2_wT"], 1024), dst, scr["xh2"],
             "gelu", hwidth=2048, h_bias_col=c1b,
             out_bias_row=io["L1_attn_b_conv2_b"], extra_res=src, gamma_bc=gam)
    stg.close()
    st.close()


SUB_INPUTS = {
    "s1": ["L0_attn_a_wqkv", "L0_attn_a_wo"],
    "s2": ["L0_ffn_w_fc", "L0_ffn_w_proj", "L0_ffn_vT", "L0_ffn_fbT",
           "L0_ffn_uT", "L0_ffn_gate"],
    "s3": ["L0_attn_b_encT", "L0_attn_b_scorer_cols", "L0_attn_b_ln1_g",
           "L0_attn_b_ln1_b", "L0_attn_b_c_attn_w", "L0_attn_b_c_attn_b",
           "L0_attn_b_c_attn_b_qcols", "L0_attn_b_c_attn_b_kcols",
           "L0_attn_b_attn_proj_w", "L0_attn_b_attn_proj_b",
           "L0_attn_b_lnx_g", "L0_attn_b_lnx_b", "L0_attn_b_q_attn_w",
           "L0_attn_b_q_attn_b_cols", "L0_attn_b_x_kv_w", "L0_attn_b_x_kv_b",
           "L0_attn_b_x_kv_b_kcols", "L0_attn_b_x_proj_w", "L0_attn_b_x_proj_b",
           "L0_attn_b_ln2_g", "L0_attn_b_ln2_b", "L0_attn_b_fc_w",
           "L0_attn_b_fc_b_cols", "L0_attn_b_mlp_proj_w", "L0_attn_b_mlp_proj_b"],
    "s5": ["L1_attn_a_Wqkv", "L1_attn_a_Wout", "L1_attn_a_bout",
           "rope_cos", "rope_sin", "rope_swpT", "identf"],
    "s6": ["L1_ffn_gate_W", "L1_ffn_W1cat", "L1_ffn_b1cols", "L1_ffn_W2cat",
           "L1_ffn_b2", "L1_ffn_Ws1", "L1_ffn_bs1cols", "L1_ffn_Ws2",
           "L1_ffn_bs2", "L1_ffn_route_scale", "identf"],
    "s7": ["L1_attn_b_scale1", "L1_attn_b_scale2", "L1_attn_b_gamma",
           "L1_attn_b_qkv_wT", "L1_attn_b_proj_wT", "L1_attn_b_proj_b",
           "L1_attn_b_conv1_wT", "L1_attn_b_conv1_b_cols", "L1_attn_b_conv2_wT",
           "L1_attn_b_conv2_b"],
}
SUB_INPUTS["s4"] = SUB_INPUTS["s2"]
SUB_INPUTS["s8"] = SUB_INPUTS["s6"]

INPUT_SHAPES = {
    "L0_attn_a_wqkv": [C, 3 * C], "L0_attn_a_wo": [C, C],
    "L0_ffn_w_fc": [4096, C], "L0_ffn_w_proj": [C, 4096],
    "L0_ffn_vT": [C, 16], "L0_ffn_fbT": [16, 16], "L0_ffn_uT": [16, 4096],
    "L0_ffn_gate": [1, 1],
    "L0_attn_b_encT": [C, 2048], "L0_attn_b_scorer_cols": [128, 16],
    "L0_attn_b_ln1_g": [C], "L0_attn_b_ln1_b": [C],
    "L0_attn_b_c_attn_w": [C, 3 * C], "L0_attn_b_c_attn_b": [3 * C],
    "L0_attn_b_c_attn_b_qcols": [128, 8], "L0_attn_b_c_attn_b_kcols": [128, 8],
    "L0_attn_b_attn_proj_w": [C, C], "L0_attn_b_attn_proj_b": [C],
    "L0_attn_b_lnx_g": [C], "L0_attn_b_lnx_b": [C],
    "L0_attn_b_q_attn_w": [C, C], "L0_attn_b_q_attn_b_cols": [128, 8],
    "L0_attn_b_x_kv_w": [C, 2 * C], "L0_attn_b_x_kv_b": [2 * C],
    "L0_attn_b_x_kv_b_kcols": [128, 8],
    "L0_attn_b_x_proj_w": [C, C], "L0_attn_b_x_proj_b": [C],
    "L0_attn_b_ln2_g": [C], "L0_attn_b_ln2_b": [C],
    "L0_attn_b_fc_w": [C, 4096], "L0_attn_b_fc_b_cols": [128, 32],
    "L0_attn_b_mlp_proj_w": [4096, C], "L0_attn_b_mlp_proj_b": [C],
    "L1_attn_a_Wqkv": [C, 3 * C], "L1_attn_a_Wout": [C, C], "L1_attn_a_bout": [C],
    "rope_cos": [128, T], "rope_sin": [128, T],
    "L1_ffn_gate_W": [C, 8], "L1_ffn_W1cat": [C, 4096], "L1_ffn_b1cols": [128, 32],
    "L1_ffn_W2cat": [4096, C], "L1_ffn_b2": [8, C], "L1_ffn_Ws1": [C, 512],
    "L1_ffn_bs1cols": [128, 4], "L1_ffn_Ws2": [512, C], "L1_ffn_bs2": [C],
    "L1_ffn_route_scale": [1, 1],
    "L1_attn_b_scale1": [C], "L1_attn_b_scale2": [C], "L1_attn_b_gamma": [C],
    "L1_attn_b_qkv_wT": [C, 3 * C], "L1_attn_b_proj_wT": [C, C],
    "L1_attn_b_proj_b": [C], "L1_attn_b_conv1_wT": [C, 2048],
    "L1_attn_b_conv1_b_cols": [128, 16], "L1_attn_b_conv2_wT": [2048, C],
    "L1_attn_b_conv2_b": [C],
    "identf": [128, 128],
}
BF16_INPUTS = {"identb", "masks_le", "masks_le2", "rope_swpT"}

EMITTERS = {
    "s1": lambda b, io, scr, src, dst, first: emit_s1(b, io, scr, src, dst),
    "s2": emit_s2,
    "s3": lambda b, io, scr, src, dst, first: emit_s3(b, io, scr, src, dst, first),
    "s4": emit_s2,
    "s5": lambda b, io, scr, src, dst, first: emit_s5(b, io, scr, src, dst),
    "s6": lambda b, io, scr, src, dst, first: emit_s6(b, io, scr, src, dst),
    "s7": lambda b, io, scr, src, dst, first: emit_s7(b, io, scr, src, dst),
    "s8": lambda b, io, scr, src, dst, first: emit_s6(b, io, scr, src, dst),
}


def build_program(parts, debug_scr=False):
    nc = bass.Bass("TRN2", target_bir_lowering=False)
    io = {}
    names = {"identb", "masks_le", "masks_le2", "rope_swpT"}
    for p in parts:
        names |= set(SUB_INPUTS[p])
    io["identb"] = nc.dram_tensor("identb", [128, 128], BF16, kind="ExternalInput")
    io["masks_le"] = nc.dram_tensor("masks_le", [128, 4 * 512], BF16, kind="ExternalInput")
    if "s3" in parts:
        io["masks_le2"] = nc.dram_tensor("masks_le2", [128, 5 * 512], BF16, kind="ExternalInput")
    if "s5" in parts:
        io["rope_swpT"] = nc.dram_tensor("rope_swpT", [128, 128], BF16, kind="ExternalInput")
    for n in sorted(names):
        if n in io or n in ("identb", "masks_le", "masks_le2", "rope_swpT"):
            continue
        io[n] = nc.dram_tensor(n, INPUT_SHAPES[n], F32, kind="ExternalInput")
    io["x"] = nc.dram_tensor("x", [T, C], F32, kind="ExternalInput")
    io["y"] = nc.dram_tensor("y", [T, C], F32, kind="ExternalOutput")

    scr = {}
    kind = "ExternalOutput" if debug_scr else "Internal"

    def mkscr(name, shape, dt=BF16):
        scr[name] = nc.dram_tensor("scr_" + name, shape, dt, kind=kind)

    mkscr("qT", [C, T]); mkscr("kT", [C, T]); mkscr("kn", [T, C])
    mkscr("vn", [T, C]); mkscr("aT", [C, T]); mkscr("xTs", [C, T])
    if "s5" in parts:
        mkscr("qrT", [C, T]); mkscr("krT", [C, T])
    if "s2" in parts or "s4" in parts:
        mkscr("weff_fc", [4096, C]); mkscr("weff_proj", [C, 4096])
    if "s3" in parts:
        mkscr("ekT", [C, 2048]); mkscr("ev", [2048, C])
    if "s3" in parts or "s7" in parts:
        mkscr("xh1", [T, C], F32); mkscr("xh2", [T, C], F32)
    nx = sum(1 for p in parts) - 1
    xbuf = []
    for i in range(max(0, len(parts) - 1)):
        xbuf.append(nc.dram_tensor(f"xcur{i}", [T, C], F32))

    S2_WEFF_MARK.clear()
    b = B(nc)
    b.cb = emit_consts(b, io)
    seen = set()
    cur = io["x"]
    for idx, p in enumerate(parts):
        dst = io["y"] if idx == len(parts) - 1 else xbuf[idx]
        first = p[:2] not in seen
        base = {"s4": "s2", "s8": "s6"}.get(p, p)
        first = base not in seen
        seen.add(base)
        EMITTERS[p](b, io, scr, cur, dst, first)
        cur = dst
    b.barrier()
    b.finalize()
    return nc



def _cols(v, n):
    return np.ascontiguousarray(v.reshape(n, 128).T.astype(np.float32))


def host_consts():
    identb = np.eye(128).astype(BF)
    identf = np.eye(128).astype(np.float32)
    j = np.arange(4)[:, None, None]
    kk_ = np.arange(128)[None, :, None]
    qq = np.arange(512)[None, None, :]
    m = (128 * j + kk_ <= qq).astype(BF)
    masks_le = np.ascontiguousarray(m.transpose(1, 0, 2).reshape(128, 4 * 512))
    j5 = np.arange(5)[:, None, None]
    m2 = (128 * j5 + kk_ <= qq + 2).astype(BF)
    masks_le2 = np.ascontiguousarray(m2.transpose(1, 0, 2).reshape(128, 5 * 512))
    i_ = np.arange(32, dtype=np.float64)
    theta = np.power(10000.0, -2.0 * (i_ - 1.0) / 64.0)
    ang = np.arange(T, dtype=np.float64)[:, None] * theta[None, :]
    cosE = np.cos(ang); sinE = np.sin(ang)
    cosT = np.repeat(cosE.T, 2, axis=0).astype(np.float32)
    sinT = np.repeat(sinE.T, 2, axis=0).astype(np.float32)
    rope_cos = np.ascontiguousarray(np.tile(cosT, (2, 1)))
    rope_sin = np.ascontiguousarray(np.tile(sinT, (2, 1)))
    S = np.zeros((64, 64), np.float32)
    for ii in range(32):
        S[2 * ii, 2 * ii + 1] = -1.0
        S[2 * ii + 1, 2 * ii] = 1.0
    S128 = np.zeros((128, 128), np.float32)
    S128[0:64, 0:64] = S
    S128[64:128, 64:128] = S
    swpT = np.ascontiguousarray(S128.T).astype(BF)
    return dict(identb=identb, identf=identf, masks_le=masks_le,
                masks_le2=masks_le2, rope_cos=rope_cos, rope_sin=rope_sin,
                rope_swpT=swpT)


def host_inputs(inputs):
    f = np.ascontiguousarray
    d = dict(host_consts())
    d["L0_attn_a_wqkv"] = inputs["L0_attn_a_wqkv"]
    d["L0_attn_a_wo"] = inputs["L0_attn_a_wo"]
    d["L0_ffn_w_fc"] = inputs["L0_ffn_w_fc"]
    d["L0_ffn_w_proj"] = inputs["L0_ffn_w_proj"]
    d["L0_ffn_vT"] = f(inputs["L0_ffn_v"].T)
    d["L0_ffn_fbT"] = f(inputs["L0_ffn_fast_b"].T)
    d["L0_ffn_uT"] = f(inputs["L0_ffn_u"].T)
    d["L0_ffn_gate"] = f(np.asarray(inputs["L0_ffn_gate"]).reshape(1, 1))
    d["L0_attn_b_encT"] = f(np.asarray(inputs["L0_attn_b_encoder_hidden_states"])[0].T)
    d["L0_attn_b_scorer_cols"] = _cols(np.asarray(inputs["L0_attn_b_scorer_logits"])[0], 16)
    for nm in ("ln1_g", "ln1_b", "c_attn_w", "c_attn_b", "attn_proj_w",
               "attn_proj_b", "lnx_g", "lnx_b", "q_attn_w", "x_kv_w", "x_kv_b",
               "x_proj_w", "x_proj_b", "ln2_g", "ln2_b", "fc_w",
               "mlp_proj_w", "mlp_proj_b"):
        d["L0_attn_b_" + nm] = inputs["L0_attn_b_" + nm]
    cab = np.asarray(inputs["L0_attn_b_c_attn_b"])
    d["L0_attn_b_c_attn_b_qcols"] = _cols(cab[0:1024], 8)
    d["L0_attn_b_c_attn_b_kcols"] = _cols(cab[1024:2048], 8)
    d["L0_attn_b_q_attn_b_cols"] = _cols(np.asarray(inputs["L0_attn_b_q_attn_b"]), 8)
    d["L0_attn_b_x_kv_b_kcols"] = _cols(np.asarray(inputs["L0_attn_b_x_kv_b"])[0:1024], 8)
    d["L0_attn_b_fc_b_cols"] = _cols(np.asarray(inputs["L0_attn_b_fc_b"]), 32)
    d["L1_attn_a_Wqkv"] = inputs["L1_attn_a_Wqkv"]
    d["L1_attn_a_Wout"] = inputs["L1_attn_a_Wout"]
    d["L1_attn_a_bout"] = inputs["L1_attn_a_bout"]
    d["L1_ffn_gate_W"] = inputs["L1_ffn_gate_W"]
    d["L1_ffn_W1cat"] = f(np.asarray(inputs["L1_ffn_W1"]).transpose(1, 0, 2).reshape(C, 8 * 512))
    d["L1_ffn_b1cols"] = _cols(np.asarray(inputs["L1_ffn_b1"]).reshape(-1), 32)
    d["L1_ffn_W2cat"] = f(np.asarray(inputs["L1_ffn_W2"]).reshape(8 * 512, C))
    d["L1_ffn_b2"] = inputs["L1_ffn_b2"]
    d["L1_ffn_Ws1"] = inputs["L1_ffn_Ws1"]
    d["L1_ffn_bs1cols"] = _cols(np.asarray(inputs["L1_ffn_bs1"]), 4)
    d["L1_ffn_Ws2"] = inputs["L1_ffn_Ws2"]
    d["L1_ffn_bs2"] = inputs["L1_ffn_bs2"]
    d["L1_ffn_route_scale"] = f(np.asarray(inputs["L1_ffn_route_scale"]).reshape(1, 1))
    d["L1_attn_b_scale1"] = inputs["L1_attn_b_scale1"]
    d["L1_attn_b_scale2"] = inputs["L1_attn_b_scale2"]
    d["L1_attn_b_gamma"] = inputs["L1_attn_b_gamma"]
    d["L1_attn_b_qkv_wT"] = f(np.asarray(inputs["L1_attn_b_qkv_w"]).T)
    d["L1_attn_b_proj_wT"] = f(np.asarray(inputs["L1_attn_b_proj_w"]).T)
    d["L1_attn_b_proj_b"] = inputs["L1_attn_b_proj_b"]
    d["L1_attn_b_conv1_wT"] = f(np.asarray(inputs["L1_attn_b_conv1_w"]).T)
    d["L1_attn_b_conv1_b_cols"] = _cols(np.asarray(inputs["L1_attn_b_conv1_b"]), 16)
    d["L1_attn_b_conv2_wT"] = f(np.asarray(inputs["L1_attn_b_conv2_w"]).T)
    d["L1_attn_b_conv2_b"] = inputs["L1_attn_b_conv2_b"]
    return {k: (np.ascontiguousarray(v) if isinstance(v, np.ndarray) else
                np.ascontiguousarray(np.asarray(v))) for k, v in d.items()}


ALL_PARTS = ["s1", "s2", "s3", "s4", "s5", "s6", "s7", "s8"]
GROUPS = [["s1", "s2", "s3", "s4", "s5", "s6", "s7", "s8"]]


def _in_map_for(nc, host, x):
    want = set()
    for a in nc.m.functions[0].allocations:
        try:
            if a.kind == "ExternalInput":
                want.add(a.memorylocations[0].name)
        except AttributeError:
            pass
    m = {}
    for n in want:
        if n == "x":
            m[n] = x
        elif n in host:
            m[n] = host[n]
    return m


def kernel(**inputs):
    host = host_inputs(inputs)
    x = np.ascontiguousarray(np.asarray(inputs["L0_attn_a_x"], np.float32))
    cur = [np.ascontiguousarray(x[c]) for c in range(N_CORE)]
    for grp in GROUPS:
        nc = build_program(grp)
        in_maps = [_in_map_for(nc, host, cur[c]) for c in range(N_CORE)]
        res = run_bass_kernel_spmd(nc, in_maps, core_ids=list(range(N_CORE)))
        cur = [np.asarray(res.results[c]["y"], np.float32) for c in range(N_CORE)]
    return np.stack(cur)
